# version 01
# baseline (speedup 1.0000x reference)
import contextlib
import numpy as np
import ml_dtypes
import concourse.bass as bass
import concourse.bacc as bacc
import concourse.mybir as mybir
from concourse.bass_utils import run_bass_kernel_spmd

AF = mybir.ActivationFunctionType
ALU = mybir.AluOpType
AX = mybir.AxisListType
F32 = mybir.dt.float32
BF16 = mybir.dt.bfloat16
BF = ml_dtypes.bfloat16

D = 2048
DC = 16
P = 512
NTILE = 4
NSEQ = 2
F = 5632
FC = 44
E = 8
R = 16
CONVD = 8192
CC = 64
KH = 16
VH = 32
A = 128
U = 128
CH = 128
HALF = 256
ZD = 4096
KCONV = 4
LN_EPS = 1e-5
L2_EPS = 1e-6
RMS_EPS = 1e-6

def _blob_layout():
    off = {}
    r = 0
    def add(name, rows):
        nonlocal r
        off[name] = r
        r += rows
    add("w1t", FC * 128)
    add("w3t", FC * 128)
    add("w2t", DC * 352)
    add("qkvt", CC * 128)
    add("wz", DC * 256)
    add("woutt", DC * 256)
    add("wba", 64)
    add("b1p", FC * 64)
    add("b3p", FC * 64)
    add("a2p", FC * 32)
    add("b2p", DC * 64)
    add("a1t", 128)
    add("a3t", 128)
    add("gatet", 8)
    if r % 256:
        add("_pad", 256 - r % 256)
    return off, r

BLOB_OFF, BLOB_ROWS = _blob_layout()

def _tile_lhs(w):
    K_, M_ = w.shape
    return w.reshape(K_ // 128, 128, M_ // 128, 128).transpose(2, 0, 1, 3)


def prep_weights(i):
    blob = np.zeros((BLOB_ROWS, 2048), np.float32)

    def put(name, arr):
        rows = arr.size // 2048
        assert rows * 2048 == arr.size
        blob[BLOB_OFF[name]:BLOB_OFF[name] + rows] = \
            np.ascontiguousarray(arr, np.float32).reshape(rows, 2048)

    w1 = np.asarray(i["L0_ffn_w1"], np.float32)
    w3 = np.asarray(i["L0_ffn_w3"], np.float32)
    w2 = np.asarray(i["L0_ffn_w2"], np.float32)
    put("w1t", np.ascontiguousarray(_tile_lhs(w1.T).transpose(0, 2, 1, 3)))
    put("w3t", np.ascontiguousarray(_tile_lhs(w3.T).transpose(0, 2, 1, 3)))
    w2t = _tile_lhs(w2.T)
    put("w2t", np.ascontiguousarray(w2t.transpose(0, 2, 1, 3)).reshape(DC, 128, FC * 128))
    wqkv = np.asarray(i["L0_attn_W_qkv"], np.float32)
    put("qkvt", np.ascontiguousarray(_tile_lhs(wqkv).transpose(0, 2, 1, 3)))
    wz = np.asarray(i["L0_attn_W_z"], np.float32)
    put("wz", wz.reshape(DC, 128, ZD))
    wout = np.asarray(i["L0_attn_W_out"], np.float32)
    wot = wout.reshape(32, 128, DC, 128).transpose(2, 0, 1, 3)
    put("woutt", np.ascontiguousarray(wot.transpose(0, 2, 1, 3)).reshape(DC, 128, 32 * 128))
    wba = np.concatenate([np.asarray(i["L0_attn_W_b"], np.float32),
                          np.asarray(i["L0_attn_W_a"], np.float32)], axis=1)
    put("wba", wba.reshape(DC, 128, 64))

    def bpad(b):
        out = np.zeros((FC, 128, 1024), np.float32)
        for e in range(E):
            t = b[e].reshape(FC, 128, R).transpose(0, 2, 1)
            out[:, 16 * e:16 * e + 16, 128 * e:128 * (e + 1)] = t
        return out
    put("b1p", bpad(np.asarray(i["L0_ffn_b1"], np.float32)))
    put("b3p", bpad(np.asarray(i["L0_ffn_b3"], np.float32)))
    a2 = np.asarray(i["L0_ffn_a2"], np.float32)
    a2p = np.zeros((FC, 128, 512), np.float32)
    for e in range(E):
        t = a2[e].reshape(R, FC, 128).transpose(1, 2, 0)
        a2p[:, :, 64 * e + 16 * (e % 4):64 * e + 16 * (e % 4) + 16] = t
    put("a2p", a2p)
    b2 = np.asarray(i["L0_ffn_b2"], np.float32)
    b2p = np.zeros((DC, 128, 1024), np.float32)
    for e in range(E):
        rbase = 64 * (e // 4) + 16 * (e % 4)
        t = b2[e].reshape(DC, 128, R).transpose(0, 2, 1)
        b2p[:, rbase:rbase + 16, 128 * e:128 * (e + 1)] = t
    put("b2p", b2p)
    a1 = np.asarray(i["L0_ffn_a1"], np.float32).reshape(E * R, D)
    put("a1t", a1.T.reshape(DC, 128, 128))
    a3 = np.asarray(i["L0_ffn_a3"], np.float32).reshape(E * R, D)
    put("a3t", a3.T.reshape(DC, 128, 128))
    gate = np.asarray(i["L0_ffn_gate_w"], np.float32)
    put("gatet", gate.T.reshape(DC, 128, 8))
    return blob


def prep_consts():
    c = {}
    idx = np.arange(128)
    l, m = np.meshgrid(idx, idx, indexing="ij")
    c["mask_low01"] = (m <= l).astype(np.float32)
    c["mask_up01"] = (m >= l).astype(np.float32)
    c["mask_lowneg"] = -(m < l).astype(np.float32)
    c["mask_upneg"] = -(m > l).astype(np.float32)
    c["id_bf"] = np.eye(128).astype(BF)
    c["id_f32"] = np.eye(128).astype(np.float32)
    c["zeros_bf"] = np.zeros((128, 128), BF)
    c["ones128_bf"] = np.ones((128, 128), BF)
    c["ones128_f32"] = np.ones((128, 128), np.float32)
    o8 = np.zeros((128, 8 * 128), BF)
    for e in range(8):
        o8[e, 128 * e:128 * (e + 1)] = 1.0
    c["onehot8p"] = o8
    return c


CONST_SPECS = [("mask_low01", [128, 128], F32), ("mask_up01", [128, 128], F32),
               ("mask_lowneg", [128, 128], F32), ("mask_upneg", [128, 128], F32),
               ("id_bf", [128, 128], BF16), ("id_f32", [128, 128], F32),
               ("zeros_bf", [128, 128], BF16),
               ("ones128_bf", [128, 128], BF16),
               ("ones128_f32", [128, 128], F32),
               ("onehot8p", [128, 1024], BF16)]


def prep_core_x(inputs, core):
    hid = np.asarray(inputs["L0_attn_hidden"], np.float32)
    xs = hid[2 * core:2 * core + 2]
    xf = xs.transpose(0, 2, 1).reshape(NSEQ, DC, 128, NTILE, P).transpose(0, 3, 1, 2, 4)
    return np.ascontiguousarray(xf, np.float32).reshape(NSEQ * NTILE, DC, 128, P)


class Ctr:
    def __init__(self, nc, name, unit=1):
        self._cm = nc.semaphore(name)
        self.h = self._cm.__enter__()
        self.stat = 0
        self.terms = []
        self.unit = unit

    def inc(self, instr, n=1):
        instr.then_inc(self.h, n * self.unit)
        self.stat += n * self.unit

    def val(self, ofs=0):
        v = self.stat + ofs * self.unit
        for reg, coef in self.terms:
            if coef:
                v = reg * coef + v
        return v

    def is_static(self):
        return not any(coef for _, coef in self.terms)

    def clear(self):
        assert not self.terms
        self.stat = 0


def affine_fori(nc, ctrs, start, n, body, deltas):
    if n <= start:
        return
    entry = [c.stat for c in ctrs]
    with nc.Fori(start, n) as i:
        for c, d in zip(ctrs, deltas):
            c.terms.append([i + (-start), d])
        body(i)
    for c, en, d in zip(ctrs, entry, deltas):
        c.terms.pop()
        assert c.stat - en == d, f"body delta mismatch {c.stat - en} vs {d}"
        c.stat = en + d * (n - start)


class InnerLoop:
    def __init__(self, nc, ctrs, n, body):
        assert n >= 2
        s0 = [c.stat for c in ctrs]
        body(0)
        s1 = [c.stat for c in ctrs]
        body(1)
        deltas = [c.stat - b for c, b in zip(ctrs, s1)]
        for c, b0, b1, d in zip(ctrs, s0, s1, deltas):
            assert b1 - b0 == d, f"nonuniform delta {b1 - b0} vs {d}"
        if n > 2:
            entry = [c.stat for c in ctrs]
            with nc.Fori(2, n) as i:
                for c, d in zip(ctrs, deltas):
                    c.terms.append([i + (-2), d])
                body(i)
            for c, en, d in zip(ctrs, entry, deltas):
                c.terms.pop()
                assert c.stat - en == d, f"body delta changed {c.stat - en} vs {d}"
                c.stat = en + d * (n - 2)


class Builder:
    def __init__(self, do_attn=True, dev_tiles=None, dev_stage=None):
        self.do_attn = do_attn
        self.dev_tiles = dev_tiles
        self.dev_stage = dev_stage
        nc = self.nc = bacc.Bacc("TRN2", target_bir_lowering=False, debug=False,
                                 dynamic_dma_scratch_size=16384)
        self.ctx = []
        self.xin = nc.dram_tensor("xin", [NSEQ * NTILE, DC, 128, P], F32, kind="ExternalInput").ap()
        self.xout = nc.dram_tensor("xout", [NSEQ * NTILE, DC, 128, P], F32, kind="ExternalOutput").ap()
        self.wf32 = nc.dram_tensor("wf32", [BLOB_ROWS, 2048], F32, kind="ExternalInput").ap()
        self.wbf = nc.dram_tensor("wbf", [BLOB_ROWS, 2048], BF16, kind="Internal").ap()
        self.convw_d = nc.dram_tensor("convw", [128, 256], F32, kind="ExternalInput").ap()
        self.smalls_d = nc.dram_tensor("smalls", [128, 512], F32, kind="ExternalInput").ap()
        self.cvals_d = nc.dram_tensor("cvals", [128, 16], F32, kind="ExternalInput").ap()
        self.cst_d = {}
        for nm, shp, dt in CONST_SPECS:
            self.cst_d[nm] = nc.dram_tensor("c_" + nm, shp, dt, kind="ExternalInput").ap()

    def sb(self, name, shape, dtype):
        t = self.nc.sbuf_tensor(name, shape, dtype)
        h = t.__enter__()
        self.ctx.append(t)
        return h

    def sync_fence(self):
        nc = self.nc
        nc.sync.wait_ge(self.c_ws[0].h, self.c_ws[0].val())
        nc.sync.wait_ge(self.c_ws[1].h, self.c_ws[1].val())

    def w(self, eng, ctr, ofs=0):
        v = ctr.val(ofs)
        if isinstance(v, int):
            if v <= 0:
                return
        eng.wait_ge(ctr.h, v)

    def build(self):
        nc = self.nc
        self.x_res = self.sb("x_res", [128, DC * P], F32)
        self.xln = self.sb("xln", [128, DC * P], BF16)
        self.WS = self.sb("WS", [128, 13312], BF16)
        self.st = self.sb("st", [128, VH * 128], F32)
        self.stbf = self.sb("stbf", [128, VH * 128], BF16)
        self.K = {}
        for nm, shp, dt in CONST_SPECS:
            self.K[nm] = self.sb("k_" + nm, shp, dt)
        self.convw = self.sb("convw_sb", [128, 256], F32)
        self.dtb_b = self.sb("dtb_b", [128, 32], F32)
        self.negeA_b = self.sb("negeA_b", [128, 32], F32)
        self.normw_rep = self.sb("normw_rep", [128, ZD], BF16)
        self.gates = self.sb("gates", [128, DC * 8], BF16)
        self.wba = self.sb("wba_sb", [128, DC * 64], BF16)
        self.a1s = self.sb("a1s", [128, DC * 128], BF16)
        self.a3s = self.sb("a3s", [128, DC * 128], BF16)
        self.smalls = self.sb("smalls_sb", [128, 512], F32)
        self.cvals = self.sb("cvals_sb", [128, 16], F32)
        self.ctail = self.sb("ctail", [128, CC * 4], BF16)

        self.c_w = Ctr(nc, "c_w", unit=16)
        self.c_ws = [Ctr(nc, "c_ws0", unit=16), Ctr(nc, "c_ws1", unit=16)]
        self.c_x = Ctr(nc, "c_x", unit=16)
        self.c_s = Ctr(nc, "c_s", unit=16)
        self.c_pe = Ctr(nc, "c_pe")
        self.c_act = Ctr(nc, "c_act")
        self.c_dve = Ctr(nc, "c_dve")
        self.c_pool = Ctr(nc, "c_pool")
        self.tile_ctrs = [self.c_w, self.c_ws[0], self.c_ws[1], self.c_x, self.c_s,
                          self.c_pe, self.c_act, self.c_dve, self.c_pool]

        self.emit_prologue()
        ntiles_total = self.dev_tiles if self.dev_tiles is not None else NSEQ * NTILE
        if ntiles_total == 0:
            nc.finalize()
            return nc
        done = 0
        deltas = None
        for seq in range(NSEQ):
            if done >= ntiles_total:
                break
            self.emit_seq_reset()
            ntile = min(NTILE, ntiles_total - done)
            start = 0
            if deltas is None:
                s0 = [c.stat for c in self.tile_ctrs]
                self.emit_tile(seq, 0)
                deltas = [c.stat - b for c, b in zip(self.tile_ctrs, s0)]
                start = 1
            affine_fori(nc, self.tile_ctrs, start, ntile,
                        lambda t, seq=seq: self.emit_tile(seq, t), deltas)
            done += ntile
        nc.finalize()
        return nc

    def emit_prologue(self):
        nc = self.nc
        c_w, c_pe, c_act, c_dve, c_pool = self.c_w, self.c_pe, self.c_act, self.c_dve, self.c_pool
        for nm, shp, dt in CONST_SPECS:
            c_w.inc(nc.sync.dma_start(out=self.K[nm][:, :], in_=self.cst_d[nm][:, :]))
        c_w.inc(nc.sync.dma_start(out=self.convw[:, :], in_=self.convw_d[:, :]))
        c_w.inc(nc.sync.dma_start(out=self.smalls[:, :], in_=self.smalls_d[:, :]))
        c_w.inc(nc.sync.dma_start(out=self.cvals[:, :], in_=self.cvals_d[:, :]))
        NCAST = BLOB_ROWS // 128
        wr32 = self.wf32.rearrange("(n p) d -> n p d", p=128)
        wrbf = self.wbf.rearrange("(n p) d -> n p d", p=128)
        c_cast = Ctr(nc, "c_cast")
        c_cp = [Ctr(nc, "c_cp0"), Ctr(nc, "c_cp1")]
        c_ld = [Ctr(nc, "c_ld0", unit=16), Ctr(nc, "c_ld1", unit=16)]
        c_st = [Ctr(nc, "c_st0", unit=16), Ctr(nc, "c_st1", unit=16)]
        with nc.sbuf_tensor("cast_in", [128, 2 * 2048], F32) as cin, \
             nc.sbuf_tensor("cast_out", [128, 2 * 2048], BF16) as cout:
            def cast_body(j):
                for par in range(2):
                    self.w(nc.sync, c_cp[par], 0)
                    c_ld[par].inc(nc.sync.dma_start(out=cin[:, par * 2048:(par + 1) * 2048],
                                                    in_=wr32[j * 2 + par, :, :]))
                for par in range(2):
                    self.w(nc.gpsimd, c_ld[par])
                    self.w(nc.gpsimd, c_st[par], 0)
                    c_cp[par].inc(nc.gpsimd.tensor_copy(out=cout[:, par * 2048:(par + 1) * 2048],
                                                        in_=cin[:, par * 2048:(par + 1) * 2048]))
                for par in range(2):
                    self.w(nc.scalar, c_cp[par])
                    c_st[par].inc(nc.scalar.dma_start(out=wrbf[j * 2 + par, :, :],
                                                      in_=cout[:, par * 2048:(par + 1) * 2048]))
            InnerLoop(nc, [c_ld[0], c_ld[1], c_cp[0], c_cp[1], c_st[0], c_st[1]],
                      NCAST // 2, cast_body)
            nc.scalar.wait_ge(c_st[0].h, c_st[0].val())
            nc.scalar.wait_ge(c_st[1].h, c_st[1].val())
            nc.sync.wait_ge(c_ld[0].h, c_ld[0].val())
            nc.sync.wait_ge(c_ld[1].h, c_ld[1].val())
        nc.all_engine_barrier()
        with nc.psum_tensor("p_pro", [128, 512], F32) as pb:
            nc.tensor.wait_ge(c_w.h, c_w.val())
            mm = nc.tensor.matmul(pb[:, 0:192], self.K["ones128_f32"][0:1, :],
                                  self.smalls[0:1, 0:192], start=True, stop=True)
            c_pe.inc(mm)
            self.w(nc.vector, self.c_pe)
            c_dve.inc(nc.vector.tensor_copy(out=self.dtb_b[:, :], in_=pb[:, 0:32]))
            self.w(nc.scalar, self.c_pe)
            c_act.inc(nc.scalar.activation(out=self.negeA_b[:, :], in_=pb[:, 32:64], func=AF.Exp))
            self.w(nc.vector, self.c_act)
            c_dve.inc(nc.vector.tensor_scalar_mul(self.negeA_b[:, :], self.negeA_b[:, :], -1.0))
            self.w(nc.vector, self.c_pe)
            c_dve.inc(nc.vector.tensor_copy(out=self.normw_rep[:, 0:128], in_=pb[:, 64:192]))
            for rep in range(1, VH):
                self.w(nc.gpsimd, self.c_dve)
                c_pool.inc(nc.gpsimd.tensor_copy(out=self.normw_rep[:, 128 * rep:128 * (rep + 1)],
                                                 in_=self.normw_rep[:, 0:128]))
            nc.sync.wait_ge(c_w.h, c_w.val())
            for nm, dst, width in [("gatet", self.gates, 8),
                                   ("a1t", self.a1s, 128),
                                   ("a3t", self.a3s, 128),
                                   ("wba", self.wba, 64)]:
                base = BLOB_OFF[nm] * 2048
                ap = bass.AP(self.wbf.tensor, base,
                             [[width, 128], [128 * width, DC], [1, width]])
                dstv = dst[:, :].rearrange("p (k c) -> p k c", c=width)
                c_w.inc(nc.sync.dma_start(out=dstv, in_=ap))
            nc.sync.wait_ge(c_w.h, c_w.val())
        nc.all_engine_barrier()

    def emit_seq_reset(self):
        nc = self.nc
        nc.vector.memset(self.st[:, :], 0.0)
        nc.vector.memset(self.stbf[:, :], 0.0)
        nc.gpsimd.memset(self.ctail[:, :], 0.0)
        nc.all_engine_barrier()

    def wdma(self, ctr, dst_ap, name, row_off, nfree):
        base = (BLOB_OFF[name] + row_off) * 2048
        ap = bass.AP(self.wbf.tensor, base, [[nfree, 128], [1, nfree]])
        ctr.inc(self.nc.sync.dma_start(out=dst_ap, in_=ap))

    def emit_tile(self, seq, t):
        nc = self.nc
        idx = seq * NTILE + t
        xap = bass.AP(self.xin.tensor, idx * (DC * 128 * P),
                      [[P, 128], [128 * P, DC], [1, P]])
        self.c_x.inc(nc.gpsimd.dma_start(
            out=self.x_res[:, :].rearrange("p (k c) -> p k c", c=P), in_=xap))
        nc.gpsimd.wait_ge(self.c_x.h, self.c_x.val())
        nc.all_engine_barrier()
        self.emit_ffn()
        nc.all_engine_barrier()
        if self.do_attn:
            self.emit_ln_standalone()
            nc.all_engine_barrier()
            self.emit_attn()
            nc.all_engine_barrier()
        if not (self.dev_stage or "").startswith("attn"):
            self.emit_ffn()
            nc.all_engine_barrier()
        yap = bass.AP(self.xout.tensor, idx * (DC * 128 * P),
                      [[P, 128], [128 * P, DC], [1, P]])
        self.c_x.inc(nc.gpsimd.dma_start(
            out=yap, in_=self.x_res[:, :].rearrange("p (k c) -> p k c", c=P)))
        nc.gpsimd.wait_ge(self.c_x.h, self.c_x.val())
        nc.sync.wait_ge(self.c_w.h, self.c_w.val())
        nc.sync.wait_ge(self.c_ws[0].h, self.c_ws[0].val())
        nc.sync.wait_ge(self.c_ws[1].h, self.c_ws[1].val())
        nc.scalar.wait_ge(self.c_s.h, self.c_s.val())
        nc.all_engine_barrier()

    def emit_ln_standalone(self):
        nc = self.nc
        K = self.K
        c_pe, c_act, c_dve, c_pool = self.c_pe, self.c_act, self.c_dve, self.c_pool
        self._ln_n = getattr(self, "_ln_n", 0) + 1
        lsx = f"_l{self._ln_n}"
        with nc.sbuf_tensor("l_sc" + lsx, [128, 2 * 512], F32) as sc32, \
             nc.sbuf_tensor("l_rows" + lsx, [1, 2048], F32) as rows, \
             nc.sbuf_tensor("l_sq" + lsx, [128, 2 * 512], BF16) as sqb, \
             nc.psum_tensor("lp6" + lsx, [128, 512], F32) as p6, \
             nc.psum_tensor("lp7" + lsx, [128, 512], F32) as p7:
            xr = self.x_res
            xln = self.xln
            for k in range(DC):
                c_pool.inc(nc.gpsimd.tensor_copy(out=xln[:, 512 * k:512 * (k + 1)],
                                                 in_=xr[:, 512 * k:512 * (k + 1)]))
            pe_sq0 = c_pe.val(0)
            for k in range(DC):
                self.w(nc.scalar, c_pool, -(DC - 1 - k))
                if k >= 2:
                    nc.scalar.wait_ge(c_pe.h, pe_sq0 + (k - 1))
                c_act.inc(nc.scalar.activation(out=sqb[:, 512 * (k % 2):512 * (k % 2 + 1)],
                                               in_=xln[:, 512 * k:512 * (k + 1)], func=AF.Square))
                self.w(nc.tensor, c_pool, -(DC - 1 - k))
                nc.tensor.matmul(p6[0:1, :], K["ones128_bf"][:, 0:1], xln[:, 512 * k:512 * (k + 1)],
                                 start=(k == 0), stop=(k == DC - 1))
                self.w(nc.tensor, c_act)
                c_pe.inc(nc.tensor.matmul(p7[0:1, :], K["ones128_bf"][:, 0:1],
                                          sqb[:, 512 * (k % 2):512 * (k % 2 + 1)],
                                          start=(k == 0), stop=(k == DC - 1)))
            self.w(nc.scalar, c_pe)
            c_act.inc(nc.scalar.mul(rows[0:1, 0:512], p6[0:1, :], 1.0 / D))
            c_act.inc(nc.scalar.mul(rows[0:1, 512:1024], p7[0:1, :], 1.0 / D))
            self.w(nc.vector, c_act)
            c_dve.inc(nc.vector.tensor_mul(rows[0:1, 1024:1536], rows[0:1, 0:512], rows[0:1, 0:512]))
            self.w(nc.vector, c_dve)
            c_dve.inc(nc.vector.tensor_sub(rows[0:1, 512:1024], rows[0:1, 512:1024], rows[0:1, 1024:1536]))
            self.w(nc.scalar, c_dve)
            c_act.inc(nc.scalar.activation(out=rows[0:1, 1024:1536], in_=rows[0:1, 512:1024],
                                           func=AF.Ln, bias=self.cvals[0:1, 0:1]))
            self.w(nc.scalar, c_act)
            c_act.inc(nc.scalar.activation(out=rows[0:1, 512:1024], in_=rows[0:1, 1024:1536],
                                           func=AF.Exp, scale=-0.5))
            self.w(nc.vector, c_act)
            c_dve.inc(nc.vector.tensor_mul(rows[0:1, 1024:1536], rows[0:1, 0:512], rows[0:1, 512:1024]))
            self.w(nc.tensor, c_dve)
            nc.tensor.matmul(p6[:, :], K["ones128_f32"][0:1, :], rows[0:1, 512:1024], start=True, stop=True)
            c_pe.inc(nc.tensor.matmul(p7[:, :], K["ones128_f32"][0:1, :], rows[0:1, 1024:1536],
                                      start=True, stop=True))
            self.w(nc.scalar, c_pe)
            c_act.inc(nc.scalar.copy(out=sc32[:, 0:512], in_=p6[:, :]))
            c_act.inc(nc.scalar.copy(out=sc32[:, 512:1024], in_=p7[:, :]))
            for k in range(DC):
                self.w(nc.gpsimd, c_act)
                c_pool.inc(nc.gpsimd.tensor_mul(xln[:, 512 * k:512 * (k + 1)],
                                                xr[:, 512 * k:512 * (k + 1)], sc32[:, 0:512]))
                self.w(nc.vector, c_pool)
                c_dve.inc(nc.vector.tensor_sub(xln[:, 512 * k:512 * (k + 1)],
                                               xln[:, 512 * k:512 * (k + 1)], sc32[:, 512:1024]))
            nc.vector.wait_ge(c_dve.h, c_dve.val())

    def emit_ffn(self):
        nc = self.nc
        K = self.K
        self._ffn_n = getattr(self, "_ffn_n", 0) + 1
        sfx = f"_{self._ffn_n}"
        c_w, c_pe, c_act, c_dve, c_pool = self.c_w, self.c_pe, self.c_act, self.c_dve, self.c_pool
        with nc.sbuf_tensor("f_sc" + sfx, [128, 3 * 512], F32) as sc32, \
             nc.sbuf_tensor("f_rows" + sfx, [1, 2048], F32) as rows, \
             nc.sbuf_tensor("f_sq" + sfx, [128, 2 * 512], BF16) as sqb, \
             nc.sbuf_tensor("f_base" + sfx, [128, 2 * 512], BF16) as baseb, \
             nc.sbuf_tensor("f_sg" + sfx, [128, 6 * 512], BF16) as sgb, \
             nc.sbuf_tensor("f_t1" + sfx, [128, 2 * 512], BF16) as t1b, \
             nc.sbuf_tensor("f_t2" + sfx, [128, 512], BF16) as t2ev, \
             nc.sbuf_tensor("f_wb" + sfx, [128, 8 * 512], BF16) as wb8, \
             nc.sbuf_tensor("f_wt" + sfx, [32, 512], BF16) as wT32, \
             nc.sbuf_tensor("f_wfl" + sfx, [128, 64], F32) as wfl, \
             nc.sbuf_tensor("f_sact" + sfx, [128, FC * 512], BF16) as sact, \
             nc.psum_tensor("fp0" + sfx, [128, 512], F32) as p0, \
             nc.psum_tensor("fp1" + sfx, [128, 512], F32) as p1, \
             nc.psum_tensor("fp2" + sfx, [128, 512], F32) as p2, \
             nc.psum_tensor("fp3" + sfx, [128, 512], F32) as p3, \
             nc.psum_tensor("fp4" + sfx, [128, 512], F32) as p4, \
             nc.psum_tensor("fp5" + sfx, [128, 512], F32) as p5, \
             nc.psum_tensor("fp6" + sfx, [128, 512], F32) as p6, \
             nc.psum_tensor("fp7" + sfx, [128, 512], F32) as p7:
            xr = self.x_res
            xln = self.xln
            for k in range(DC):
                c_pool.inc(nc.gpsimd.tensor_copy(out=xln[:, 512 * k:512 * (k + 1)],
                                                 in_=xr[:, 512 * k:512 * (k + 1)]))
            pe_sq0 = c_pe.val(0)
            for k in range(DC):
                self.w(nc.scalar, c_pool, -(DC - 1 - k))
                if k >= 2:
                    nc.scalar.wait_ge(c_pe.h, pe_sq0 + (k - 1))
                c_act.inc(nc.scalar.activation(out=sqb[:, 512 * (k % 2):512 * (k % 2 + 1)],
                                               in_=xln[:, 512 * k:512 * (k + 1)], func=AF.Square))
                self.w(nc.tensor, c_pool, -(DC - 1 - k))
                nc.tensor.matmul(p6[0:1, :], K["ones128_bf"][:, 0:1], xln[:, 512 * k:512 * (k + 1)],
                                 start=(k == 0), stop=(k == DC - 1))
                self.w(nc.tensor, c_act)
                c_pe.inc(nc.tensor.matmul(p7[0:1, :], K["ones128_bf"][:, 0:1],
                                          sqb[:, 512 * (k % 2):512 * (k % 2 + 1)],
                                          start=(k == 0), stop=(k == DC - 1)))
            self.w(nc.scalar, c_pe)
            c_act.inc(nc.scalar.mul(rows[0:1, 0:512], p6[0:1, :], 1.0 / D))
            c_act.inc(nc.scalar.mul(rows[0:1, 512:1024], p7[0:1, :], 1.0 / D))
            self.w(nc.vector, c_act)
            c_dve.inc(nc.vector.tensor_mul(rows[0:1, 1024:1536], rows[0:1, 0:512], rows[0:1, 0:512]))
            self.w(nc.vector, c_dve)
            c_dve.inc(nc.vector.tensor_sub(rows[0:1, 512:1024], rows[0:1, 512:1024], rows[0:1, 1024:1536]))
            self.w(nc.scalar, c_dve)
            c_act.inc(nc.scalar.activation(out=rows[0:1, 1024:1536], in_=rows[0:1, 512:1024],
                                           func=AF.Ln, bias=self.cvals[0:1, 0:1]))
            self.w(nc.scalar, c_act)
            c_act.inc(nc.scalar.activation(out=rows[0:1, 512:1024], in_=rows[0:1, 1024:1536],
                                           func=AF.Exp, scale=-0.5))
            self.w(nc.vector, c_act)
            c_dve.inc(nc.vector.tensor_mul(rows[0:1, 1024:1536], rows[0:1, 0:512], rows[0:1, 512:1024]))
            self.w(nc.tensor, c_dve)
            nc.tensor.matmul(p6[:, :], K["ones128_f32"][0:1, :], rows[0:1, 512:1024], start=True, stop=True)
            c_pe.inc(nc.tensor.matmul(p7[:, :], K["ones128_f32"][0:1, :], rows[0:1, 1024:1536],
                                      start=True, stop=True))
            self.w(nc.scalar, c_pe)
            c_act.inc(nc.scalar.copy(out=sc32[:, 0:512], in_=p6[:, :]))
            c_act.inc(nc.scalar.copy(out=sc32[:, 512:1024], in_=p7[:, :]))
            for k in range(DC):
                self.w(nc.gpsimd, c_act)
                c_pool.inc(nc.gpsimd.tensor_mul(xln[:, 512 * k:512 * (k + 1)],
                                                xr[:, 512 * k:512 * (k + 1)], sc32[:, 0:512]))
                self.w(nc.vector, c_pool)
                c_dve.inc(nc.vector.tensor_sub(xln[:, 512 * k:512 * (k + 1)],
                                               xln[:, 512 * k:512 * (k + 1)], sc32[:, 512:1024]))
            ln_dve_done = c_dve.val()
            if self.dev_stage == "ln":
                return
            nc.tensor.wait_ge(c_dve.h, ln_dve_done)
            for g in range(4):
                for k in range(DC):
                    mm = nc.tensor.matmul(p0[:, 8 * g:8 * (g + 1)],
                                          xln[:, 512 * k + 128 * g:512 * k + 128 * g + 128],
                                          self.gates[:, 8 * k:8 * (k + 1)],
                                          start=(k == 0), stop=(k == DC - 1))
            c_pe.inc(mm)
            self.w(nc.vector, c_pe)
            for g in range(4):
                lg = p0[:, 8 * g:8 * (g + 1)]
                m1 = sc32[:, 1024 + 16 * g:1024 + 16 * g + 1]
                m2 = sc32[:, 1024 + 16 * g + 1:1024 + 16 * g + 2]
                w1c = sc32[:, 1024 + 16 * g + 2:1024 + 16 * g + 3]
                w2c = sc32[:, 1024 + 16 * g + 3:1024 + 16 * g + 4]
                eq1 = sc32[:, 1024 + 16 * g + 4:1024 + 16 * g + 12]
                l2v = wfl[:, 16 * g:16 * g + 8]
                eq2 = wfl[:, 16 * g + 8:16 * g + 16]
                c_dve.inc(nc.vector.reduce_max(m1, lg, axis=AX.X))
                self.w(nc.vector, c_dve)
                c_dve.inc(nc.vector.tensor_scalar(eq1, lg, m1, None, op0=ALU.is_equal))
                self.w(nc.vector, c_dve)
                c_dve.inc(nc.vector.scalar_tensor_tensor(out=l2v, in0=eq1, scalar=-1e30, in1=lg,
                                                         op0=ALU.mult, op1=ALU.add))
                self.w(nc.vector, c_dve)
                c_dve.inc(nc.vector.reduce_max(m2, l2v, axis=AX.X))
                self.w(nc.vector, c_dve)
                c_dve.inc(nc.vector.tensor_sub(w1c, m1, m2))
                self.w(nc.scalar, c_dve)
                c_act.inc(nc.scalar.activation(out=w1c, in_=w1c, func=AF.Sigmoid))
                self.w(nc.vector, c_act)
                c_dve.inc(nc.vector.tensor_scalar(w2c, w1c, -1.0, 1.0, op0=ALU.mult, op1=ALU.add))
                c_dve.inc(nc.vector.tensor_scalar(eq2, l2v, m2, None, op0=ALU.is_equal))
                self.w(nc.vector, c_dve)
                c_dve.inc(nc.vector.tensor_scalar(eq1, eq1, w1c, None, op0=ALU.mult))
                self.w(nc.vector, c_dve)
                c_dve.inc(nc.vector.tensor_scalar(eq2, eq2, w2c, None, op0=ALU.mult))
                self.w(nc.vector, c_dve)
                c_dve.inc(nc.vector.tensor_add(sgb[:, 128 * g:128 * g + 8], eq1, eq2))
            c_pool.inc(nc.gpsimd.memset(wT32[:, :], 0.0))
            self.w(nc.tensor, c_dve)
            for g in range(4):
                mm = nc.tensor.matmul(p1[0:8, 128 * g:128 * (g + 1)],
                                      sgb[:, 128 * g:128 * g + 8], K["id_bf"][:, :],
                                      start=True, stop=True)
            c_pe.inc(mm)
            self.w(nc.scalar, c_pe)
            self.w(nc.scalar, c_pool)
            c_act.inc(nc.scalar.copy(out=wT32[0:8, :], in_=p1[0:8, :]))
            for e in range(8):
                pp = p2 if e % 2 == 0 else p3
                self.w(nc.tensor, c_act)
                c_pe.inc(nc.tensor.matmul(pp[:, :], K["onehot8p"][0:32, 128 * e:128 * (e + 1)],
                                          wT32[:, :], start=True, stop=True))
                self.w(nc.scalar, c_pe)
                c_act.inc(nc.scalar.mul(wb8[:, 512 * e:512 * (e + 1)], pp[:, :], 0.5))
            if self.dev_stage == "gate":
                return
            for k in range(DC):
                nc.tensor.matmul(p4[:, :], self.a1s[:, 128 * k:128 * (k + 1)],
                                 xln[:, 512 * k:512 * (k + 1)], start=(k == 0), stop=(k == DC - 1))
                mm = nc.tensor.matmul(p5[:, :], self.a3s[:, 128 * k:128 * (k + 1)],
                                      xln[:, 512 * k:512 * (k + 1)], start=(k == 0), stop=(k == DC - 1))
            c_pe.inc(mm)
            self.w(nc.scalar, c_pe)
            c_act.inc(nc.scalar.mul(t1b[:, 0:512], p4[:, :], 2.0))
            c_act.inc(nc.scalar.mul(t1b[:, 512:1024], p5[:, :], 2.0))
            self.w(nc.tensor, c_act)
            c_pe.inc(nc.tensor.matmul(p5[:, :], K["zeros_bf"][:, :], t1b[:, 0:512],
                                      start=True, stop=True))
            if self.dev_stage == "t1":
                return
            WS = self.WS
            SLOT = 6656
            ACT_PER_F = 19

            def fpair(j):
                for par in range(2):
                    f = j * 2 + par
                    self.w(nc.sync, c_act, -ACT_PER_F * (1 - par))
                    self.wdma(self.c_ws[par], WS[:, SLOT * par:SLOT * par + 2048], "w1t", f * 128, 2048)
                    self.wdma(self.c_ws[par], WS[:, SLOT * par + 2048:SLOT * par + 4096], "w3t", f * 128, 2048)
                    self.wdma(self.c_ws[par], WS[:, SLOT * par + 4096:SLOT * par + 5120], "b1p", f * 64, 1024)
                    self.wdma(self.c_ws[par], WS[:, SLOT * par + 5120:SLOT * par + 6144], "b3p", f * 64, 1024)
                    self.wdma(self.c_ws[par], WS[:, SLOT * par + 6144:SLOT * par + 6656], "a2p", f * 32, 512)
                for par in range(2):
                    f = j * 2 + par
                    so = SLOT * par
                    bo = 512 * par
                    self.w(nc.tensor, self.c_ws[par])
                    for k in range(DC):
                        mm = nc.tensor.matmul(p0[:, :], WS[:, so + 128 * k:so + 128 * (k + 1)],
                                              xln[:, 512 * k:512 * (k + 1)],
                                              start=(k == 0), stop=(k == DC - 1))
                    c_pe.inc(mm)
                    for k in range(DC):
                        mm = nc.tensor.matmul(p1[:, :], WS[:, so + 2048 + 128 * k:so + 2048 + 128 * (k + 1)],
                                              xln[:, 512 * k:512 * (k + 1)],
                                              start=(k == 0), stop=(k == DC - 1))
                    c_pe.inc(mm)
                    self.w(nc.scalar, c_pe, -1)
                    c_act.inc(nc.scalar.copy(out=baseb[:, bo:bo + 512], in_=p0[:, :]))
                    self.w(nc.scalar, c_pe)
                    c_act.inc(nc.scalar.copy(out=sgb[:, 2048 + bo:2048 + bo + 512], in_=p1[:, :]))
                    for e in range(8):
                        ep = e % 2
                        hp1 = p2 if ep == 0 else p3
                        hp3 = p6 if ep == 0 else p7
                        g64 = 64 * (e // 4)
                        c128 = 128 * (e % 4)
                        self.w(nc.tensor, c_act, 0 if e == 0 else (-2 if e == 1 else -3))
                        nc.tensor.matmul(hp1[:, :],
                                         WS[:, so + 4096 + 128 * e:so + 4096 + 128 * (e + 1)],
                                         t1b[:, 0:512], start=True, stop=False)
                        c_pe.inc(nc.tensor.matmul(hp1[:, :], K["id_bf"][:, :],
                                                  baseb[:, bo:bo + 512], start=False, stop=True))
                        nc.tensor.matmul(hp3[:, :],
                                         WS[:, so + 5120 + 128 * e:so + 5120 + 128 * (e + 1)],
                                         t1b[:, 512:1024], start=True, stop=False)
                        c_pe.inc(nc.tensor.matmul(hp3[:, :], K["id_bf"][:, :],
                                                  sgb[:, 2048 + bo:2048 + bo + 512], start=False, stop=True))
                        self.w(nc.scalar, c_pe, -1)
                        c_act.inc(nc.scalar.activation(out=sgb[:, 1024 + 512 * ep:1024 + 512 * (ep + 1)],
                                                       in_=hp1[:, :], func=AF.Silu))
                        self.w(nc.scalar, c_pe)
                        c_act.inc(nc.scalar.copy(out=sgb[:, 512 * ep:512 * (ep + 1)], in_=hp3[:, :]))
                        self.w(nc.vector, c_act)
                        c_dve.inc(nc.vector.tensor_mul(sgb[:, 512 * ep:512 * (ep + 1)],
                                                       sgb[:, 512 * ep:512 * (ep + 1)],
                                                       wb8[:, 512 * e:512 * (e + 1)]))
                        self.w(nc.vector, c_dve)
                        c_dve.inc(nc.vector.tensor_mul(sgb[:, 512 * ep:512 * (ep + 1)],
                                                       sgb[:, 512 * ep:512 * (ep + 1)],
                                                       sgb[:, 1024 + 512 * ep:1024 + 512 * (ep + 1)]))
                        self.w(nc.tensor, c_dve)
                        nc.tensor.matmul(p4[:, :], K["id_bf"][:, :], sgb[:, 512 * ep:512 * (ep + 1)],
                                         start=(e == 0), stop=(e == 7))
                        c_pe.inc(nc.tensor.matmul(p5[64 * (e // 4):64 * (e // 4) + 64, :],
                                                  WS[:, so + 6144 + 64 * e:so + 6144 + 64 * (e + 1)],
                                                  sgb[:, 512 * ep:512 * (ep + 1)], start=False,
                                                  stop=False, skip_group_check=True))
                    self.w(nc.scalar, c_pe)
                    c_act.inc(nc.scalar.copy(out=sact[:, bass.ds(512 * f, 512)], in_=p4[:, :]))

            self.sync_fence()
            InnerLoop(nc, self.tile_ctrs, FC // 2, fpair)
            if self.dev_stage == "floop":
                return
            self.w(nc.scalar, c_pe)
            c_act.inc(nc.scalar.mul(t2ev[:, :], p5[:, :], 2.0))
            if self.dev_stage == "t2ev":
                return
            DSLOT = 6656
            self.sync_fence()
            nc.tensor.wait_ge(c_act.h, c_act.val())
            nc.sync.wait_ge(c_pe.h, c_pe.val())

            def dpair(j):
                for par in range(2):
                    d = j * 2 + par
                    self.w(nc.sync, c_dve, par - 1)
                    self.wdma(self.c_ws[par], WS[:, DSLOT * par:DSLOT * par + 5632], "w2t", d * 352, 5632)
                    self.wdma(self.c_ws[par], WS[:, DSLOT * par + 5632:DSLOT * par + 6656], "b2p", d * 64, 1024)
                for par in range(2):
                    d = j * 2 + par
                    so = DSLOT * par
                    pp = p0 if par == 0 else p1
                    self.w(nc.tensor, self.c_ws[par])
                    for f in range(FC):
                        nc.tensor.matmul(pp[:, :], WS[:, so + 128 * f:so + 128 * (f + 1)],
                                         sact[:, 512 * f:512 * (f + 1)],
                                         start=(f == 0), stop=False)
                    for e in range(8):
                        mm = nc.tensor.matmul(pp[:, :],
                                              WS[:, so + 5632 + 128 * e:so + 5632 + 128 * (e + 1)],
                                              t2ev[:, :], start=False, stop=(e == 7))
                    c_pe.inc(mm)
                    self.w(nc.vector, c_pe)
                    c_dve.inc(nc.vector.tensor_add(self.x_res[:, bass.ds(512 * d, 512)],
                                                   self.x_res[:, bass.ds(512 * d, 512)], pp[:, :]))
            InnerLoop(nc, self.tile_ctrs, DC // 2, dpair)
            nc.vector.wait_ge(c_dve.h, c_dve.val())

    def emit_attn(self):
        nc = self.nc
        K = self.K
        c_pe, c_act, c_dve, c_pool = self.c_pe, self.c_act, self.c_dve, self.c_pool
        self._attn_n = getattr(self, "_attn_n", 0) + 1
        sx = f"_a{self._attn_n}"
        WS = self.WS
        _es = contextlib.ExitStack()
        q_sb = _es.enter_context(nc.sbuf_tensor("q_sb" + sx, [128, KH * HALF], BF16))
        k_sb = _es.enter_context(nc.sbuf_tensor("k_sb" + sx, [128, KH * HALF], BF16))
        v_sb = _es.enter_context(nc.sbuf_tensor("v_sb" + sx, [128, VH * HALF], BF16))
        mp = _es.enter_context(nc.sbuf_tensor("mp" + sx, [128, 2 * 262], BF16))
        cvt = _es.enter_context(nc.sbuf_tensor("cvt" + sx, [128, 2 * 256], BF16))
        qsr = _es.enter_context(nc.sbuf_tensor("qsr" + sx, [128, 2 * 256], BF16))
        rrow = _es.enter_context(nc.sbuf_tensor("rrow" + sx, [1, 2 * 256], BF16))
        rrf = _es.enter_context(nc.sbuf_tensor("rrf" + sx, [1, 2 * 256], F32))
        siluz = _es.enter_context(nc.sbuf_tensor("siluz" + sx, [128, ZD], BF16))
        xg = _es.enter_context(nc.sbuf_tensor("xg" + sx, [128, ZD], BF16))
        nrm = _es.enter_context(nc.sbuf_tensor("nrm" + sx, [128, 32 * HALF], BF16))
        tmp8 = _es.enter_context(nc.sbuf_tensor("tmp8" + sx, [128, 10 * 32], F32))
        gjfm = _es.enter_context(nc.sbuf_tensor("gjfm" + sx, [32, 128], F32))
        colb = _es.enter_context(nc.sbuf_tensor("colb" + sx, [128, 128], F32))
        colbb = _es.enter_context(nc.sbuf_tensor("colbb" + sx, [128, 128], BF16))
        tsc = _es.enter_context(nc.sbuf_tensor("tsc" + sx, [128, 128], F32))
        eA = _es.enter_context(nc.sbuf_tensor("eA" + sx, [128, 128], F32))
        eTb = _es.enter_context(nc.sbuf_tensor("eT" + sx, [128, 2 * 128], F32))
        gmlb = _es.enter_context(nc.sbuf_tensor("gml" + sx, [128, 2 * 128], F32))
        gmub = _es.enter_context(nc.sbuf_tensor("gmu" + sx, [128, 2 * 128], F32))
        t2f = _es.enter_context(nc.sbuf_tensor("t2f" + sx, [128, 128], F32))
        xcur = _es.enter_context(nc.sbuf_tensor("xcur" + sx, [128, 2 * 128], BF16))
        ycur = _es.enter_context(nc.sbuf_tensor("ycur" + sx, [128, 2 * 128], BF16))
        xrh = _es.enter_context(nc.sbuf_tensor("xrh" + sx, [128, 2 * 256], BF16))
        kcd = _es.enter_context(nc.sbuf_tensor("kcd" + sx, [128, 2 * 128], BF16))
        nkcdt = _es.enter_context(nc.sbuf_tensor("nkcdt" + sx, [128, 2 * 128], BF16))
        vnew = _es.enter_context(nc.sbuf_tensor("vnew" + sx, [128, 2 * 128], BF16))
        intrat = _es.enter_context(nc.sbuf_tensor("intrat" + sx, [128, 2 * 128], BF16))
        qeb = _es.enter_context(nc.sbuf_tensor("qe" + sx, [128, 2 * 128], BF16))
        kdecb = _es.enter_context(nc.sbuf_tensor("kdec" + sx, [128, 2 * 128], BF16))
        pa0 = _es.enter_context(nc.psum_tensor("pa0" + sx, [128, 512], F32))
        pa1 = _es.enter_context(nc.psum_tensor("pa1" + sx, [128, 512], F32))
        pa2 = _es.enter_context(nc.psum_tensor("pa2" + sx, [128, 512], F32))
        pa3 = _es.enter_context(nc.psum_tensor("pa3" + sx, [128, 512], F32))
        pa4 = _es.enter_context(nc.psum_tensor("pa4" + sx, [128, 1024], BF16))
        pa5 = _es.enter_context(nc.psum_tensor("pa5" + sx, [128, 1024], BF16))
        pa6 = _es.enter_context(nc.psum_tensor("pa6" + sx, [128, 512], F32))
        pa7 = _es.enter_context(nc.psum_tensor("pa7" + sx, [128, 512], F32))
        if True:
            t_beta = tmp8[:, 0:32]
            t_gj = tmp8[:, 32:64]
            t_egj = tmp8[:, 64:96]
            t_edec = tmp8[:, 96:128]
            t_begj = tmp8[:, 128:160]
            t_cb = tmp8[:, 160:192]
            t_gs = tmp8[:, 192:224]
            env = dict(locals())
            ds_ = self.dev_stage or ""
            for half in range(2):
                self.attn_frontend(half, env)
                if ds_ == "attn_fe" and half == 0:
                    self._dump([(q_sb, 0, KH * HALF), (k_sb, KH * HALF, KH * HALF)])
                    break
                if ds_ in ("attn_v", "attn_v_noconv") and half == 0:
                    self._dump([(v_sb, 0, VH * HALF)])
                    break
                stop = False
                for cpar in range(2):
                    self.attn_chunk(half, cpar, env)
                    if ds_ in ("attn_z", "attn_core") and half == 0 and cpar == 0:
                        if ds_ == "attn_z":
                            self._dump([(siluz, 0, ZD), (tmp8, ZD, 320)])
                        else:
                            self._dump([(xg, 0, ZD), (nrm, ZD, 4096)])
                        stop = True
                        break
                if stop:
                    break
                self.attn_outproj(half, env)
        _es.close()

    def _dump(self, items):
        nc = self.nc
        nc.all_engine_barrier()
        for src, off, width in items:
            nc.vector.tensor_copy(out=self.x_res[:, off:off + width], in_=src[:, 0:width])
        nc.all_engine_barrier()

    def attn_frontend(self, half, L):
        nc = self.nc
        K = self.K
        c_pe, c_act, c_dve, c_pool = self.c_pe, self.c_act, self.c_dve, self.c_pool
        WS = self.WS
        q_sb, k_sb, v_sb = L["q_sb"], L["k_sb"], L["v_sb"]
        mp, cvt, qsr, rrow, rrf = L["mp"], L["cvt"], L["qsr"], L["rrow"], L["rrf"]
        pa0, pa1, pa2 = L["pa0"], L["pa1"], L["pa2"]
        xln = self.xln
        tok0 = half * HALF

        self.sync_fence()
        nc.tensor.wait_ge(c_dve.h, c_dve.val())

        def cpair(kind, cbase, dst):
            def body(j):
                for par in range(2):
                    c = cbase + j * 2 + par
                    self.w(nc.sync, c_pe, -(1 - par) * self._fe_pe_per_c[kind])
                    self.wdma(self.c_ws[par], WS[:, 2048 * par:2048 * (par + 1)],
                              "qkvt", c * 128, 2048)
                for par in range(2):
                    c = cbase + j * 2 + par
                    pp = pa0 if par == 0 else pa1
                    self.w(nc.tensor, self.c_ws[par])
                    apc = 5 if kind != "v" else 2
                    self.w(nc.tensor, c_act, 1 - 2 * apc)
                    for k in range(DC):
                        mm = nc.tensor.matmul(
                            pp[:, 0:HALF], WS[:, 2048 * par + 128 * k:2048 * par + 128 * (k + 1)],
                            xln[:, bass.ds(512 * k + tok0, HALF)],
                            start=(k == 0), stop=(k == DC - 1))
                    c_pe.inc(mm)
                    self.w(nc.scalar, c_pe)
                    c_act.inc(nc.scalar.copy(out=mp[:, 262 * par + 3:262 * par + 259], in_=pp[:, 0:HALF]))
                    self.w(nc.gpsimd, c_act)
                    c_pool.inc(nc.gpsimd.tensor_copy(out=mp[:, 262 * par:262 * par + 3],
                                                     in_=self.ctail[:, bass.ds(4 * c, 3)]))
                    self.w(nc.vector, c_pool)
                    if (self.dev_stage or "").endswith("noconv"):
                        c_dve.inc(nc.vector.tensor_copy(out=cvt[:, 256 * par:256 * (par + 1)],
                                                        in_=mp[:, 262 * par + 3:262 * par + 259]))
                    else:
                        c_dve.inc(nc.vector.tensor_scalar(cvt[:, 256 * par:256 * (par + 1)],
                                                          mp[:, 262 * par:262 * par + 256],
                                                          self.convw[:, bass.ds(4 * c, 1)], None, op0=ALU.mult))
                    for kk in range(1, KCONV):
                        if (self.dev_stage or "").endswith("noconv"):
                            self.w(nc.vector, c_dve)
                            c_dve.inc(nc.vector.tensor_copy(out=cvt[:, 256 * par:256 * (par + 1)],
                                                            in_=mp[:, 262 * par + 3:262 * par + 259]))
                            continue
                        self.w(nc.vector, c_dve)
                        c_dve.inc(nc.vector.scalar_tensor_tensor(
                            out=cvt[:, 256 * par:256 * (par + 1)],
                            in0=mp[:, 262 * par + kk:262 * par + kk + 256],
                            scalar=self.convw[:, bass.ds(4 * c + kk, 1)],
                            in1=cvt[:, 256 * par:256 * (par + 1)], op0=ALU.mult, op1=ALU.add))
                    self.w(nc.gpsimd, c_dve)
                    c_pool.inc(nc.gpsimd.tensor_copy(out=self.ctail[:, bass.ds(4 * c, 3)],
                                                     in_=mp[:, 262 * par + 256:262 * par + 259]))
                    if kind == "v":
                        self.w(nc.scalar, c_dve)
                        c_act.inc(nc.scalar.activation(
                            out=dst[:, bass.ds((c - cbase) * HALF, HALF)],
                            in_=cvt[:, 256 * par:256 * (par + 1)], func=AF.Silu))
                    else:
                        self.w(nc.scalar, c_dve)
                        c_act.inc(nc.scalar.activation(out=qsr[:, 256 * par:256 * (par + 1)],
                                                       in_=cvt[:, 256 * par:256 * (par + 1)], func=AF.Silu))
                        self.w(nc.scalar, c_act)
                        c_act.inc(nc.scalar.activation(out=cvt[:, 256 * par:256 * (par + 1)],
                                                       in_=qsr[:, 256 * par:256 * (par + 1)], func=AF.Square))
                        self.w(nc.tensor, c_act)
                        self.w(nc.tensor, c_dve)
                        c_pe.inc(nc.tensor.matmul(pa2[0:1, bass.ds(256 * par, 256)],
                                                  K["ones128_bf"][:, 0:1],
                                                  cvt[:, 256 * par:256 * (par + 1)], start=True, stop=True))
                        self.w(nc.scalar, c_pe)
                        c_act.inc(nc.scalar.activation(out=rrf[0:1, 256 * par:256 * (par + 1)],
                                                       in_=pa2[0:1, bass.ds(256 * par, 256)],
                                                       func=AF.Ln, bias=self.cvals[0:1, 1:2]))
                        self.w(nc.scalar, c_act)
                        if kind == "q":
                            c_act.inc(nc.scalar.activation(out=rrow[0:1, 256 * par:256 * (par + 1)],
                                                           in_=rrf[0:1, 256 * par:256 * (par + 1)],
                                                           func=AF.Exp, scale=-0.5,
                                                           bias=self.cvals[0:1, 3:4]))
                        else:
                            c_act.inc(nc.scalar.activation(out=rrow[0:1, 256 * par:256 * (par + 1)],
                                                           in_=rrf[0:1, 256 * par:256 * (par + 1)],
                                                           func=AF.Exp, scale=-0.5))
                        self.w(nc.tensor, c_act)
                        c_pe.inc(nc.tensor.matmul(pa2[:, bass.ds(256 * par, 256)],
                                                  K["ones128_bf"][0:1, :],
                                                  rrow[0:1, 256 * par:256 * (par + 1)],
                                                  start=True, stop=True))
                        self.w(nc.vector, c_pe)
                        c_dve.inc(nc.vector.tensor_mul(
                            dst[:, bass.ds((c - cbase) * HALF, HALF)],
                            qsr[:, 256 * par:256 * (par + 1)],
                            pa2[:, bass.ds(256 * par, 256)]))
            return body
        self._fe_pe_per_c = {"q": 3, "k": 3, "v": 1}
        InnerLoop(nc, self.tile_ctrs, 8, cpair("q", 0, q_sb))
        InnerLoop(nc, self.tile_ctrs, 8, cpair("k", 16, k_sb))
        InnerLoop(nc, self.tile_ctrs, 16, cpair("v", 32, v_sb))

    def attn_chunk(self, half, cpar, L):
        nc = self.nc
        K = self.K
        c_pe, c_act, c_dve, c_pool = self.c_pe, self.c_act, self.c_dve, self.c_pool
        WS = self.WS
        xln = self.xln
        q_sb, k_sb, v_sb, siluz, xg, nrm = L["q_sb"], L["k_sb"], L["v_sb"], L["siluz"], L["xg"], L["nrm"]
        gjfm, colb, colbb, tsc, eA, eTb = L["gjfm"], L["colb"], L["colbb"], L["tsc"], L["eA"], L["eTb"]
        gmlb, gmub, t2f = L["gmlb"], L["gmub"], L["t2f"]
        xcur, ycur, xrh, kcd, nkcdt, vnew, intrat, qeb, kdecb = \
            L["xcur"], L["ycur"], L["xrh"], L["kcd"], L["nkcdt"], L["vnew"], L["intrat"], L["qeb"], L["kdecb"]
        pa0, pa1, pa2, pa3, pa4, pa5, pa6, pa7 = \
            L["pa0"], L["pa1"], L["pa2"], L["pa3"], L["pa4"], L["pa5"], L["pa6"], L["pa7"]
        t_beta, t_gj, t_egj, t_edec, t_begj, t_cb, t_gs = \
            L["t_beta"], L["t_gj"], L["t_egj"], L["t_edec"], L["t_begj"], L["t_cb"], L["t_gs"]
        tmp8 = L["tmp8"]
        t_ssq = tmp8[:, 224:256]
        t_rstd = tmp8[:, 256:288]
        tok0 = half * HALF + cpar * CH
        csl = cpar * CH

        self.sync_fence()
        nc.tensor.wait_ge(c_dve.h, c_dve.val())
        nc.tensor.wait_ge(c_act.h, c_act.val())
        for g in range(4):
            for k in range(DC):
                par = k % 2
                self.w(nc.sync, c_pe, -1)
                base = (BLOB_OFF["wz"]) * 2048 + k * 128 * ZD + 1024 * g
                ap = bass.AP(self.wbf.tensor, base, [[ZD, 128], [1, 1024]])
                self.c_ws[par].inc(nc.sync.dma_start(
                    out=WS[:, 4096 + 1024 * par:4096 + 1024 * (par + 1)], in_=ap))
                self.w(nc.tensor, self.c_ws[par])
                if k == 0 and g > 0:
                    nc.tensor.wait_ge(c_act.h, c_act.val())
                nc.tensor.matmul(pa0[:, :], xln[:, bass.ds(512 * k + tok0, 128)],
                                 WS[:, 4096 + 1024 * par:4096 + 1024 * par + 512],
                                 start=(k == 0), stop=(k == DC - 1))
                mm = nc.tensor.matmul(pa1[:, :], xln[:, bass.ds(512 * k + tok0, 128)],
                                      WS[:, 4096 + 1024 * par + 512:4096 + 1024 * (par + 1)],
                                      start=(k == 0), stop=(k == DC - 1))
                c_pe.inc(mm)
            self.w(nc.scalar, c_pe)
            c_act.inc(nc.scalar.activation(out=siluz[:, 1024 * g:1024 * g + 512],
                                           in_=pa0[:, :], func=AF.Silu))
            c_act.inc(nc.scalar.activation(out=siluz[:, 1024 * g + 512:1024 * (g + 1)],
                                           in_=pa1[:, :], func=AF.Silu))
        for k in range(DC):
            mm = nc.tensor.matmul(pa2[:, 256:320], xln[:, bass.ds(512 * k + tok0, 128)],
                                  self.wba[:, 64 * k:64 * (k + 1)],
                                  start=(k == 0), stop=(k == DC - 1))
        c_pe.inc(mm)
        self.w(nc.scalar, c_pe)
        c_act.inc(nc.scalar.activation(out=t_beta, in_=pa2[:, 256:288], func=AF.Sigmoid))
        self.w(nc.vector, c_pe)
        c_dve.inc(nc.vector.tensor_add(tmp8[:, 288:320], pa2[:, 288:320], self.dtb_b[:, :]))
        self.w(nc.scalar, c_dve)
        c_act.inc(nc.scalar.activation(out=t_gj, in_=tmp8[:, 288:320], func=AF.Abs))
        self.w(nc.scalar, c_act)
        c_act.inc(nc.scalar.activation(out=t_egj, in_=t_gj, func=AF.Exp, scale=-1.0))
        self.w(nc.scalar, c_act)
        c_act.inc(nc.scalar.activation(out=t_egj, in_=t_egj, func=AF.Ln, bias=self.cvals[:, 2:3]))
        c_act.inc(nc.scalar.activation(out=t_edec, in_=tmp8[:, 288:320], func=AF.Relu))
        self.w(nc.vector, c_act)
        c_dve.inc(nc.vector.tensor_add(t_edec, t_edec, t_egj))
        self.w(nc.vector, c_dve)
        c_dve.inc(nc.vector.tensor_mul(tmp8[:, 288:320], t_edec, self.negeA_b[:, :]))
        g_tm = tmp8[:, 288:320]
        self.w(nc.tensor, c_dve)
        c_pe.inc(nc.tensor.matmul(pa3[0:32, 0:128], g_tm, K["mask_up01"][:, :],
                                  start=True, stop=True))
        self.w(nc.scalar, c_pe)
        c_act.inc(nc.scalar.copy(out=gjfm[:, :], in_=pa3[0:32, 0:128]))
        self.w(nc.tensor, c_act)
        c_pe.inc(nc.tensor.matmul(pa3[:, 128:160], K["ones128_f32"][:, :], g_tm,
                                  start=True, stop=True))
        c_pe.inc(nc.tensor.transpose(pa3[:, 160:192], gjfm[:, :], K["id_f32"][0:32, 0:32]))
        self.w(nc.scalar, c_pe)
        c_act.inc(nc.scalar.copy(out=t_gj, in_=pa3[:, 160:192]))
        self.w(nc.scalar, c_act)
        c_act.inc(nc.scalar.activation(out=t_egj, in_=t_gj, func=AF.Exp))
        self.w(nc.vector, c_act)
        c_dve.inc(nc.vector.tensor_sub(t_edec, pa3[:, 128:160], t_gj))
        self.w(nc.scalar, c_dve)
        c_act.inc(nc.scalar.activation(out=t_edec, in_=t_edec, func=AF.Exp))
        c_act.inc(nc.scalar.activation(out=t_cb, in_=pa3[:, 128:160], func=AF.Exp))
        self.w(nc.vector, c_act)
        c_dve.inc(nc.vector.tensor_mul(t_begj, t_beta, t_egj))
        for h in range(VH):
            j = h // 2
            hp = h % 2
            p6 = pa6 if hp == 0 else pa7
            pb = pa4 if hp == 0 else pa5
            prb = pa3[:, 0:128]
            qsl = q_sb[:, j * HALF + csl:j * HALF + csl + CH]
            ksl = k_sb[:, j * HALF + csl:j * HALF + csl + CH]
            vsl = v_sb[:, h * HALF + csl:h * HALF + csl + CH]
            stsl = self.st[:, 128 * h:128 * (h + 1)]
            stbsl = self.stbf[:, 128 * h:128 * (h + 1)]
            eT = eTb[:, 128 * hp:128 * (hp + 1)]
            gml = gmlb[:, 128 * (j % 2):128 * (j % 2 + 1)]
            gmu = gmub[:, 128 * (j % 2):128 * (j % 2 + 1)]
            xc = xcur[:, 128 * hp:128 * (hp + 1)]
            yc = ycur[:, 128 * hp:128 * (hp + 1)]
            xr = xrh[:, 256 * hp:256 * (hp + 1)]
            if hp == 0:
                nc.tensor.wait_ge(c_dve.h, c_dve.val())
                nc.tensor.matmul(pa2[:, 0:128], ksl, ksl, start=True, stop=True)
                c_pe.inc(nc.tensor.matmul(pa2[:, 128:256], ksl, qsl, start=True, stop=True))
                nc.tensor.wait_ge(c_dve.h, c_dve.val())
                c_pe.inc(nc.tensor.transpose(pa4[:, 512:640], ksl, K["id_bf"][:, :]))
                self.w(nc.vector, c_pe)
                c_dve.inc(nc.vector.tensor_mul(gml, pa2[:, 0:128], K["mask_lowneg"][:, :]))
                c_dve.inc(nc.vector.tensor_mul(gmu, pa2[:, 0:128], K["mask_upneg"][:, :]))
            c_dve.inc(nc.vector.tensor_scalar(colb[:, :], K["ones128_f32"][:, :],
                                              t_gj[:, h:h + 1], None, op0=ALU.mult))
            self.w(nc.tensor, c_dve)
            c_pe.inc(nc.tensor.transpose(prb, colb[:, :], K["id_f32"][:, :]))
            self.w(nc.vector, c_pe)
            c_dve.inc(nc.vector.tensor_scalar(tsc[:, :], prb, t_gj[:, h:h + 1], None,
                                              op0=ALU.subtract))
            self.w(nc.vector, c_dve)
            c_dve.inc(nc.vector.tensor_mul(eA[:, :], tsc[:, :], K["mask_lowneg"][:, :]))
            self.w(nc.scalar, c_dve)
            c_act.inc(nc.scalar.activation(out=eA[:, :], in_=eA[:, :], func=AF.Exp))
            c_dve.inc(nc.vector.tensor_mul(eT, tsc[:, :], K["mask_up01"][:, :]))
            self.w(nc.scalar, c_dve)
            c_act.inc(nc.scalar.activation(out=eT, in_=eT, func=AF.Exp))
            self.w(nc.vector, c_act)
            c_dve.inc(nc.vector.scalar_tensor_tensor(out=xc, in0=eA[:, :],
                                                     scalar=t_beta[:, h:h + 1], in1=gml,
                                                     op0=ALU.mult, op1=ALU.mult))
            c_dve.inc(nc.vector.tensor_scalar(colbb[:, :], K["ones128_bf"][:, :],
                                              t_beta[:, h:h + 1], None, op0=ALU.mult))
            self.w(nc.tensor, c_dve)
            c_pe.inc(nc.tensor.transpose(pb[:, 0:128], colbb[:, :], K["id_bf"][:, :]))
            self.w(nc.vector, c_pe)
            c_dve.inc(nc.vector.tensor_mul(t2f[:, :], eT, gmu))
            self.w(nc.vector, c_dve)
            c_dve.inc(nc.vector.tensor_mul(yc, t2f[:, :], pb[:, 0:128]))
            self.w(nc.tensor, c_dve)
            c_pe.inc(nc.tensor.transpose(pb[:, 256:384], vsl, K["id_bf"][:, :]))
            self.w(nc.vector, c_pe)
            c_dve.inc(nc.vector.tensor_scalar(xr[:, 0:128], pb[:, 256:384],
                                              t_beta[:, h:h + 1], None, op0=ALU.mult))
            c_dve.inc(nc.vector.tensor_scalar(xr[:, 128:256], pa4[:, 512:640],
                                              t_begj[:, h:h + 1], None, op0=ALU.mult))
            c_dve.inc(nc.vector.tensor_scalar(kdecb[:, 128 * hp:128 * (hp + 1)],
                                              pa4[:, 512:640],
                                              t_edec[:, h:h + 1], None, op0=ALU.mult))
            self.w(nc.tensor, c_dve)
            self.w(nc.tensor, c_act)
            nc.tensor.matmul(p6[:, 0:256], K["id_bf"][:, :], xr, start=True, stop=True)
            c_pe.inc(nc.tensor.matmul(p6[:, 0:256], yc, xr, start=False, stop=True,
                                      skip_group_check=True))
            for lev in range(1, 7):
                nc.tensor.wait_ge(c_act.h, c_act.val())
                nc.tensor.matmul(pa2[:, 256:384], xc, yc, start=True, stop=True)
                if lev < 6:
                    c_pe.inc(nc.tensor.matmul(pa2[:, 384:512], yc, xc, start=True, stop=True,
                                              skip_group_check=True))
                else:
                    c_pe.inc(nc.tensor.matmul(pa2[:, 384:512], yc, xc, start=True, stop=True,
                                              skip_group_check=True))
                self.w(nc.scalar, c_pe, 0)
                c_act.inc(nc.scalar.copy(out=xr, in_=p6[:, 0:256]))
                c_act.inc(nc.scalar.copy(out=yc, in_=pa2[:, 256:384]))
                c_act.inc(nc.scalar.copy(out=xc, in_=pa2[:, 384:512]))
                self.w(nc.tensor, c_act)
                c_pe.inc(nc.tensor.matmul(p6[:, 0:256], yc, xr, start=False, stop=True,
                                          skip_group_check=True))
            self.w(nc.scalar, c_pe)
            c_act.inc(nc.scalar.copy(out=kcd[:, 128 * hp:128 * (hp + 1)], in_=p6[:, 128:256]))
            self.w(nc.tensor, c_act)
            c_pe.inc(nc.tensor.transpose(pb[:, 384:512], kcd[:, 128 * hp:128 * (hp + 1)],
                                         K["id_bf"][:, :]))
            self.w(nc.scalar, c_pe)
            c_act.inc(nc.scalar.mul(nkcdt[:, 128 * hp:128 * (hp + 1)], pb[:, 384:512], -1.0))
            self.w(nc.tensor, c_act)
            self.w(nc.tensor, c_pool)
            c_pe.inc(nc.tensor.matmul(p6[:, 0:128], nkcdt[:, 128 * hp:128 * (hp + 1)],
                                      stbsl, start=False, stop=True, skip_group_check=True))
            self.w(nc.scalar, c_pe)
            c_act.inc(nc.scalar.copy(out=vnew[:, 128 * hp:128 * (hp + 1)], in_=p6[:, 0:128]))
            self.w(nc.vector, c_act)
            c_dve.inc(nc.vector.tensor_mul(t2f[:, :], pa2[:, 128:256], eT))
            self.w(nc.vector, c_dve)
            c_dve.inc(nc.vector.tensor_mul(intrat[:, 128 * hp:128 * (hp + 1)], t2f[:, :],
                                           K["mask_up01"][:, :]))
            c_dve.inc(nc.vector.tensor_scalar(colbb[:, :], K["ones128_bf"][:, :],
                                              t_egj[:, h:h + 1], None, op0=ALU.mult))
            self.w(nc.tensor, c_dve)
            c_pe.inc(nc.tensor.transpose(pb[:, 128:256], colbb[:, :], K["id_bf"][:, :]))
            self.w(nc.vector, c_pe)
            c_dve.inc(nc.vector.tensor_mul(qeb[:, 128 * hp:128 * (hp + 1)], qsl,
                                           pb[:, 128:256]))
            self.w(nc.tensor, c_dve)
            nc.tensor.wait_ge(c_dve.h, c_dve.val())
            self.w(nc.tensor, c_pool)
            nc.tensor.matmul(p6[:, 384:512], intrat[:, 128 * hp:128 * (hp + 1)],
                             vnew[:, 128 * hp:128 * (hp + 1)], start=True, stop=True)
            c_pe.inc(nc.tensor.matmul(p6[:, 384:512], qeb[:, 128 * hp:128 * (hp + 1)],
                                      stbsl, start=False, stop=True, skip_group_check=True))
            self.w(nc.vector, c_pe)
            c_dve.inc(nc.vector.tensor_mul(xg[:, 128 * h:128 * (h + 1)], p6[:, 384:512],
                                           siluz[:, 128 * h:128 * (h + 1)]))
            pss = pa3[:, 288:416]
            c_pe.inc(nc.tensor.matmul(pss, kdecb[:, 128 * hp:128 * (hp + 1)],
                                      vnew[:, 128 * hp:128 * (hp + 1)], start=True, stop=True))
            self.w(nc.vector, c_pe)
            c_dve.inc(nc.vector.tensor_scalar(stsl, stsl, t_cb[:, h:h + 1], None, op0=ALU.mult))
            self.w(nc.vector, c_dve)
            c_dve.inc(nc.vector.tensor_add(stsl, stsl, pss))
            self.w(nc.gpsimd, c_dve)
            c_pool.inc(nc.gpsimd.tensor_copy(out=stbsl, in_=stsl))
        self.w(nc.scalar, c_dve)
        c_act.inc(nc.scalar.activation(out=siluz[:, :], in_=xg[:, :], func=AF.Square))
        self.w(nc.vector, c_act)
        c_dve.inc(nc.vector.tensor_reduce(
            t_ssq, siluz[:, :].rearrange("p (h u) -> p h u", u=128), axis=AX.X,
            op=ALU.add))
        self.w(nc.vector, c_dve)
        c_dve.inc(nc.vector.tensor_scalar(t_ssq, t_ssq, 1.0 / 128.0, None, op0=ALU.mult))
        self.w(nc.scalar, c_dve)
        c_act.inc(nc.scalar.activation(out=t_rstd, in_=t_ssq, func=AF.Ln,
                                       bias=self.cvals[:, 4:5]))
        self.w(nc.scalar, c_act)
        c_act.inc(nc.scalar.activation(out=t_rstd, in_=t_rstd, func=AF.Exp, scale=-0.5))
        self.w(nc.vector, c_act)
        for h in range(VH):
            c_dve.inc(nc.vector.tensor_scalar(xg[:, 128 * h:128 * (h + 1)],
                                              xg[:, 128 * h:128 * (h + 1)],
                                              t_rstd[:, h:h + 1], None, op0=ALU.mult))
        self.w(nc.vector, c_dve)
        c_dve.inc(nc.vector.tensor_mul(xg[:, :], xg[:, :], self.normw_rep[:, :]))
        for fc in range(32):
            pslot = pa4 if fc % 2 == 0 else pa5
            self.w(nc.tensor, c_dve)
            if fc >= 2:
                self.w(nc.tensor, c_act)
            c_pe.inc(nc.tensor.transpose(pslot[:, 640:768], xg[:, 128 * fc:128 * (fc + 1)],
                                         K["id_bf"][:, :]))
            self.w(nc.scalar, c_pe)
            c_act.inc(nc.scalar.copy(out=nrm[:, fc * HALF + 128 * cpar:fc * HALF + 128 * cpar + 128],
                                     in_=pslot[:, 640:768]))

    def attn_outproj(self, half, L):
        nc = self.nc
        c_pe, c_act, c_dve = self.c_pe, self.c_act, self.c_dve
        WS = self.WS
        nrm = L["nrm"]
        pa0, pa1 = L["pa0"], L["pa1"]
        self.sync_fence()
        nc.sync.wait_ge(c_pe.h, c_pe.val())
        nc.tensor.wait_ge(c_act.h, c_act.val())

        def dpair(jj):
            for par in range(2):
                d = jj * 2 + par
                self.w(nc.sync, c_dve, par - 1)
                self.wdma(self.c_ws[par], WS[:, 4096 + 4096 * par:4096 + 4096 * (par + 1)],
                          "woutt", d * 256, 4096)
            for par in range(2):
                d = jj * 2 + par
                pp = pa0 if par == 0 else pa1
                self.w(nc.tensor, self.c_ws[par])
                for fc in range(32):
                    mm = nc.tensor.matmul(pp[:, 0:HALF],
                                          WS[:, 4096 + 4096 * par + 128 * fc:4096 + 4096 * par + 128 * (fc + 1)],
                                          nrm[:, fc * HALF:(fc + 1) * HALF],
                                          start=(fc == 0), stop=(fc == 31))
                c_pe.inc(mm)
                self.w(nc.vector, c_pe)
                c_dve.inc(nc.vector.tensor_add(
                    self.x_res[:, bass.ds(512 * d + HALF * half, HALF)],
                    self.x_res[:, bass.ds(512 * d + HALF * half, HALF)], pp[:, 0:HALF]))
        InnerLoop(nc, self.tile_ctrs, 8, dpair)



def make_in_maps(inputs):
    blob = prep_weights(inputs)
    consts = prep_consts()
    convw = np.asarray(inputs["L0_attn_conv_weight"], np.float32)
    convw_r = np.ascontiguousarray(
        convw.reshape(CC, 128, KCONV).transpose(1, 0, 2).reshape(128, CC * KCONV))
    smalls = np.zeros((128, 512), np.float32)
    smalls[0, 0:32] = np.asarray(inputs["L0_attn_dt_bias"], np.float32)
    smalls[0, 32:64] = np.asarray(inputs["L0_attn_A_log"], np.float32)
    smalls[0, 64:192] = np.asarray(inputs["L0_attn_norm_w"], np.float32)
    cvals = np.zeros((128, 16), np.float32)
    cvals[:, 0] = LN_EPS
    cvals[:, 1] = L2_EPS
    cvals[:, 2] = 1.0
    cvals[:, 3] = -0.5 * np.log(128.0)
    cvals[:, 4] = RMS_EPS
    in_maps = []
    for core in range(8):
        m = {"xin": prep_core_x(inputs, core),
             "wf32": blob, "convw": convw_r, "smalls": smalls, "cvals": cvals}
        for nm, shp, dt in CONST_SPECS:
            m["c_" + nm] = consts[nm]
        in_maps.append(m)
    return in_maps


def run_dev(inputs, do_attn=True, dev_tiles=None, dev_stage=None):
    b = Builder(do_attn=do_attn, dev_tiles=dev_tiles, dev_stage=dev_stage)
    nc = b.build()
    in_maps = make_in_maps(inputs)
    res = run_bass_kernel_spmd(nc, in_maps, core_ids=list(range(8)))
    return [r["xout"] for r in res.results]


def kernel(**inputs):
    outs = run_dev(inputs, do_attn=True, dev_tiles=None)
    full = np.empty((16, 2048, 2048), np.float32)
    for core in range(8):
        o = outs[core].reshape(NSEQ, NTILE, DC, 128, P)
        xf = o.transpose(0, 2, 3, 1, 4).reshape(NSEQ, D, 2048)
        full[2 * core:2 * core + 2] = xf.transpose(0, 2, 1)
    return full


# version 02
# speedup vs baseline: 1.1899x; 1.1899x over previous
import contextlib
import numpy as np
import ml_dtypes
import concourse.bass as bass
import concourse.bacc as bacc
import concourse.mybir as mybir
from concourse.bass_utils import run_bass_kernel_spmd

AF = mybir.ActivationFunctionType
ALU = mybir.AluOpType
AX = mybir.AxisListType
F32 = mybir.dt.float32
BF16 = mybir.dt.bfloat16
BF = ml_dtypes.bfloat16

D = 2048
DC = 16
P = 512
NTILE = 4
NSEQ = 2
F = 5632
FC = 44
E = 8
R = 16
CONVD = 8192
CC = 64
KH = 16
VH = 32
A = 128
U = 128
CH = 128
HALF = 256
ZD = 4096
KCONV = 4
LN_EPS = 1e-5
L2_EPS = 1e-6
RMS_EPS = 1e-6

def _blob_layout():
    off = {}
    r = 0
    def add(name, rows):
        nonlocal r
        off[name] = r
        r += rows
    add("w1t", FC * 128)
    add("w3t", FC * 128)
    add("w2t", DC * 352)
    add("qkvt", CC * 128)
    add("wz", DC * 256)
    add("woutt", DC * 256)
    add("wba", 64)
    add("b1p", FC * 64)
    add("b3p", FC * 64)
    add("a2p", FC * 32)
    add("b2p", DC * 64)
    add("a1t", 128)
    add("a3t", 128)
    add("gatet", 8)
    if r % 256:
        add("_pad", 256 - r % 256)
    return off, r

BLOB_OFF, BLOB_ROWS = _blob_layout()

def _tile_lhs(w):
    K_, M_ = w.shape
    return w.reshape(K_ // 128, 128, M_ // 128, 128).transpose(2, 0, 1, 3)


def prep_weights(i):
    blob = np.zeros((BLOB_ROWS, 2048), np.float32)

    def put(name, arr):
        rows = arr.size // 2048
        assert rows * 2048 == arr.size
        blob[BLOB_OFF[name]:BLOB_OFF[name] + rows] = \
            np.ascontiguousarray(arr, np.float32).reshape(rows, 2048)

    w1 = np.asarray(i["L0_ffn_w1"], np.float32)
    w3 = np.asarray(i["L0_ffn_w3"], np.float32)
    w2 = np.asarray(i["L0_ffn_w2"], np.float32)
    put("w1t", np.ascontiguousarray(_tile_lhs(w1.T).transpose(0, 2, 1, 3)))
    put("w3t", np.ascontiguousarray(_tile_lhs(w3.T).transpose(0, 2, 1, 3)))
    w2t = _tile_lhs(w2.T)
    put("w2t", np.ascontiguousarray(w2t.transpose(0, 2, 1, 3)).reshape(DC, 128, FC * 128))
    wqkv = np.asarray(i["L0_attn_W_qkv"], np.float32)
    put("qkvt", np.ascontiguousarray(_tile_lhs(wqkv).transpose(0, 2, 1, 3)))
    wz = np.asarray(i["L0_attn_W_z"], np.float32)
    put("wz", wz.reshape(DC, 128, ZD))
    wout = np.asarray(i["L0_attn_W_out"], np.float32)
    wot = wout.reshape(32, 128, DC, 128).transpose(2, 0, 1, 3)
    put("woutt", np.ascontiguousarray(wot.transpose(0, 2, 1, 3)).reshape(DC, 128, 32 * 128))
    wba = np.concatenate([np.asarray(i["L0_attn_W_b"], np.float32),
                          np.asarray(i["L0_attn_W_a"], np.float32)], axis=1)
    put("wba", wba.reshape(DC, 128, 64))

    def bpad(b):
        out = np.zeros((FC, 128, 1024), np.float32)
        for e in range(E):
            t = b[e].reshape(FC, 128, R).transpose(0, 2, 1)
            out[:, 16 * e:16 * e + 16, 128 * e:128 * (e + 1)] = t
        return out
    put("b1p", bpad(np.asarray(i["L0_ffn_b1"], np.float32)))
    put("b3p", bpad(np.asarray(i["L0_ffn_b3"], np.float32)))
    a2 = np.asarray(i["L0_ffn_a2"], np.float32)
    a2p = np.zeros((FC, 128, 512), np.float32)
    for e in range(E):
        t = a2[e].reshape(R, FC, 128).transpose(1, 2, 0)
        a2p[:, :, 64 * e + 16 * (e % 4):64 * e + 16 * (e % 4) + 16] = t
    put("a2p", a2p)
    b2 = np.asarray(i["L0_ffn_b2"], np.float32)
    b2p = np.zeros((DC, 128, 1024), np.float32)
    for e in range(E):
        rbase = 64 * (e // 4) + 16 * (e % 4)
        t = b2[e].reshape(DC, 128, R).transpose(0, 2, 1)
        b2p[:, rbase:rbase + 16, 128 * e:128 * (e + 1)] = t
    put("b2p", b2p)
    a1 = np.asarray(i["L0_ffn_a1"], np.float32).reshape(E * R, D)
    put("a1t", a1.T.reshape(DC, 128, 128))
    a3 = np.asarray(i["L0_ffn_a3"], np.float32).reshape(E * R, D)
    put("a3t", a3.T.reshape(DC, 128, 128))
    gate = np.asarray(i["L0_ffn_gate_w"], np.float32)
    put("gatet", gate.T.reshape(DC, 128, 8))
    return blob


def prep_consts():
    c = {}
    idx = np.arange(128)
    l, m = np.meshgrid(idx, idx, indexing="ij")
    c["mask_low01"] = (m <= l).astype(np.float32)
    c["mask_up01"] = (m >= l).astype(np.float32)
    c["mask_lowneg"] = -(m < l).astype(np.float32)
    c["mask_upneg"] = -(m > l).astype(np.float32)
    c["id_bf"] = np.eye(128).astype(BF)
    c["id_f32"] = np.eye(128).astype(np.float32)
    c["zeros_bf"] = np.zeros((128, 128), BF)
    c["ones128_bf"] = np.ones((128, 128), BF)
    c["ones128_f32"] = np.ones((128, 128), np.float32)
    o8 = np.zeros((128, 8 * 128), BF)
    for e in range(8):
        o8[e, 128 * e:128 * (e + 1)] = 1.0
    c["onehot8p"] = o8
    return c


CONST_SPECS = [("mask_low01", [128, 128], F32), ("mask_up01", [128, 128], F32),
               ("mask_lowneg", [128, 128], F32), ("mask_upneg", [128, 128], F32),
               ("id_bf", [128, 128], BF16), ("id_f32", [128, 128], F32),
               ("zeros_bf", [128, 128], BF16),
               ("ones128_bf", [128, 128], BF16),
               ("ones128_f32", [128, 128], F32),
               ("onehot8p", [128, 1024], BF16)]


def prep_core_x(inputs, core):
    hid = np.asarray(inputs["L0_attn_hidden"], np.float32)
    xs = hid[2 * core:2 * core + 2]
    xf = xs.transpose(0, 2, 1).reshape(NSEQ, DC, 128, NTILE, P).transpose(0, 3, 1, 2, 4)
    return np.ascontiguousarray(xf, np.float32).reshape(NSEQ * NTILE, DC, 128, P)


class Ctr:
    def __init__(self, nc, name, unit=1):
        self._cm = nc.semaphore(name)
        self.h = self._cm.__enter__()
        self.stat = 0
        self.terms = []
        self.unit = unit

    def inc(self, instr, n=1):
        instr.then_inc(self.h, n * self.unit)
        self.stat += n * self.unit

    def val(self, ofs=0):
        v = self.stat + ofs * self.unit
        for reg, coef in self.terms:
            if coef:
                v = reg * coef + v
        return v

    def is_static(self):
        return not any(coef for _, coef in self.terms)

    def clear(self):
        assert not self.terms
        self.stat = 0


def affine_fori(nc, ctrs, start, n, body, deltas):
    if n <= start:
        return
    entry = [c.stat for c in ctrs]
    with nc.Fori(start, n) as i:
        for c, d in zip(ctrs, deltas):
            c.terms.append([i + (-start), d])
        body(i)
    for c, en, d in zip(ctrs, entry, deltas):
        c.terms.pop()
        assert c.stat - en == d, f"body delta mismatch {c.stat - en} vs {d}"
        c.stat = en + d * (n - start)


class InnerLoop:
    def __init__(self, nc, ctrs, n, body):
        assert n >= 2
        s0 = [c.stat for c in ctrs]
        body(0)
        s1 = [c.stat for c in ctrs]
        body(1)
        deltas = [c.stat - b for c, b in zip(ctrs, s1)]
        for c, b0, b1, d in zip(ctrs, s0, s1, deltas):
            assert b1 - b0 == d, f"nonuniform delta {b1 - b0} vs {d}"
        if n > 2:
            entry = [c.stat for c in ctrs]
            with nc.Fori(2, n) as i:
                for c, d in zip(ctrs, deltas):
                    c.terms.append([i + (-2), d])
                body(i)
            for c, en, d in zip(ctrs, entry, deltas):
                c.terms.pop()
                assert c.stat - en == d, f"body delta changed {c.stat - en} vs {d}"
                c.stat = en + d * (n - 2)


class Builder:
    def __init__(self, do_attn=True, dev_tiles=None, dev_stage=None):
        self.do_attn = do_attn
        self.dev_tiles = dev_tiles
        self.dev_stage = dev_stage
        nc = self.nc = bacc.Bacc("TRN2", target_bir_lowering=False, debug=False,
                                 dynamic_dma_scratch_size=16384)
        self.ctx = []
        self.xin = nc.dram_tensor("xin", [NSEQ * NTILE, DC, 128, P], F32, kind="ExternalInput").ap()
        self.xout = nc.dram_tensor("xout", [NSEQ * NTILE, DC, 128, P], F32, kind="ExternalOutput").ap()
        self.wf32 = nc.dram_tensor("wf32", [BLOB_ROWS, 2048], F32, kind="ExternalInput").ap()
        self.wbf = nc.dram_tensor("wbf", [BLOB_ROWS, 2048], BF16, kind="Internal").ap()
        self.convw_d = nc.dram_tensor("convw", [128, 256], F32, kind="ExternalInput").ap()
        self.smalls_d = nc.dram_tensor("smalls", [128, 512], F32, kind="ExternalInput").ap()
        self.cvals_d = nc.dram_tensor("cvals", [128, 16], F32, kind="ExternalInput").ap()
        self.cst_d = {}
        for nm, shp, dt in CONST_SPECS:
            self.cst_d[nm] = nc.dram_tensor("c_" + nm, shp, dt, kind="ExternalInput").ap()

    def sb(self, name, shape, dtype):
        t = self.nc.sbuf_tensor(name, shape, dtype)
        h = t.__enter__()
        self.ctx.append(t)
        return h

    def sync_fence(self):
        nc = self.nc
        nc.sync.wait_ge(self.c_ws[0].h, self.c_ws[0].val())
        nc.sync.wait_ge(self.c_ws[1].h, self.c_ws[1].val())

    def w(self, eng, ctr, ofs=0):
        v = ctr.val(ofs)
        if isinstance(v, int):
            if v <= 0:
                return
        eng.wait_ge(ctr.h, v)

    def build(self):
        nc = self.nc
        self.x_res = self.sb("x_res", [128, DC * P], F32)
        self.xln = self.sb("xln", [128, DC * P], BF16)
        self.WS = self.sb("WS", [128, 13312], BF16)
        self.st = self.sb("st", [128, VH * 128], F32)
        self.stbf = self.sb("stbf", [128, VH * 128], BF16)
        self.K = {}
        for nm, shp, dt in CONST_SPECS:
            self.K[nm] = self.sb("k_" + nm, shp, dt)
        self.convw = self.sb("convw_sb", [128, 256], F32)
        self.dtb_b = self.sb("dtb_b", [128, 32], F32)
        self.negeA_b = self.sb("negeA_b", [128, 32], F32)
        self.normw_rep = self.sb("normw_rep", [128, ZD], BF16)
        self.gates = self.sb("gates", [128, DC * 8], BF16)
        self.wba = self.sb("wba_sb", [128, DC * 64], BF16)
        self.a1s = self.sb("a1s", [128, DC * 128], BF16)
        self.a3s = self.sb("a3s", [128, DC * 128], BF16)
        self.smalls = self.sb("smalls_sb", [128, 512], F32)
        self.cvals = self.sb("cvals_sb", [128, 16], F32)
        self.ctail = self.sb("ctail", [128, CC * 4], BF16)

        self.c_w = Ctr(nc, "c_w", unit=16)
        self.c_ws = [Ctr(nc, "c_ws0", unit=16), Ctr(nc, "c_ws1", unit=16)]
        self.c_x = Ctr(nc, "c_x", unit=16)
        self.c_s = Ctr(nc, "c_s", unit=16)
        self.c_pe = Ctr(nc, "c_pe")
        self.c_act = Ctr(nc, "c_act")
        self.c_dve = Ctr(nc, "c_dve")
        self.c_pool = Ctr(nc, "c_pool")
        self.tile_ctrs = [self.c_w, self.c_ws[0], self.c_ws[1], self.c_x, self.c_s,
                          self.c_pe, self.c_act, self.c_dve, self.c_pool]

        self.emit_prologue()
        ntiles_total = self.dev_tiles if self.dev_tiles is not None else NSEQ * NTILE
        if ntiles_total == 0:
            nc.finalize()
            return nc
        done = 0
        deltas = None
        for seq in range(NSEQ):
            if done >= ntiles_total:
                break
            self.emit_seq_reset()
            ntile = min(NTILE, ntiles_total - done)
            start = 0
            if deltas is None:
                s0 = [c.stat for c in self.tile_ctrs]
                self.emit_tile(seq, 0)
                deltas = [c.stat - b for c, b in zip(self.tile_ctrs, s0)]
                start = 1
            affine_fori(nc, self.tile_ctrs, start, ntile,
                        lambda t, seq=seq: self.emit_tile(seq, t), deltas)
            done += ntile
        nc.finalize()
        return nc

    def emit_prologue(self):
        nc = self.nc
        c_w, c_pe, c_act, c_dve, c_pool = self.c_w, self.c_pe, self.c_act, self.c_dve, self.c_pool
        for nm, shp, dt in CONST_SPECS:
            c_w.inc(nc.sync.dma_start(out=self.K[nm][:, :], in_=self.cst_d[nm][:, :]))
        c_w.inc(nc.sync.dma_start(out=self.convw[:, :], in_=self.convw_d[:, :]))
        c_w.inc(nc.sync.dma_start(out=self.smalls[:, :], in_=self.smalls_d[:, :]))
        c_w.inc(nc.sync.dma_start(out=self.cvals[:, :], in_=self.cvals_d[:, :]))
        NCAST = BLOB_ROWS // 128
        wr32 = self.wf32.rearrange("(n p) d -> n p d", p=128)
        wrbf = self.wbf.rearrange("(n p) d -> n p d", p=128)
        c_cast = Ctr(nc, "c_cast")
        c_cp = [Ctr(nc, "c_cp0"), Ctr(nc, "c_cp1")]
        c_ld = [Ctr(nc, "c_ld0", unit=16), Ctr(nc, "c_ld1", unit=16)]
        c_st = [Ctr(nc, "c_st0", unit=16), Ctr(nc, "c_st1", unit=16)]
        with nc.sbuf_tensor("cast_in", [128, 2 * 2048], F32) as cin, \
             nc.sbuf_tensor("cast_out", [128, 2 * 2048], BF16) as cout:
            def cast_body(j):
                for par in range(2):
                    self.w(nc.sync, c_cp[par], 0)
                    c_ld[par].inc(nc.sync.dma_start(out=cin[:, par * 2048:(par + 1) * 2048],
                                                    in_=wr32[j * 2 + par, :, :]))
                for par in range(2):
                    self.w(nc.gpsimd, c_ld[par])
                    self.w(nc.gpsimd, c_st[par], 0)
                    c_cp[par].inc(nc.gpsimd.tensor_copy(out=cout[:, par * 2048:(par + 1) * 2048],
                                                        in_=cin[:, par * 2048:(par + 1) * 2048]))
                for par in range(2):
                    self.w(nc.scalar, c_cp[par])
                    c_st[par].inc(nc.scalar.dma_start(out=wrbf[j * 2 + par, :, :],
                                                      in_=cout[:, par * 2048:(par + 1) * 2048]))
            InnerLoop(nc, [c_ld[0], c_ld[1], c_cp[0], c_cp[1], c_st[0], c_st[1]],
                      NCAST // 2, cast_body)
            nc.scalar.wait_ge(c_st[0].h, c_st[0].val())
            nc.scalar.wait_ge(c_st[1].h, c_st[1].val())
            nc.sync.wait_ge(c_ld[0].h, c_ld[0].val())
            nc.sync.wait_ge(c_ld[1].h, c_ld[1].val())
        nc.all_engine_barrier()
        with nc.psum_tensor("p_pro", [128, 512], F32) as pb:
            nc.tensor.wait_ge(c_w.h, c_w.val())
            mm = nc.tensor.matmul(pb[:, 0:192], self.K["ones128_f32"][0:1, :],
                                  self.smalls[0:1, 0:192], start=True, stop=True)
            c_pe.inc(mm)
            self.w(nc.vector, self.c_pe)
            c_dve.inc(nc.vector.tensor_copy(out=self.dtb_b[:, :], in_=pb[:, 0:32]))
            self.w(nc.scalar, self.c_pe)
            c_act.inc(nc.scalar.activation(out=self.negeA_b[:, :], in_=pb[:, 32:64], func=AF.Exp))
            self.w(nc.vector, self.c_act)
            c_dve.inc(nc.vector.tensor_scalar_mul(self.negeA_b[:, :], self.negeA_b[:, :], -1.0))
            self.w(nc.vector, self.c_pe)
            c_dve.inc(nc.vector.tensor_copy(out=self.normw_rep[:, 0:128], in_=pb[:, 64:192]))
            for rep in range(1, VH):
                self.w(nc.gpsimd, self.c_dve)
                c_pool.inc(nc.gpsimd.tensor_copy(out=self.normw_rep[:, 128 * rep:128 * (rep + 1)],
                                                 in_=self.normw_rep[:, 0:128]))
            nc.sync.wait_ge(c_w.h, c_w.val())
            for nm, dst, width in [("gatet", self.gates, 8),
                                   ("a1t", self.a1s, 128),
                                   ("a3t", self.a3s, 128),
                                   ("wba", self.wba, 64)]:
                base = BLOB_OFF[nm] * 2048
                ap = bass.AP(self.wbf.tensor, base,
                             [[width, 128], [128 * width, DC], [1, width]])
                dstv = dst[:, :].rearrange("p (k c) -> p k c", c=width)
                c_w.inc(nc.sync.dma_start(out=dstv, in_=ap))
            nc.sync.wait_ge(c_w.h, c_w.val())
        nc.all_engine_barrier()

    def emit_seq_reset(self):
        nc = self.nc
        nc.vector.memset(self.st[:, :], 0.0)
        nc.vector.memset(self.stbf[:, :], 0.0)
        nc.gpsimd.memset(self.ctail[:, :], 0.0)
        nc.all_engine_barrier()

    def wdma(self, ctr, dst_ap, name, row_off, nfree):
        base = (BLOB_OFF[name] + row_off) * 2048
        ap = bass.AP(self.wbf.tensor, base, [[nfree, 128], [1, nfree]])
        ctr.inc(self.nc.sync.dma_start(out=dst_ap, in_=ap))

    def emit_tile(self, seq, t):
        nc = self.nc
        idx = seq * NTILE + t
        xap = bass.AP(self.xin.tensor, idx * (DC * 128 * P),
                      [[P, 128], [128 * P, DC], [1, P]])
        self.c_x.inc(nc.gpsimd.dma_start(
            out=self.x_res[:, :].rearrange("p (k c) -> p k c", c=P), in_=xap))
        nc.gpsimd.wait_ge(self.c_x.h, self.c_x.val())
        nc.all_engine_barrier()
        self.emit_ffn()
        nc.all_engine_barrier()
        if self.do_attn:
            self.emit_ln_standalone()
            nc.all_engine_barrier()
            self.emit_attn()
            nc.all_engine_barrier()
        if not (self.dev_stage or "").startswith("attn"):
            self.emit_ffn()
            nc.all_engine_barrier()
        yap = bass.AP(self.xout.tensor, idx * (DC * 128 * P),
                      [[P, 128], [128 * P, DC], [1, P]])
        self.c_x.inc(nc.gpsimd.dma_start(
            out=yap, in_=self.x_res[:, :].rearrange("p (k c) -> p k c", c=P)))
        nc.gpsimd.wait_ge(self.c_x.h, self.c_x.val())
        nc.sync.wait_ge(self.c_w.h, self.c_w.val())
        nc.sync.wait_ge(self.c_ws[0].h, self.c_ws[0].val())
        nc.sync.wait_ge(self.c_ws[1].h, self.c_ws[1].val())
        nc.scalar.wait_ge(self.c_s.h, self.c_s.val())
        nc.all_engine_barrier()

    def emit_ln_standalone(self):
        nc = self.nc
        K = self.K
        c_pe, c_act, c_dve, c_pool = self.c_pe, self.c_act, self.c_dve, self.c_pool
        self._ln_n = getattr(self, "_ln_n", 0) + 1
        lsx = f"_l{self._ln_n}"
        with nc.sbuf_tensor("l_sc" + lsx, [128, 2 * 512], F32) as sc32, \
             nc.sbuf_tensor("l_rows" + lsx, [1, 2048], F32) as rows, \
             nc.sbuf_tensor("l_sq" + lsx, [128, 2 * 512], BF16) as sqb, \
             nc.psum_tensor("lp6" + lsx, [128, 512], F32) as p6, \
             nc.psum_tensor("lp7" + lsx, [128, 512], F32) as p7:
            xr = self.x_res
            xln = self.xln
            for k in range(DC):
                c_pool.inc(nc.gpsimd.tensor_copy(out=xln[:, 512 * k:512 * (k + 1)],
                                                 in_=xr[:, 512 * k:512 * (k + 1)]))
            pe_sq0 = c_pe.val(0)
            for k in range(DC):
                self.w(nc.scalar, c_pool, -(DC - 1 - k))
                if k >= 2:
                    nc.scalar.wait_ge(c_pe.h, pe_sq0 + (k - 1))
                c_act.inc(nc.scalar.activation(out=sqb[:, 512 * (k % 2):512 * (k % 2 + 1)],
                                               in_=xln[:, 512 * k:512 * (k + 1)], func=AF.Square))
                self.w(nc.tensor, c_pool, -(DC - 1 - k))
                nc.tensor.matmul(p6[0:1, :], K["ones128_bf"][:, 0:1], xln[:, 512 * k:512 * (k + 1)],
                                 start=(k == 0), stop=(k == DC - 1))
                self.w(nc.tensor, c_act)
                c_pe.inc(nc.tensor.matmul(p7[0:1, :], K["ones128_bf"][:, 0:1],
                                          sqb[:, 512 * (k % 2):512 * (k % 2 + 1)],
                                          start=(k == 0), stop=(k == DC - 1)))
            self.w(nc.scalar, c_pe)
            c_act.inc(nc.scalar.mul(rows[0:1, 0:512], p6[0:1, :], 1.0 / D))
            c_act.inc(nc.scalar.mul(rows[0:1, 512:1024], p7[0:1, :], 1.0 / D))
            self.w(nc.vector, c_act)
            c_dve.inc(nc.vector.tensor_mul(rows[0:1, 1024:1536], rows[0:1, 0:512], rows[0:1, 0:512]))
            self.w(nc.vector, c_dve)
            c_dve.inc(nc.vector.tensor_sub(rows[0:1, 512:1024], rows[0:1, 512:1024], rows[0:1, 1024:1536]))
            self.w(nc.scalar, c_dve)
            c_act.inc(nc.scalar.activation(out=rows[0:1, 1024:1536], in_=rows[0:1, 512:1024],
                                           func=AF.Ln, bias=self.cvals[0:1, 0:1]))
            self.w(nc.scalar, c_act)
            c_act.inc(nc.scalar.activation(out=rows[0:1, 512:1024], in_=rows[0:1, 1024:1536],
                                           func=AF.Exp, scale=-0.5))
            self.w(nc.vector, c_act)
            c_dve.inc(nc.vector.tensor_mul(rows[0:1, 1024:1536], rows[0:1, 0:512], rows[0:1, 512:1024]))
            self.w(nc.tensor, c_dve)
            nc.tensor.matmul(p6[:, :], K["ones128_f32"][0:1, :], rows[0:1, 512:1024], start=True, stop=True)
            c_pe.inc(nc.tensor.matmul(p7[:, :], K["ones128_f32"][0:1, :], rows[0:1, 1024:1536],
                                      start=True, stop=True))
            self.w(nc.scalar, c_pe)
            c_act.inc(nc.scalar.copy(out=sc32[:, 0:512], in_=p6[:, :]))
            c_act.inc(nc.scalar.copy(out=sc32[:, 512:1024], in_=p7[:, :]))
            for k in range(DC):
                self.w(nc.gpsimd, c_act)
                c_pool.inc(nc.gpsimd.tensor_mul(xln[:, 512 * k:512 * (k + 1)],
                                                xr[:, 512 * k:512 * (k + 1)], sc32[:, 0:512]))
                self.w(nc.vector, c_pool)
                c_dve.inc(nc.vector.tensor_sub(xln[:, 512 * k:512 * (k + 1)],
                                               xln[:, 512 * k:512 * (k + 1)], sc32[:, 512:1024]))
            nc.vector.wait_ge(c_dve.h, c_dve.val())

    def emit_ffn(self):
        nc = self.nc
        K = self.K
        self._ffn_n = getattr(self, "_ffn_n", 0) + 1
        sfx = f"_{self._ffn_n}"
        c_w, c_pe, c_act, c_dve, c_pool = self.c_w, self.c_pe, self.c_act, self.c_dve, self.c_pool
        with nc.sbuf_tensor("f_sc" + sfx, [128, 3 * 512], F32) as sc32, \
             nc.sbuf_tensor("f_rows" + sfx, [1, 2048], F32) as rows, \
             nc.sbuf_tensor("f_sq" + sfx, [128, 2 * 512], BF16) as sqb, \
             nc.sbuf_tensor("f_base" + sfx, [128, 2 * 512], BF16) as baseb, \
             nc.sbuf_tensor("f_sg" + sfx, [128, 6 * 512], BF16) as sgb, \
             nc.sbuf_tensor("f_t1" + sfx, [128, 2 * 512], BF16) as t1b, \
             nc.sbuf_tensor("f_t2" + sfx, [128, 512], BF16) as t2ev, \
             nc.sbuf_tensor("f_wb" + sfx, [128, 8 * 512], BF16) as wb8, \
             nc.sbuf_tensor("f_wt" + sfx, [32, 512], BF16) as wT32, \
             nc.sbuf_tensor("f_wfl" + sfx, [128, 64], F32) as wfl, \
             nc.sbuf_tensor("f_sact" + sfx, [128, FC * 512], BF16) as sact, \
             nc.psum_tensor("fp0" + sfx, [128, 512], F32) as p0, \
             nc.psum_tensor("fp1" + sfx, [128, 512], F32) as p1, \
             nc.psum_tensor("fp2" + sfx, [128, 512], F32) as p2, \
             nc.psum_tensor("fp3" + sfx, [128, 512], F32) as p3, \
             nc.psum_tensor("fp4" + sfx, [128, 512], F32) as p4, \
             nc.psum_tensor("fp5" + sfx, [128, 512], F32) as p5, \
             nc.psum_tensor("fp6" + sfx, [128, 512], F32) as p6, \
             nc.psum_tensor("fp7" + sfx, [128, 512], F32) as p7:
            xr = self.x_res
            xln = self.xln
            for k in range(DC):
                c_pool.inc(nc.gpsimd.tensor_copy(out=xln[:, 512 * k:512 * (k + 1)],
                                                 in_=xr[:, 512 * k:512 * (k + 1)]))
            pe_sq0 = c_pe.val(0)
            for k in range(DC):
                self.w(nc.scalar, c_pool, -(DC - 1 - k))
                if k >= 2:
                    nc.scalar.wait_ge(c_pe.h, pe_sq0 + (k - 1))
                c_act.inc(nc.scalar.activation(out=sqb[:, 512 * (k % 2):512 * (k % 2 + 1)],
                                               in_=xln[:, 512 * k:512 * (k + 1)], func=AF.Square))
                self.w(nc.tensor, c_pool, -(DC - 1 - k))
                nc.tensor.matmul(p6[0:1, :], K["ones128_bf"][:, 0:1], xln[:, 512 * k:512 * (k + 1)],
                                 start=(k == 0), stop=(k == DC - 1))
                self.w(nc.tensor, c_act)
                c_pe.inc(nc.tensor.matmul(p7[0:1, :], K["ones128_bf"][:, 0:1],
                                          sqb[:, 512 * (k % 2):512 * (k % 2 + 1)],
                                          start=(k == 0), stop=(k == DC - 1)))
            self.w(nc.scalar, c_pe)
            c_act.inc(nc.scalar.mul(rows[0:1, 0:512], p6[0:1, :], 1.0 / D))
            c_act.inc(nc.scalar.mul(rows[0:1, 512:1024], p7[0:1, :], 1.0 / D))
            self.w(nc.vector, c_act)
            c_dve.inc(nc.vector.tensor_mul(rows[0:1, 1024:1536], rows[0:1, 0:512], rows[0:1, 0:512]))
            self.w(nc.vector, c_dve)
            c_dve.inc(nc.vector.tensor_sub(rows[0:1, 512:1024], rows[0:1, 512:1024], rows[0:1, 1024:1536]))
            self.w(nc.scalar, c_dve)
            c_act.inc(nc.scalar.activation(out=rows[0:1, 1024:1536], in_=rows[0:1, 512:1024],
                                           func=AF.Ln, bias=self.cvals[0:1, 0:1]))
            self.w(nc.scalar, c_act)
            c_act.inc(nc.scalar.activation(out=rows[0:1, 512:1024], in_=rows[0:1, 1024:1536],
                                           func=AF.Exp, scale=-0.5))
            self.w(nc.vector, c_act)
            c_dve.inc(nc.vector.tensor_mul(rows[0:1, 1024:1536], rows[0:1, 0:512], rows[0:1, 512:1024]))
            self.w(nc.tensor, c_dve)
            nc.tensor.matmul(p6[:, :], K["ones128_f32"][0:1, :], rows[0:1, 512:1024], start=True, stop=True)
            c_pe.inc(nc.tensor.matmul(p7[:, :], K["ones128_f32"][0:1, :], rows[0:1, 1024:1536],
                                      start=True, stop=True))
            self.w(nc.scalar, c_pe)
            c_act.inc(nc.scalar.copy(out=sc32[:, 0:512], in_=p6[:, :]))
            c_act.inc(nc.scalar.copy(out=sc32[:, 512:1024], in_=p7[:, :]))
            for k in range(DC):
                self.w(nc.gpsimd, c_act)
                c_pool.inc(nc.gpsimd.tensor_mul(xln[:, 512 * k:512 * (k + 1)],
                                                xr[:, 512 * k:512 * (k + 1)], sc32[:, 0:512]))
                self.w(nc.vector, c_pool)
                c_dve.inc(nc.vector.tensor_sub(xln[:, 512 * k:512 * (k + 1)],
                                               xln[:, 512 * k:512 * (k + 1)], sc32[:, 512:1024]))
            ln_dve_done = c_dve.val()
            if self.dev_stage == "ln":
                return
            nc.tensor.wait_ge(c_dve.h, ln_dve_done)
            for g in range(4):
                for k in range(DC):
                    mm = nc.tensor.matmul(p0[:, 8 * g:8 * (g + 1)],
                                          xln[:, 512 * k + 128 * g:512 * k + 128 * g + 128],
                                          self.gates[:, 8 * k:8 * (k + 1)],
                                          start=(k == 0), stop=(k == DC - 1))
            c_pe.inc(mm)
            self.w(nc.vector, c_pe)
            for g in range(4):
                lg = p0[:, 8 * g:8 * (g + 1)]
                m1 = sc32[:, 1024 + 16 * g:1024 + 16 * g + 1]
                m2 = sc32[:, 1024 + 16 * g + 1:1024 + 16 * g + 2]
                w1c = sc32[:, 1024 + 16 * g + 2:1024 + 16 * g + 3]
                w2c = sc32[:, 1024 + 16 * g + 3:1024 + 16 * g + 4]
                eq1 = sc32[:, 1024 + 16 * g + 4:1024 + 16 * g + 12]
                l2v = wfl[:, 16 * g:16 * g + 8]
                eq2 = wfl[:, 16 * g + 8:16 * g + 16]
                c_dve.inc(nc.vector.reduce_max(m1, lg, axis=AX.X))
                self.w(nc.vector, c_dve)
                c_dve.inc(nc.vector.tensor_scalar(eq1, lg, m1, None, op0=ALU.is_equal))
                self.w(nc.vector, c_dve)
                c_dve.inc(nc.vector.scalar_tensor_tensor(out=l2v, in0=eq1, scalar=-1e30, in1=lg,
                                                         op0=ALU.mult, op1=ALU.add))
                self.w(nc.vector, c_dve)
                c_dve.inc(nc.vector.reduce_max(m2, l2v, axis=AX.X))
                self.w(nc.vector, c_dve)
                c_dve.inc(nc.vector.tensor_sub(w1c, m1, m2))
                self.w(nc.scalar, c_dve)
                c_act.inc(nc.scalar.activation(out=w1c, in_=w1c, func=AF.Sigmoid))
                self.w(nc.vector, c_act)
                c_dve.inc(nc.vector.tensor_scalar(w2c, w1c, -1.0, 1.0, op0=ALU.mult, op1=ALU.add))
                c_dve.inc(nc.vector.tensor_scalar(eq2, l2v, m2, None, op0=ALU.is_equal))
                self.w(nc.vector, c_dve)
                c_dve.inc(nc.vector.tensor_scalar(eq1, eq1, w1c, None, op0=ALU.mult))
                self.w(nc.vector, c_dve)
                c_dve.inc(nc.vector.tensor_scalar(eq2, eq2, w2c, None, op0=ALU.mult))
                self.w(nc.vector, c_dve)
                c_dve.inc(nc.vector.tensor_add(sgb[:, 128 * g:128 * g + 8], eq1, eq2))
            c_pool.inc(nc.gpsimd.memset(wT32[:, :], 0.0))
            self.w(nc.tensor, c_dve)
            for g in range(4):
                mm = nc.tensor.matmul(p1[0:8, 128 * g:128 * (g + 1)],
                                      sgb[:, 128 * g:128 * g + 8], K["id_bf"][:, :],
                                      start=True, stop=True)
            c_pe.inc(mm)
            self.w(nc.scalar, c_pe)
            self.w(nc.scalar, c_pool)
            c_act.inc(nc.scalar.copy(out=wT32[0:8, :], in_=p1[0:8, :]))
            for e in range(8):
                pp = p2 if e % 2 == 0 else p3
                self.w(nc.tensor, c_act)
                c_pe.inc(nc.tensor.matmul(pp[:, :], K["onehot8p"][0:32, 128 * e:128 * (e + 1)],
                                          wT32[:, :], start=True, stop=True))
                self.w(nc.scalar, c_pe)
                c_act.inc(nc.scalar.mul(wb8[:, 512 * e:512 * (e + 1)], pp[:, :], 0.5))
            if self.dev_stage == "gate":
                return
            for k in range(DC):
                nc.tensor.matmul(p4[:, :], self.a1s[:, 128 * k:128 * (k + 1)],
                                 xln[:, 512 * k:512 * (k + 1)], start=(k == 0), stop=(k == DC - 1))
                mm = nc.tensor.matmul(p5[:, :], self.a3s[:, 128 * k:128 * (k + 1)],
                                      xln[:, 512 * k:512 * (k + 1)], start=(k == 0), stop=(k == DC - 1))
            c_pe.inc(mm)
            self.w(nc.scalar, c_pe)
            c_act.inc(nc.scalar.mul(t1b[:, 0:512], p4[:, :], 2.0))
            c_act.inc(nc.scalar.mul(t1b[:, 512:1024], p5[:, :], 2.0))
            self.w(nc.tensor, c_act)
            c_pe.inc(nc.tensor.matmul(p5[:, :], K["zeros_bf"][:, :], t1b[:, 0:512],
                                      start=True, stop=True))
            if self.dev_stage == "t1":
                return
            WS = self.WS
            SLOT = 6656
            ACT_PER_F = 19

            def fpair(j):
                for par in range(2):
                    f = j * 2 + par
                    self.w(nc.sync, c_act, -ACT_PER_F * (1 - par))
                    self.wdma(self.c_ws[par], WS[:, SLOT * par:SLOT * par + 2048], "w1t", f * 128, 2048)
                    self.wdma(self.c_ws[par], WS[:, SLOT * par + 2048:SLOT * par + 4096], "w3t", f * 128, 2048)
                    self.wdma(self.c_ws[par], WS[:, SLOT * par + 4096:SLOT * par + 5120], "b1p", f * 64, 1024)
                    self.wdma(self.c_ws[par], WS[:, SLOT * par + 5120:SLOT * par + 6144], "b3p", f * 64, 1024)
                    self.wdma(self.c_ws[par], WS[:, SLOT * par + 6144:SLOT * par + 6656], "a2p", f * 32, 512)
                for par in range(2):
                    f = j * 2 + par
                    so = SLOT * par
                    bo = 512 * par
                    self.w(nc.tensor, self.c_ws[par])
                    for k in range(DC):
                        mm = nc.tensor.matmul(p0[:, :], WS[:, so + 128 * k:so + 128 * (k + 1)],
                                              xln[:, 512 * k:512 * (k + 1)],
                                              start=(k == 0), stop=(k == DC - 1))
                    c_pe.inc(mm)
                    for k in range(DC):
                        mm = nc.tensor.matmul(p1[:, :], WS[:, so + 2048 + 128 * k:so + 2048 + 128 * (k + 1)],
                                              xln[:, 512 * k:512 * (k + 1)],
                                              start=(k == 0), stop=(k == DC - 1))
                    c_pe.inc(mm)
                    self.w(nc.scalar, c_pe, -1)
                    c_act.inc(nc.scalar.copy(out=baseb[:, bo:bo + 512], in_=p0[:, :]))
                    self.w(nc.scalar, c_pe)
                    c_act.inc(nc.scalar.copy(out=sgb[:, 2048 + bo:2048 + bo + 512], in_=p1[:, :]))
                    def sact_t2(ep_, e_):
                        self.w(nc.tensor, c_dve)
                        nc.tensor.matmul(p4[:, :], K["id_bf"][:, :], sgb[:, 512 * ep_:512 * (ep_ + 1)],
                                         start=(e_ == 0), stop=(e_ == 7))
                        c_pe.inc(nc.tensor.matmul(p5[64 * (e_ // 4):64 * (e_ // 4) + 64, :],
                                                  WS[:, so + 6144 + 64 * e_:so + 6144 + 64 * (e_ + 1)],
                                                  sgb[:, 512 * ep_:512 * (ep_ + 1)], start=False,
                                                  stop=False, skip_group_check=True))
                    for e in range(8):
                        ep = e % 2
                        hp1 = p2 if ep == 0 else p3
                        hp3 = p6 if ep == 0 else p7
                        if e == 0:
                            self.w(nc.tensor, c_act)
                        elif e >= 2:
                            self.w(nc.tensor, c_act, -3)
                        nc.tensor.matmul(hp1[:, :],
                                         WS[:, so + 4096 + 128 * e:so + 4096 + 128 * (e + 1)],
                                         t1b[:, 0:512], start=True, stop=False)
                        c_pe.inc(nc.tensor.matmul(hp1[:, :], K["id_bf"][:, :],
                                                  baseb[:, bo:bo + 512], start=False, stop=True))
                        nc.tensor.matmul(hp3[:, :],
                                         WS[:, so + 5120 + 128 * e:so + 5120 + 128 * (e + 1)],
                                         t1b[:, 512:1024], start=True, stop=False)
                        c_pe.inc(nc.tensor.matmul(hp3[:, :], K["id_bf"][:, :],
                                                  sgb[:, 2048 + bo:2048 + bo + 512], start=False, stop=True))
                        if e >= 1:
                            sact_t2(1 - ep, e - 1)
                        nc.scalar.wait_ge(c_pe.h, c_pe.val(-2 if e >= 1 else -1))
                        c_act.inc(nc.scalar.activation(out=sgb[:, 1024 + 512 * ep:1024 + 512 * (ep + 1)],
                                                       in_=hp1[:, :], func=AF.Silu))
                        nc.scalar.wait_ge(c_pe.h, c_pe.val(-1 if e >= 1 else 0))
                        c_act.inc(nc.scalar.copy(out=sgb[:, 512 * ep:512 * (ep + 1)], in_=hp3[:, :]))
                        self.w(nc.vector, c_act)
                        c_dve.inc(nc.vector.tensor_mul(sgb[:, 512 * ep:512 * (ep + 1)],
                                                       sgb[:, 512 * ep:512 * (ep + 1)],
                                                       wb8[:, 512 * e:512 * (e + 1)]))
                        self.w(nc.vector, c_dve)
                        c_dve.inc(nc.vector.tensor_mul(sgb[:, 512 * ep:512 * (ep + 1)],
                                                       sgb[:, 512 * ep:512 * (ep + 1)],
                                                       sgb[:, 1024 + 512 * ep:1024 + 512 * (ep + 1)]))
                    sact_t2(1, 7)
                    self.w(nc.scalar, c_pe)
                    c_act.inc(nc.scalar.copy(out=sact[:, bass.ds(512 * f, 512)], in_=p4[:, :]))

            self.sync_fence()
            InnerLoop(nc, self.tile_ctrs, FC // 2, fpair)
            if self.dev_stage == "floop":
                return
            self.w(nc.scalar, c_pe)
            c_act.inc(nc.scalar.mul(t2ev[:, :], p5[:, :], 2.0))
            if self.dev_stage == "t2ev":
                return
            DSLOT = 6656
            self.sync_fence()
            nc.tensor.wait_ge(c_act.h, c_act.val())
            nc.sync.wait_ge(c_pe.h, c_pe.val())

            def dpair(j):
                for par in range(2):
                    d = j * 2 + par
                    self.w(nc.sync, c_dve, par - 1)
                    self.wdma(self.c_ws[par], WS[:, DSLOT * par:DSLOT * par + 5632], "w2t", d * 352, 5632)
                    self.wdma(self.c_ws[par], WS[:, DSLOT * par + 5632:DSLOT * par + 6656], "b2p", d * 64, 1024)
                for par in range(2):
                    d = j * 2 + par
                    so = DSLOT * par
                    pp = p0 if par == 0 else p1
                    self.w(nc.tensor, self.c_ws[par])
                    for f in range(FC):
                        nc.tensor.matmul(pp[:, :], WS[:, so + 128 * f:so + 128 * (f + 1)],
                                         sact[:, 512 * f:512 * (f + 1)],
                                         start=(f == 0), stop=False)
                    for e in range(8):
                        mm = nc.tensor.matmul(pp[:, :],
                                              WS[:, so + 5632 + 128 * e:so + 5632 + 128 * (e + 1)],
                                              t2ev[:, :], start=False, stop=(e == 7))
                    c_pe.inc(mm)
                    self.w(nc.vector, c_pe)
                    c_dve.inc(nc.vector.tensor_add(self.x_res[:, bass.ds(512 * d, 512)],
                                                   self.x_res[:, bass.ds(512 * d, 512)], pp[:, :]))
            InnerLoop(nc, self.tile_ctrs, DC // 2, dpair)
            nc.vector.wait_ge(c_dve.h, c_dve.val())

    def emit_attn(self):
        nc = self.nc
        K = self.K
        c_pe, c_act, c_dve, c_pool = self.c_pe, self.c_act, self.c_dve, self.c_pool
        self._attn_n = getattr(self, "_attn_n", 0) + 1
        sx = f"_a{self._attn_n}"
        WS = self.WS
        _es = contextlib.ExitStack()
        q_sb = _es.enter_context(nc.sbuf_tensor("q_sb" + sx, [128, KH * HALF], BF16))
        k_sb = _es.enter_context(nc.sbuf_tensor("k_sb" + sx, [128, KH * HALF], BF16))
        v_sb = _es.enter_context(nc.sbuf_tensor("v_sb" + sx, [128, VH * HALF], BF16))
        mp = _es.enter_context(nc.sbuf_tensor("mp" + sx, [128, 2 * 262], BF16))
        cvt = _es.enter_context(nc.sbuf_tensor("cvt" + sx, [128, 2 * 256], BF16))
        qsr = _es.enter_context(nc.sbuf_tensor("qsr" + sx, [128, 2 * 256], BF16))
        rrow = _es.enter_context(nc.sbuf_tensor("rrow" + sx, [1, 2 * 256], BF16))
        rrf = _es.enter_context(nc.sbuf_tensor("rrf" + sx, [1, 2 * 256], F32))
        siluz = _es.enter_context(nc.sbuf_tensor("siluz" + sx, [128, ZD], BF16))
        xg = _es.enter_context(nc.sbuf_tensor("xg" + sx, [128, ZD], BF16))
        nrm = _es.enter_context(nc.sbuf_tensor("nrm" + sx, [128, 32 * HALF], BF16))
        tmp8 = _es.enter_context(nc.sbuf_tensor("tmp8" + sx, [128, 10 * 32], F32))
        gjfm = _es.enter_context(nc.sbuf_tensor("gjfm" + sx, [32, 128], F32))
        colb = _es.enter_context(nc.sbuf_tensor("colb" + sx, [128, 128], F32))
        colbb = _es.enter_context(nc.sbuf_tensor("colbb" + sx, [128, 128], BF16))
        tsc = _es.enter_context(nc.sbuf_tensor("tsc" + sx, [128, 128], F32))
        eA = _es.enter_context(nc.sbuf_tensor("eA" + sx, [128, 128], F32))
        eTb = _es.enter_context(nc.sbuf_tensor("eT" + sx, [128, 2 * 128], F32))
        gmlb = _es.enter_context(nc.sbuf_tensor("gml" + sx, [128, 2 * 128], F32))
        gmub = _es.enter_context(nc.sbuf_tensor("gmu" + sx, [128, 2 * 128], F32))
        t2f = _es.enter_context(nc.sbuf_tensor("t2f" + sx, [128, 128], F32))
        xcur = _es.enter_context(nc.sbuf_tensor("xcur" + sx, [128, 2 * 128], BF16))
        ycur = _es.enter_context(nc.sbuf_tensor("ycur" + sx, [128, 2 * 128], BF16))
        xrh = _es.enter_context(nc.sbuf_tensor("xrh" + sx, [128, 2 * 256], BF16))
        kcd = _es.enter_context(nc.sbuf_tensor("kcd" + sx, [128, 2 * 128], BF16))
        nkcdt = _es.enter_context(nc.sbuf_tensor("nkcdt" + sx, [128, 2 * 128], BF16))
        vnew = _es.enter_context(nc.sbuf_tensor("vnew" + sx, [128, 2 * 128], BF16))
        intrat = _es.enter_context(nc.sbuf_tensor("intrat" + sx, [128, 2 * 128], BF16))
        qeb = _es.enter_context(nc.sbuf_tensor("qe" + sx, [128, 2 * 128], BF16))
        kdecb = _es.enter_context(nc.sbuf_tensor("kdec" + sx, [128, 2 * 128], BF16))
        pa0 = _es.enter_context(nc.psum_tensor("pa0" + sx, [128, 512], F32))
        pa1 = _es.enter_context(nc.psum_tensor("pa1" + sx, [128, 512], F32))
        pa2 = _es.enter_context(nc.psum_tensor("pa2" + sx, [128, 512], F32))
        pa3 = _es.enter_context(nc.psum_tensor("pa3" + sx, [128, 512], F32))
        pa4 = _es.enter_context(nc.psum_tensor("pa4" + sx, [128, 1024], BF16))
        pa5 = _es.enter_context(nc.psum_tensor("pa5" + sx, [128, 1024], BF16))
        pa6 = _es.enter_context(nc.psum_tensor("pa6" + sx, [128, 512], F32))
        pa7 = _es.enter_context(nc.psum_tensor("pa7" + sx, [128, 512], F32))
        if True:
            t_beta = tmp8[:, 0:32]
            t_gj = tmp8[:, 32:64]
            t_egj = tmp8[:, 64:96]
            t_edec = tmp8[:, 96:128]
            t_begj = tmp8[:, 128:160]
            t_cb = tmp8[:, 160:192]
            t_gs = tmp8[:, 192:224]
            env = dict(locals())
            ds_ = self.dev_stage or ""
            for half in range(2):
                self.attn_frontend(half, env)
                if ds_ == "attn_fe" and half == 0:
                    self._dump([(q_sb, 0, KH * HALF), (k_sb, KH * HALF, KH * HALF)])
                    break
                if ds_ in ("attn_v", "attn_v_noconv") and half == 0:
                    self._dump([(v_sb, 0, VH * HALF)])
                    break
                stop = False
                for cpar in range(2):
                    self.attn_chunk(half, cpar, env)
                    if ds_ in ("attn_z", "attn_core") and half == 0 and cpar == 0:
                        if ds_ == "attn_z":
                            self._dump([(siluz, 0, ZD), (tmp8, ZD, 320)])
                        else:
                            self._dump([(xg, 0, ZD), (nrm, ZD, 4096)])
                        stop = True
                        break
                if stop:
                    break
                self.attn_outproj(half, env)
        _es.close()

    def _dump(self, items):
        nc = self.nc
        nc.all_engine_barrier()
        for src, off, width in items:
            nc.vector.tensor_copy(out=self.x_res[:, off:off + width], in_=src[:, 0:width])
        nc.all_engine_barrier()

    def attn_frontend(self, half, L):
        nc = self.nc
        K = self.K
        c_pe, c_act, c_dve, c_pool = self.c_pe, self.c_act, self.c_dve, self.c_pool
        WS = self.WS
        q_sb, k_sb, v_sb = L["q_sb"], L["k_sb"], L["v_sb"]
        mp, cvt, qsr, rrow, rrf = L["mp"], L["cvt"], L["qsr"], L["rrow"], L["rrf"]
        pa0, pa1, pa2 = L["pa0"], L["pa1"], L["pa2"]
        xln = self.xln
        tok0 = half * HALF

        self.sync_fence()
        nc.tensor.wait_ge(c_dve.h, c_dve.val())

        def cpair(kind, cbase, dst):
            def body(j):
                for par in range(2):
                    c = cbase + j * 2 + par
                    self.w(nc.sync, c_pe, -(1 - par) * self._fe_pe_per_c[kind])
                    self.wdma(self.c_ws[par], WS[:, 2048 * par:2048 * (par + 1)],
                              "qkvt", c * 128, 2048)
                for par in range(2):
                    c = cbase + j * 2 + par
                    pp = pa0 if par == 0 else pa1
                    self.w(nc.tensor, self.c_ws[par])
                    apc = 5 if kind != "v" else 2
                    self.w(nc.tensor, c_act, 1 - 2 * apc)
                    for k in range(DC):
                        mm = nc.tensor.matmul(
                            pp[:, 0:HALF], WS[:, 2048 * par + 128 * k:2048 * par + 128 * (k + 1)],
                            xln[:, bass.ds(512 * k + tok0, HALF)],
                            start=(k == 0), stop=(k == DC - 1))
                    c_pe.inc(mm)
                    self.w(nc.scalar, c_pe)
                    c_act.inc(nc.scalar.copy(out=mp[:, 262 * par + 3:262 * par + 259], in_=pp[:, 0:HALF]))
                    self.w(nc.gpsimd, c_act)
                    c_pool.inc(nc.gpsimd.tensor_copy(out=mp[:, 262 * par:262 * par + 3],
                                                     in_=self.ctail[:, bass.ds(4 * c, 3)]))
                    self.w(nc.vector, c_pool)
                    if (self.dev_stage or "").endswith("noconv"):
                        c_dve.inc(nc.vector.tensor_copy(out=cvt[:, 256 * par:256 * (par + 1)],
                                                        in_=mp[:, 262 * par + 3:262 * par + 259]))
                    else:
                        c_dve.inc(nc.vector.tensor_scalar(cvt[:, 256 * par:256 * (par + 1)],
                                                          mp[:, 262 * par:262 * par + 256],
                                                          self.convw[:, bass.ds(4 * c, 1)], None, op0=ALU.mult))
                    for kk in range(1, KCONV):
                        if (self.dev_stage or "").endswith("noconv"):
                            self.w(nc.vector, c_dve)
                            c_dve.inc(nc.vector.tensor_copy(out=cvt[:, 256 * par:256 * (par + 1)],
                                                            in_=mp[:, 262 * par + 3:262 * par + 259]))
                            continue
                        self.w(nc.vector, c_dve)
                        c_dve.inc(nc.vector.scalar_tensor_tensor(
                            out=cvt[:, 256 * par:256 * (par + 1)],
                            in0=mp[:, 262 * par + kk:262 * par + kk + 256],
                            scalar=self.convw[:, bass.ds(4 * c + kk, 1)],
                            in1=cvt[:, 256 * par:256 * (par + 1)], op0=ALU.mult, op1=ALU.add))
                    self.w(nc.gpsimd, c_dve)
                    c_pool.inc(nc.gpsimd.tensor_copy(out=self.ctail[:, bass.ds(4 * c, 3)],
                                                     in_=mp[:, 262 * par + 256:262 * par + 259]))
                    if kind == "v":
                        self.w(nc.scalar, c_dve)
                        c_act.inc(nc.scalar.activation(
                            out=dst[:, bass.ds((c - cbase) * HALF, HALF)],
                            in_=cvt[:, 256 * par:256 * (par + 1)], func=AF.Silu))
                    else:
                        self.w(nc.scalar, c_dve)
                        c_act.inc(nc.scalar.activation(out=qsr[:, 256 * par:256 * (par + 1)],
                                                       in_=cvt[:, 256 * par:256 * (par + 1)], func=AF.Silu))
                        self.w(nc.scalar, c_act)
                        c_act.inc(nc.scalar.activation(out=cvt[:, 256 * par:256 * (par + 1)],
                                                       in_=qsr[:, 256 * par:256 * (par + 1)], func=AF.Square))
                        self.w(nc.tensor, c_act)
                        self.w(nc.tensor, c_dve)
                        c_pe.inc(nc.tensor.matmul(pa2[0:1, bass.ds(256 * par, 256)],
                                                  K["ones128_bf"][:, 0:1],
                                                  cvt[:, 256 * par:256 * (par + 1)], start=True, stop=True))
                        self.w(nc.scalar, c_pe)
                        c_act.inc(nc.scalar.activation(out=rrf[0:1, 256 * par:256 * (par + 1)],
                                                       in_=pa2[0:1, bass.ds(256 * par, 256)],
                                                       func=AF.Ln, bias=self.cvals[0:1, 1:2]))
                        self.w(nc.scalar, c_act)
                        if kind == "q":
                            c_act.inc(nc.scalar.activation(out=rrow[0:1, 256 * par:256 * (par + 1)],
                                                           in_=rrf[0:1, 256 * par:256 * (par + 1)],
                                                           func=AF.Exp, scale=-0.5,
                                                           bias=self.cvals[0:1, 3:4]))
                        else:
                            c_act.inc(nc.scalar.activation(out=rrow[0:1, 256 * par:256 * (par + 1)],
                                                           in_=rrf[0:1, 256 * par:256 * (par + 1)],
                                                           func=AF.Exp, scale=-0.5))
                        self.w(nc.tensor, c_act)
                        c_pe.inc(nc.tensor.matmul(pa2[:, bass.ds(256 * par, 256)],
                                                  K["ones128_bf"][0:1, :],
                                                  rrow[0:1, 256 * par:256 * (par + 1)],
                                                  start=True, stop=True))
                        self.w(nc.vector, c_pe)
                        c_dve.inc(nc.vector.tensor_mul(
                            dst[:, bass.ds((c - cbase) * HALF, HALF)],
                            qsr[:, 256 * par:256 * (par + 1)],
                            pa2[:, bass.ds(256 * par, 256)]))
            return body
        self._fe_pe_per_c = {"q": 3, "k": 3, "v": 1}
        InnerLoop(nc, self.tile_ctrs, 8, cpair("q", 0, q_sb))
        InnerLoop(nc, self.tile_ctrs, 8, cpair("k", 16, k_sb))
        InnerLoop(nc, self.tile_ctrs, 16, cpair("v", 32, v_sb))

    def attn_chunk(self, half, cpar, L):
        nc = self.nc
        K = self.K
        c_pe, c_act, c_dve, c_pool = self.c_pe, self.c_act, self.c_dve, self.c_pool
        WS = self.WS
        xln = self.xln
        q_sb, k_sb, v_sb, siluz, xg, nrm = L["q_sb"], L["k_sb"], L["v_sb"], L["siluz"], L["xg"], L["nrm"]
        gjfm, colb, colbb, tsc, eA, eTb = L["gjfm"], L["colb"], L["colbb"], L["tsc"], L["eA"], L["eTb"]
        gmlb, gmub, t2f = L["gmlb"], L["gmub"], L["t2f"]
        xcur, ycur, xrh, kcd, nkcdt, vnew, intrat, qeb, kdecb = \
            L["xcur"], L["ycur"], L["xrh"], L["kcd"], L["nkcdt"], L["vnew"], L["intrat"], L["qeb"], L["kdecb"]
        pa0, pa1, pa2, pa3, pa4, pa5, pa6, pa7 = \
            L["pa0"], L["pa1"], L["pa2"], L["pa3"], L["pa4"], L["pa5"], L["pa6"], L["pa7"]
        t_beta, t_gj, t_egj, t_edec, t_begj, t_cb, t_gs = \
            L["t_beta"], L["t_gj"], L["t_egj"], L["t_edec"], L["t_begj"], L["t_cb"], L["t_gs"]
        tmp8 = L["tmp8"]
        t_ssq = tmp8[:, 224:256]
        t_rstd = tmp8[:, 256:288]
        tok0 = half * HALF + cpar * CH
        csl = cpar * CH

        self.sync_fence()
        nc.tensor.wait_ge(c_dve.h, c_dve.val())
        nc.tensor.wait_ge(c_act.h, c_act.val())
        for g in range(4):
            for k in range(DC):
                par = k % 2
                self.w(nc.sync, c_pe, -1)
                base = (BLOB_OFF["wz"]) * 2048 + k * 128 * ZD + 1024 * g
                ap = bass.AP(self.wbf.tensor, base, [[ZD, 128], [1, 1024]])
                self.c_ws[par].inc(nc.sync.dma_start(
                    out=WS[:, 4096 + 1024 * par:4096 + 1024 * (par + 1)], in_=ap))
                self.w(nc.tensor, self.c_ws[par])
                if k == 0 and g > 0:
                    nc.tensor.wait_ge(c_act.h, c_act.val())
                nc.tensor.matmul(pa0[:, :], xln[:, bass.ds(512 * k + tok0, 128)],
                                 WS[:, 4096 + 1024 * par:4096 + 1024 * par + 512],
                                 start=(k == 0), stop=(k == DC - 1))
                mm = nc.tensor.matmul(pa1[:, :], xln[:, bass.ds(512 * k + tok0, 128)],
                                      WS[:, 4096 + 1024 * par + 512:4096 + 1024 * (par + 1)],
                                      start=(k == 0), stop=(k == DC - 1))
                c_pe.inc(mm)
            self.w(nc.scalar, c_pe)
            c_act.inc(nc.scalar.activation(out=siluz[:, 1024 * g:1024 * g + 512],
                                           in_=pa0[:, :], func=AF.Silu))
            c_act.inc(nc.scalar.activation(out=siluz[:, 1024 * g + 512:1024 * (g + 1)],
                                           in_=pa1[:, :], func=AF.Silu))
        for k in range(DC):
            mm = nc.tensor.matmul(pa2[:, 256:320], xln[:, bass.ds(512 * k + tok0, 128)],
                                  self.wba[:, 64 * k:64 * (k + 1)],
                                  start=(k == 0), stop=(k == DC - 1))
        c_pe.inc(mm)
        self.w(nc.scalar, c_pe)
        c_act.inc(nc.scalar.activation(out=t_beta, in_=pa2[:, 256:288], func=AF.Sigmoid))
        self.w(nc.vector, c_pe)
        c_dve.inc(nc.vector.tensor_add(tmp8[:, 288:320], pa2[:, 288:320], self.dtb_b[:, :]))
        self.w(nc.scalar, c_dve)
        c_act.inc(nc.scalar.activation(out=t_gj, in_=tmp8[:, 288:320], func=AF.Abs))
        self.w(nc.scalar, c_act)
        c_act.inc(nc.scalar.activation(out=t_egj, in_=t_gj, func=AF.Exp, scale=-1.0))
        self.w(nc.scalar, c_act)
        c_act.inc(nc.scalar.activation(out=t_egj, in_=t_egj, func=AF.Ln, bias=self.cvals[:, 2:3]))
        c_act.inc(nc.scalar.activation(out=t_edec, in_=tmp8[:, 288:320], func=AF.Relu))
        self.w(nc.vector, c_act)
        c_dve.inc(nc.vector.tensor_add(t_edec, t_edec, t_egj))
        self.w(nc.vector, c_dve)
        c_dve.inc(nc.vector.tensor_mul(tmp8[:, 288:320], t_edec, self.negeA_b[:, :]))
        g_tm = tmp8[:, 288:320]
        self.w(nc.tensor, c_dve)
        c_pe.inc(nc.tensor.matmul(pa3[0:32, 0:128], g_tm, K["mask_up01"][:, :],
                                  start=True, stop=True))
        self.w(nc.scalar, c_pe)
        c_act.inc(nc.scalar.copy(out=gjfm[:, :], in_=pa3[0:32, 0:128]))
        self.w(nc.tensor, c_act)
        c_pe.inc(nc.tensor.matmul(pa3[:, 128:160], K["ones128_f32"][:, :], g_tm,
                                  start=True, stop=True))
        c_pe.inc(nc.tensor.transpose(pa3[:, 160:192], gjfm[:, :], K["id_f32"][0:32, 0:32]))
        self.w(nc.scalar, c_pe)
        c_act.inc(nc.scalar.copy(out=t_gj, in_=pa3[:, 160:192]))
        self.w(nc.scalar, c_act)
        c_act.inc(nc.scalar.activation(out=t_egj, in_=t_gj, func=AF.Exp))
        self.w(nc.vector, c_act)
        c_dve.inc(nc.vector.tensor_sub(t_edec, pa3[:, 128:160], t_gj))
        self.w(nc.scalar, c_dve)
        c_act.inc(nc.scalar.activation(out=t_edec, in_=t_edec, func=AF.Exp))
        c_act.inc(nc.scalar.activation(out=t_cb, in_=pa3[:, 128:160], func=AF.Exp))
        self.w(nc.vector, c_act)
        c_dve.inc(nc.vector.tensor_mul(t_begj, t_beta, t_egj))
        for h in range(VH):
            j = h // 2
            hp = h % 2
            p6 = pa6 if hp == 0 else pa7
            pb = pa4 if hp == 0 else pa5
            prb = pa3[:, 0:128]
            qsl = q_sb[:, j * HALF + csl:j * HALF + csl + CH]
            ksl = k_sb[:, j * HALF + csl:j * HALF + csl + CH]
            vsl = v_sb[:, h * HALF + csl:h * HALF + csl + CH]
            stsl = self.st[:, 128 * h:128 * (h + 1)]
            stbsl = self.stbf[:, 128 * h:128 * (h + 1)]
            eT = eTb[:, 128 * hp:128 * (hp + 1)]
            gml = gmlb[:, 128 * (j % 2):128 * (j % 2 + 1)]
            gmu = gmub[:, 128 * (j % 2):128 * (j % 2 + 1)]
            xc = xcur[:, 128 * hp:128 * (hp + 1)]
            yc = ycur[:, 128 * hp:128 * (hp + 1)]
            xr = xrh[:, 256 * hp:256 * (hp + 1)]
            if hp == 0:
                nc.tensor.wait_ge(c_dve.h, c_dve.val())
                nc.tensor.matmul(pa2[:, 0:128], ksl, ksl, start=True, stop=True)
                c_pe.inc(nc.tensor.matmul(pa2[:, 128:256], ksl, qsl, start=True, stop=True))
                nc.tensor.wait_ge(c_dve.h, c_dve.val())
                c_pe.inc(nc.tensor.transpose(pa4[:, 512:640], ksl, K["id_bf"][:, :]))
                self.w(nc.vector, c_pe)
                c_dve.inc(nc.vector.tensor_mul(gml, pa2[:, 0:128], K["mask_lowneg"][:, :]))
                c_dve.inc(nc.vector.tensor_mul(gmu, pa2[:, 0:128], K["mask_upneg"][:, :]))
            c_dve.inc(nc.vector.tensor_scalar(colb[:, :], K["ones128_f32"][:, :],
                                              t_gj[:, h:h + 1], None, op0=ALU.mult))
            self.w(nc.tensor, c_dve)
            c_pe.inc(nc.tensor.transpose(prb, colb[:, :], K["id_f32"][:, :]))
            self.w(nc.vector, c_pe)
            c_dve.inc(nc.vector.tensor_scalar(tsc[:, :], prb, t_gj[:, h:h + 1], None,
                                              op0=ALU.subtract))
            self.w(nc.vector, c_dve)
            c_dve.inc(nc.vector.tensor_mul(eA[:, :], tsc[:, :], K["mask_lowneg"][:, :]))
            self.w(nc.scalar, c_dve)
            c_act.inc(nc.scalar.activation(out=eA[:, :], in_=eA[:, :], func=AF.Exp))
            c_dve.inc(nc.vector.tensor_mul(eT, tsc[:, :], K["mask_up01"][:, :]))
            self.w(nc.scalar, c_dve)
            c_act.inc(nc.scalar.activation(out=eT, in_=eT, func=AF.Exp))
            self.w(nc.vector, c_act)
            c_dve.inc(nc.vector.scalar_tensor_tensor(out=xc, in0=eA[:, :],
                                                     scalar=t_beta[:, h:h + 1], in1=gml,
                                                     op0=ALU.mult, op1=ALU.mult))
            c_dve.inc(nc.vector.tensor_scalar(colbb[:, :], K["ones128_bf"][:, :],
                                              t_beta[:, h:h + 1], None, op0=ALU.mult))
            self.w(nc.tensor, c_dve)
            c_pe.inc(nc.tensor.transpose(pb[:, 0:128], colbb[:, :], K["id_bf"][:, :]))
            self.w(nc.vector, c_pe)
            c_dve.inc(nc.vector.tensor_mul(t2f[:, :], eT, gmu))
            self.w(nc.vector, c_dve)
            c_dve.inc(nc.vector.tensor_mul(yc, t2f[:, :], pb[:, 0:128]))
            self.w(nc.tensor, c_dve)
            c_pe.inc(nc.tensor.transpose(pb[:, 256:384], vsl, K["id_bf"][:, :]))
            self.w(nc.vector, c_pe)
            c_dve.inc(nc.vector.tensor_scalar(xr[:, 0:128], pb[:, 256:384],
                                              t_beta[:, h:h + 1], None, op0=ALU.mult))
            c_dve.inc(nc.vector.tensor_scalar(xr[:, 128:256], pa4[:, 512:640],
                                              t_begj[:, h:h + 1], None, op0=ALU.mult))
            c_dve.inc(nc.vector.tensor_scalar(kdecb[:, 128 * hp:128 * (hp + 1)],
                                              pa4[:, 512:640],
                                              t_edec[:, h:h + 1], None, op0=ALU.mult))
            self.w(nc.tensor, c_dve)
            self.w(nc.tensor, c_act)
            nc.tensor.matmul(p6[:, 0:256], K["id_bf"][:, :], xr, start=True, stop=True)
            c_pe.inc(nc.tensor.matmul(p6[:, 0:256], yc, xr, start=False, stop=True,
                                      skip_group_check=True))
            for lev in range(1, 7):
                nc.tensor.wait_ge(c_act.h, c_act.val())
                nc.tensor.matmul(pa2[:, 256:384], xc, yc, start=True, stop=True)
                if lev < 6:
                    c_pe.inc(nc.tensor.matmul(pa2[:, 384:512], yc, xc, start=True, stop=True,
                                              skip_group_check=True))
                else:
                    c_pe.inc(nc.tensor.matmul(pa2[:, 384:512], yc, xc, start=True, stop=True,
                                              skip_group_check=True))
                self.w(nc.scalar, c_pe, 0)
                c_act.inc(nc.scalar.copy(out=xr, in_=p6[:, 0:256]))
                c_act.inc(nc.scalar.copy(out=yc, in_=pa2[:, 256:384]))
                c_act.inc(nc.scalar.copy(out=xc, in_=pa2[:, 384:512]))
                self.w(nc.tensor, c_act)
                c_pe.inc(nc.tensor.matmul(p6[:, 0:256], yc, xr, start=False, stop=True,
                                          skip_group_check=True))
            self.w(nc.scalar, c_pe)
            c_act.inc(nc.scalar.copy(out=kcd[:, 128 * hp:128 * (hp + 1)], in_=p6[:, 128:256]))
            self.w(nc.tensor, c_act)
            c_pe.inc(nc.tensor.transpose(pb[:, 384:512], kcd[:, 128 * hp:128 * (hp + 1)],
                                         K["id_bf"][:, :]))
            self.w(nc.scalar, c_pe)
            c_act.inc(nc.scalar.mul(nkcdt[:, 128 * hp:128 * (hp + 1)], pb[:, 384:512], -1.0))
            self.w(nc.tensor, c_act)
            self.w(nc.tensor, c_pool)
            c_pe.inc(nc.tensor.matmul(p6[:, 0:128], nkcdt[:, 128 * hp:128 * (hp + 1)],
                                      stbsl, start=False, stop=True, skip_group_check=True))
            self.w(nc.scalar, c_pe)
            c_act.inc(nc.scalar.copy(out=vnew[:, 128 * hp:128 * (hp + 1)], in_=p6[:, 0:128]))
            self.w(nc.vector, c_act)
            c_dve.inc(nc.vector.tensor_mul(t2f[:, :], pa2[:, 128:256], eT))
            self.w(nc.vector, c_dve)
            c_dve.inc(nc.vector.tensor_mul(intrat[:, 128 * hp:128 * (hp + 1)], t2f[:, :],
                                           K["mask_up01"][:, :]))
            c_dve.inc(nc.vector.tensor_scalar(colbb[:, :], K["ones128_bf"][:, :],
                                              t_egj[:, h:h + 1], None, op0=ALU.mult))
            self.w(nc.tensor, c_dve)
            c_pe.inc(nc.tensor.transpose(pb[:, 128:256], colbb[:, :], K["id_bf"][:, :]))
            self.w(nc.vector, c_pe)
            c_dve.inc(nc.vector.tensor_mul(qeb[:, 128 * hp:128 * (hp + 1)], qsl,
                                           pb[:, 128:256]))
            self.w(nc.tensor, c_dve)
            nc.tensor.wait_ge(c_dve.h, c_dve.val())
            self.w(nc.tensor, c_pool)
            nc.tensor.matmul(p6[:, 384:512], intrat[:, 128 * hp:128 * (hp + 1)],
                             vnew[:, 128 * hp:128 * (hp + 1)], start=True, stop=True)
            c_pe.inc(nc.tensor.matmul(p6[:, 384:512], qeb[:, 128 * hp:128 * (hp + 1)],
                                      stbsl, start=False, stop=True, skip_group_check=True))
            self.w(nc.vector, c_pe)
            c_dve.inc(nc.vector.tensor_mul(xg[:, 128 * h:128 * (h + 1)], p6[:, 384:512],
                                           siluz[:, 128 * h:128 * (h + 1)]))
            pss = pa3[:, 288:416]
            c_pe.inc(nc.tensor.matmul(pss, kdecb[:, 128 * hp:128 * (hp + 1)],
                                      vnew[:, 128 * hp:128 * (hp + 1)], start=True, stop=True))
            self.w(nc.vector, c_pe)
            c_dve.inc(nc.vector.tensor_scalar(stsl, stsl, t_cb[:, h:h + 1], None, op0=ALU.mult))
            self.w(nc.vector, c_dve)
            c_dve.inc(nc.vector.tensor_add(stsl, stsl, pss))
            self.w(nc.gpsimd, c_dve)
            c_pool.inc(nc.gpsimd.tensor_copy(out=stbsl, in_=stsl))
        self.w(nc.scalar, c_dve)
        c_act.inc(nc.scalar.activation(out=siluz[:, :], in_=xg[:, :], func=AF.Square))
        self.w(nc.vector, c_act)
        c_dve.inc(nc.vector.tensor_reduce(
            t_ssq, siluz[:, :].rearrange("p (h u) -> p h u", u=128), axis=AX.X,
            op=ALU.add))
        self.w(nc.vector, c_dve)
        c_dve.inc(nc.vector.tensor_scalar(t_ssq, t_ssq, 1.0 / 128.0, None, op0=ALU.mult))
        self.w(nc.scalar, c_dve)
        c_act.inc(nc.scalar.activation(out=t_rstd, in_=t_ssq, func=AF.Ln,
                                       bias=self.cvals[:, 4:5]))
        self.w(nc.scalar, c_act)
        c_act.inc(nc.scalar.activation(out=t_rstd, in_=t_rstd, func=AF.Exp, scale=-0.5))
        self.w(nc.vector, c_act)
        for h in range(VH):
            c_dve.inc(nc.vector.tensor_scalar(xg[:, 128 * h:128 * (h + 1)],
                                              xg[:, 128 * h:128 * (h + 1)],
                                              t_rstd[:, h:h + 1], None, op0=ALU.mult))
        self.w(nc.vector, c_dve)
        c_dve.inc(nc.vector.tensor_mul(xg[:, :], xg[:, :], self.normw_rep[:, :]))
        for fc in range(32):
            pslot = pa4 if fc % 2 == 0 else pa5
            self.w(nc.tensor, c_dve)
            if fc >= 2:
                self.w(nc.tensor, c_act)
            c_pe.inc(nc.tensor.transpose(pslot[:, 640:768], xg[:, 128 * fc:128 * (fc + 1)],
                                         K["id_bf"][:, :]))
            self.w(nc.scalar, c_pe)
            c_act.inc(nc.scalar.copy(out=nrm[:, fc * HALF + 128 * cpar:fc * HALF + 128 * cpar + 128],
                                     in_=pslot[:, 640:768]))

    def attn_outproj(self, half, L):
        nc = self.nc
        c_pe, c_act, c_dve = self.c_pe, self.c_act, self.c_dve
        WS = self.WS
        nrm = L["nrm"]
        pa0, pa1 = L["pa0"], L["pa1"]
        self.sync_fence()
        nc.sync.wait_ge(c_pe.h, c_pe.val())
        nc.tensor.wait_ge(c_act.h, c_act.val())

        def dpair(jj):
            for par in range(2):
                d = jj * 2 + par
                self.w(nc.sync, c_dve, par - 1)
                self.wdma(self.c_ws[par], WS[:, 4096 + 4096 * par:4096 + 4096 * (par + 1)],
                          "woutt", d * 256, 4096)
            for par in range(2):
                d = jj * 2 + par
                pp = pa0 if par == 0 else pa1
                self.w(nc.tensor, self.c_ws[par])
                for fc in range(32):
                    mm = nc.tensor.matmul(pp[:, 0:HALF],
                                          WS[:, 4096 + 4096 * par + 128 * fc:4096 + 4096 * par + 128 * (fc + 1)],
                                          nrm[:, fc * HALF:(fc + 1) * HALF],
                                          start=(fc == 0), stop=(fc == 31))
                c_pe.inc(mm)
                self.w(nc.vector, c_pe)
                c_dve.inc(nc.vector.tensor_add(
                    self.x_res[:, bass.ds(512 * d + HALF * half, HALF)],
                    self.x_res[:, bass.ds(512 * d + HALF * half, HALF)], pp[:, 0:HALF]))
        InnerLoop(nc, self.tile_ctrs, 8, dpair)



def make_in_maps(inputs):
    blob = prep_weights(inputs)
    consts = prep_consts()
    convw = np.asarray(inputs["L0_attn_conv_weight"], np.float32)
    convw_r = np.ascontiguousarray(
        convw.reshape(CC, 128, KCONV).transpose(1, 0, 2).reshape(128, CC * KCONV))
    smalls = np.zeros((128, 512), np.float32)
    smalls[0, 0:32] = np.asarray(inputs["L0_attn_dt_bias"], np.float32)
    smalls[0, 32:64] = np.asarray(inputs["L0_attn_A_log"], np.float32)
    smalls[0, 64:192] = np.asarray(inputs["L0_attn_norm_w"], np.float32)
    cvals = np.zeros((128, 16), np.float32)
    cvals[:, 0] = LN_EPS
    cvals[:, 1] = L2_EPS
    cvals[:, 2] = 1.0
    cvals[:, 3] = -0.5 * np.log(128.0)
    cvals[:, 4] = RMS_EPS
    in_maps = []
    for core in range(8):
        m = {"xin": prep_core_x(inputs, core),
             "wf32": blob, "convw": convw_r, "smalls": smalls, "cvals": cvals}
        for nm, shp, dt in CONST_SPECS:
            m["c_" + nm] = consts[nm]
        in_maps.append(m)
    return in_maps


def run_dev(inputs, do_attn=True, dev_tiles=None, dev_stage=None):
    b = Builder(do_attn=do_attn, dev_tiles=dev_tiles, dev_stage=dev_stage)
    nc = b.build()
    in_maps = make_in_maps(inputs)
    res = run_bass_kernel_spmd(nc, in_maps, core_ids=list(range(8)))
    return [r["xout"] for r in res.results]


def kernel(**inputs):
    outs = run_dev(inputs, do_attn=True, dev_tiles=None)
    full = np.empty((16, 2048, 2048), np.float32)
    for core in range(8):
        o = outs[core].reshape(NSEQ, NTILE, DC, 128, P)
        xf = o.transpose(0, 2, 3, 1, 4).reshape(NSEQ, D, 2048)
        full[2 * core:2 * core + 2] = xf.transpose(0, 2, 1)
    return full


# version 03
# speedup vs baseline: 1.2100x; 1.0169x over previous
import contextlib
import numpy as np
import ml_dtypes
import concourse.bass as bass
import concourse.bacc as bacc
import concourse.mybir as mybir
from concourse.bass_utils import run_bass_kernel_spmd

AF = mybir.ActivationFunctionType
ALU = mybir.AluOpType
AX = mybir.AxisListType
F32 = mybir.dt.float32
BF16 = mybir.dt.bfloat16
BF = ml_dtypes.bfloat16

D = 2048
DC = 16
P = 512
NTILE = 4
NSEQ = 2
F = 5632
FC = 44
E = 8
R = 16
CONVD = 8192
CC = 64
KH = 16
VH = 32
A = 128
U = 128
CH = 128
HALF = 256
ZD = 4096
KCONV = 4
LN_EPS = 1e-5
L2_EPS = 1e-6
RMS_EPS = 1e-6

def _blob_layout():
    off = {}
    r = 0
    def add(name, rows):
        nonlocal r
        off[name] = r
        r += rows
    add("w1t", FC * 128)
    add("w3t", FC * 128)
    add("w2t", DC * 352)
    add("qkvt", CC * 128)
    add("wz", DC * 256)
    add("woutt", DC * 256)
    add("wba", 64)
    add("b1p", FC * 64)
    add("b3p", FC * 64)
    add("a2p", FC * 32)
    add("b2p", DC * 64)
    add("a1t", 128)
    add("a3t", 128)
    add("gatet", 8)
    if r % 256:
        add("_pad", 256 - r % 256)
    return off, r

BLOB_OFF, BLOB_ROWS = _blob_layout()

def _tile_lhs(w):
    K_, M_ = w.shape
    return w.reshape(K_ // 128, 128, M_ // 128, 128).transpose(2, 0, 1, 3)


def prep_weights(i):
    blob = np.zeros((BLOB_ROWS, 2048), np.float32)

    def put(name, arr):
        rows = arr.size // 2048
        assert rows * 2048 == arr.size
        blob[BLOB_OFF[name]:BLOB_OFF[name] + rows] = \
            np.ascontiguousarray(arr, np.float32).reshape(rows, 2048)

    w1 = np.asarray(i["L0_ffn_w1"], np.float32)
    w3 = np.asarray(i["L0_ffn_w3"], np.float32)
    w2 = np.asarray(i["L0_ffn_w2"], np.float32)
    put("w1t", np.ascontiguousarray(_tile_lhs(w1.T).transpose(0, 2, 1, 3)))
    put("w3t", np.ascontiguousarray(_tile_lhs(w3.T).transpose(0, 2, 1, 3)))
    w2t = _tile_lhs(w2.T)
    put("w2t", np.ascontiguousarray(w2t.transpose(0, 2, 1, 3)).reshape(DC, 128, FC * 128))
    wqkv = np.asarray(i["L0_attn_W_qkv"], np.float32)
    put("qkvt", np.ascontiguousarray(_tile_lhs(wqkv).transpose(0, 2, 1, 3)))
    wz = np.asarray(i["L0_attn_W_z"], np.float32)
    put("wz", wz.reshape(DC, 128, ZD))
    wout = np.asarray(i["L0_attn_W_out"], np.float32)
    wot = wout.reshape(32, 128, DC, 128).transpose(2, 0, 1, 3)
    put("woutt", np.ascontiguousarray(wot.transpose(0, 2, 1, 3)).reshape(DC, 128, 32 * 128))
    wba = np.concatenate([np.asarray(i["L0_attn_W_b"], np.float32),
                          np.asarray(i["L0_attn_W_a"], np.float32)], axis=1)
    put("wba", wba.reshape(DC, 128, 64))

    def bpad(b):
        out = np.zeros((FC, 128, 1024), np.float32)
        for e in range(E):
            t = b[e].reshape(FC, 128, R).transpose(0, 2, 1)
            out[:, 16 * e:16 * e + 16, 128 * e:128 * (e + 1)] = t
        return out
    put("b1p", bpad(np.asarray(i["L0_ffn_b1"], np.float32)))
    put("b3p", bpad(np.asarray(i["L0_ffn_b3"], np.float32)))
    a2 = np.asarray(i["L0_ffn_a2"], np.float32)
    a2p = np.zeros((FC, 128, 512), np.float32)
    for e in range(E):
        t = a2[e].reshape(R, FC, 128).transpose(1, 2, 0)
        a2p[:, :, 64 * e + 16 * (e % 4):64 * e + 16 * (e % 4) + 16] = t
    put("a2p", a2p)
    b2 = np.asarray(i["L0_ffn_b2"], np.float32)
    b2p = np.zeros((DC, 128, 1024), np.float32)
    for e in range(E):
        rbase = 64 * (e // 4) + 16 * (e % 4)
        t = b2[e].reshape(DC, 128, R).transpose(0, 2, 1)
        b2p[:, rbase:rbase + 16, 128 * e:128 * (e + 1)] = t
    put("b2p", b2p)
    a1 = np.asarray(i["L0_ffn_a1"], np.float32).reshape(E * R, D)
    put("a1t", a1.T.reshape(DC, 128, 128))
    a3 = np.asarray(i["L0_ffn_a3"], np.float32).reshape(E * R, D)
    put("a3t", a3.T.reshape(DC, 128, 128))
    gate = np.asarray(i["L0_ffn_gate_w"], np.float32)
    put("gatet", gate.T.reshape(DC, 128, 8))
    return blob


def prep_consts():
    c = {}
    idx = np.arange(128)
    l, m = np.meshgrid(idx, idx, indexing="ij")
    c["mask_low01"] = (m <= l).astype(np.float32)
    c["mask_up01"] = (m >= l).astype(np.float32)
    c["mask_lowneg"] = -(m < l).astype(np.float32)
    c["mask_upneg"] = -(m > l).astype(np.float32)
    c["id_bf"] = np.eye(128).astype(BF)
    c["id_f32"] = np.eye(128).astype(np.float32)
    c["zeros_bf"] = np.zeros((128, 128), BF)
    c["ones128_bf"] = np.ones((128, 128), BF)
    c["ones128_f32"] = np.ones((128, 128), np.float32)
    o8 = np.zeros((128, 8 * 128), BF)
    for e in range(8):
        o8[e, 128 * e:128 * (e + 1)] = 1.0
    c["onehot8p"] = o8
    return c


CONST_SPECS = [("mask_low01", [128, 128], F32), ("mask_up01", [128, 128], F32),
               ("mask_lowneg", [128, 128], F32), ("mask_upneg", [128, 128], F32),
               ("id_bf", [128, 128], BF16), ("id_f32", [128, 128], F32),
               ("zeros_bf", [128, 128], BF16),
               ("ones128_bf", [128, 128], BF16),
               ("ones128_f32", [128, 128], F32),
               ("onehot8p", [128, 1024], BF16)]


def prep_core_x(inputs, core):
    hid = np.asarray(inputs["L0_attn_hidden"], np.float32)
    xs = hid[2 * core:2 * core + 2]
    xf = xs.transpose(0, 2, 1).reshape(NSEQ, DC, 128, NTILE, P).transpose(0, 3, 1, 2, 4)
    return np.ascontiguousarray(xf, np.float32).reshape(NSEQ * NTILE, DC, 128, P)


class Ctr:
    def __init__(self, nc, name, unit=1):
        self._cm = nc.semaphore(name)
        self.h = self._cm.__enter__()
        self.stat = 0
        self.terms = []
        self.unit = unit

    def inc(self, instr, n=1):
        instr.then_inc(self.h, n * self.unit)
        self.stat += n * self.unit

    def val(self, ofs=0):
        v = self.stat + ofs * self.unit
        for reg, coef in self.terms:
            if coef:
                v = reg * coef + v
        return v

    def is_static(self):
        return not any(coef for _, coef in self.terms)

    def clear(self):
        assert not self.terms
        self.stat = 0


def affine_fori(nc, ctrs, start, n, body, deltas):
    if n <= start:
        return
    entry = [c.stat for c in ctrs]
    with nc.Fori(start, n) as i:
        for c, d in zip(ctrs, deltas):
            c.terms.append([i + (-start), d])
        body(i)
    for c, en, d in zip(ctrs, entry, deltas):
        c.terms.pop()
        assert c.stat - en == d, f"body delta mismatch {c.stat - en} vs {d}"
        c.stat = en + d * (n - start)


class InnerLoop:
    def __init__(self, nc, ctrs, n, body):
        assert n >= 2
        s0 = [c.stat for c in ctrs]
        body(0)
        s1 = [c.stat for c in ctrs]
        body(1)
        deltas = [c.stat - b for c, b in zip(ctrs, s1)]
        for c, b0, b1, d in zip(ctrs, s0, s1, deltas):
            assert b1 - b0 == d, f"nonuniform delta {b1 - b0} vs {d}"
        if n > 2:
            entry = [c.stat for c in ctrs]
            with nc.Fori(2, n) as i:
                for c, d in zip(ctrs, deltas):
                    c.terms.append([i + (-2), d])
                body(i)
            for c, en, d in zip(ctrs, entry, deltas):
                c.terms.pop()
                assert c.stat - en == d, f"body delta changed {c.stat - en} vs {d}"
                c.stat = en + d * (n - 2)


class Builder:
    def __init__(self, do_attn=True, dev_tiles=None, dev_stage=None):
        self.do_attn = do_attn
        self.dev_tiles = dev_tiles
        self.dev_stage = dev_stage
        nc = self.nc = bacc.Bacc("TRN2", target_bir_lowering=False, debug=False,
                                 dynamic_dma_scratch_size=16384)
        self.ctx = []
        self.xin = nc.dram_tensor("xin", [NSEQ * NTILE, DC, 128, P], F32, kind="ExternalInput").ap()
        self.xout = nc.dram_tensor("xout", [NSEQ * NTILE, DC, 128, P], F32, kind="ExternalOutput").ap()
        self.wf32 = nc.dram_tensor("wf32", [BLOB_ROWS, 2048], F32, kind="ExternalInput").ap()
        self.wbf = nc.dram_tensor("wbf", [BLOB_ROWS, 2048], BF16, kind="Internal").ap()
        self.convw_d = nc.dram_tensor("convw", [128, 256], F32, kind="ExternalInput").ap()
        self.smalls_d = nc.dram_tensor("smalls", [128, 512], F32, kind="ExternalInput").ap()
        self.cvals_d = nc.dram_tensor("cvals", [128, 16], F32, kind="ExternalInput").ap()
        self.cst_d = {}
        for nm, shp, dt in CONST_SPECS:
            self.cst_d[nm] = nc.dram_tensor("c_" + nm, shp, dt, kind="ExternalInput").ap()

    def sb(self, name, shape, dtype):
        t = self.nc.sbuf_tensor(name, shape, dtype)
        h = t.__enter__()
        self.ctx.append(t)
        return h

    def sync_fence(self):
        nc = self.nc
        nc.sync.wait_ge(self.c_ws[0].h, self.c_ws[0].val())
        nc.sync.wait_ge(self.c_ws[1].h, self.c_ws[1].val())

    def w(self, eng, ctr, ofs=0):
        v = ctr.val(ofs)
        if isinstance(v, int):
            if v <= 0:
                return
        eng.wait_ge(ctr.h, v)

    def build(self):
        nc = self.nc
        self.x_res = self.sb("x_res", [128, DC * P], F32)
        self.xln = self.sb("xln", [128, DC * P], BF16)
        self.WS = self.sb("WS", [128, 13312], BF16)
        self.st = self.sb("st", [128, VH * 128], F32)
        self.stbf = self.sb("stbf", [128, VH * 128], BF16)
        self.K = {}
        for nm, shp, dt in CONST_SPECS:
            self.K[nm] = self.sb("k_" + nm, shp, dt)
        self.convw = self.sb("convw_sb", [128, 256], F32)
        self.dtb_b = self.sb("dtb_b", [128, 32], F32)
        self.negeA_b = self.sb("negeA_b", [128, 32], F32)
        self.normw_rep = self.sb("normw_rep", [128, ZD], BF16)
        self.gates = self.sb("gates", [128, DC * 8], BF16)
        self.wba = self.sb("wba_sb", [128, DC * 64], BF16)
        self.a1s = self.sb("a1s", [128, DC * 128], BF16)
        self.a3s = self.sb("a3s", [128, DC * 128], BF16)
        self.smalls = self.sb("smalls_sb", [128, 512], F32)
        self.cvals = self.sb("cvals_sb", [128, 16], F32)
        self.ctail = self.sb("ctail", [128, CC * 4], BF16)

        self.c_w = Ctr(nc, "c_w", unit=16)
        self.c_ws = [Ctr(nc, "c_ws0", unit=16), Ctr(nc, "c_ws1", unit=16)]
        self.c_x = Ctr(nc, "c_x", unit=16)
        self.c_s = Ctr(nc, "c_s", unit=16)
        self.c_pe = Ctr(nc, "c_pe")
        self.c_act = Ctr(nc, "c_act")
        self.c_dve = Ctr(nc, "c_dve")
        self.c_pool = Ctr(nc, "c_pool")
        self.tile_ctrs = [self.c_w, self.c_ws[0], self.c_ws[1], self.c_x, self.c_s,
                          self.c_pe, self.c_act, self.c_dve, self.c_pool]

        self.emit_prologue()
        ntiles_total = self.dev_tiles if self.dev_tiles is not None else NSEQ * NTILE
        if ntiles_total == 0:
            nc.finalize()
            return nc
        done = 0
        deltas = None
        for seq in range(NSEQ):
            if done >= ntiles_total:
                break
            self.emit_seq_reset()
            ntile = min(NTILE, ntiles_total - done)
            start = 0
            if deltas is None:
                s0 = [c.stat for c in self.tile_ctrs]
                self.emit_tile(seq, 0)
                deltas = [c.stat - b for c, b in zip(self.tile_ctrs, s0)]
                start = 1
            affine_fori(nc, self.tile_ctrs, start, ntile,
                        lambda t, seq=seq: self.emit_tile(seq, t), deltas)
            done += ntile
        nc.finalize()
        return nc

    def emit_prologue(self):
        nc = self.nc
        c_w, c_pe, c_act, c_dve, c_pool = self.c_w, self.c_pe, self.c_act, self.c_dve, self.c_pool
        for nm, shp, dt in CONST_SPECS:
            c_w.inc(nc.sync.dma_start(out=self.K[nm][:, :], in_=self.cst_d[nm][:, :]))
        c_w.inc(nc.sync.dma_start(out=self.convw[:, :], in_=self.convw_d[:, :]))
        c_w.inc(nc.sync.dma_start(out=self.smalls[:, :], in_=self.smalls_d[:, :]))
        c_w.inc(nc.sync.dma_start(out=self.cvals[:, :], in_=self.cvals_d[:, :]))
        NCAST = BLOB_ROWS // 128
        wr32 = self.wf32.rearrange("(n p) d -> n p d", p=128)
        wrbf = self.wbf.rearrange("(n p) d -> n p d", p=128)
        c_cast = Ctr(nc, "c_cast")
        c_cp = [Ctr(nc, "c_cp0"), Ctr(nc, "c_cp1")]
        c_ld = [Ctr(nc, "c_ld0", unit=16), Ctr(nc, "c_ld1", unit=16)]
        c_st = [Ctr(nc, "c_st0", unit=16), Ctr(nc, "c_st1", unit=16)]
        with nc.sbuf_tensor("cast_in", [128, 2 * 2048], F32) as cin, \
             nc.sbuf_tensor("cast_out", [128, 2 * 2048], BF16) as cout:
            def cast_body(j):
                for par in range(2):
                    self.w(nc.sync, c_cp[par], 0)
                    c_ld[par].inc(nc.sync.dma_start(out=cin[:, par * 2048:(par + 1) * 2048],
                                                    in_=wr32[j * 2 + par, :, :]))
                for par in range(2):
                    self.w(nc.gpsimd, c_ld[par])
                    self.w(nc.gpsimd, c_st[par], 0)
                    c_cp[par].inc(nc.gpsimd.tensor_copy(out=cout[:, par * 2048:(par + 1) * 2048],
                                                        in_=cin[:, par * 2048:(par + 1) * 2048]))
                for par in range(2):
                    self.w(nc.scalar, c_cp[par])
                    c_st[par].inc(nc.scalar.dma_start(out=wrbf[j * 2 + par, :, :],
                                                      in_=cout[:, par * 2048:(par + 1) * 2048]))
            InnerLoop(nc, [c_ld[0], c_ld[1], c_cp[0], c_cp[1], c_st[0], c_st[1]],
                      NCAST // 2, cast_body)
            nc.scalar.wait_ge(c_st[0].h, c_st[0].val())
            nc.scalar.wait_ge(c_st[1].h, c_st[1].val())
            nc.sync.wait_ge(c_ld[0].h, c_ld[0].val())
            nc.sync.wait_ge(c_ld[1].h, c_ld[1].val())
        nc.all_engine_barrier()
        with nc.psum_tensor("p_pro", [128, 512], F32) as pb:
            nc.tensor.wait_ge(c_w.h, c_w.val())
            mm = nc.tensor.matmul(pb[:, 0:192], self.K["ones128_f32"][0:1, :],
                                  self.smalls[0:1, 0:192], start=True, stop=True)
            c_pe.inc(mm)
            self.w(nc.vector, self.c_pe)
            c_dve.inc(nc.vector.tensor_copy(out=self.dtb_b[:, :], in_=pb[:, 0:32]))
            self.w(nc.scalar, self.c_pe)
            c_act.inc(nc.scalar.activation(out=self.negeA_b[:, :], in_=pb[:, 32:64], func=AF.Exp))
            self.w(nc.vector, self.c_act)
            c_dve.inc(nc.vector.tensor_scalar_mul(self.negeA_b[:, :], self.negeA_b[:, :], -1.0))
            self.w(nc.vector, self.c_pe)
            c_dve.inc(nc.vector.tensor_copy(out=self.normw_rep[:, 0:128], in_=pb[:, 64:192]))
            for rep in range(1, VH):
                self.w(nc.gpsimd, self.c_dve)
                c_pool.inc(nc.gpsimd.tensor_copy(out=self.normw_rep[:, 128 * rep:128 * (rep + 1)],
                                                 in_=self.normw_rep[:, 0:128]))
            nc.sync.wait_ge(c_w.h, c_w.val())
            for nm, dst, width in [("gatet", self.gates, 8),
                                   ("a1t", self.a1s, 128),
                                   ("a3t", self.a3s, 128),
                                   ("wba", self.wba, 64)]:
                base = BLOB_OFF[nm] * 2048
                ap = bass.AP(self.wbf.tensor, base,
                             [[width, 128], [128 * width, DC], [1, width]])
                dstv = dst[:, :].rearrange("p (k c) -> p k c", c=width)
                c_w.inc(nc.sync.dma_start(out=dstv, in_=ap))
            nc.sync.wait_ge(c_w.h, c_w.val())
        nc.all_engine_barrier()

    def emit_seq_reset(self):
        nc = self.nc
        nc.vector.memset(self.st[:, :], 0.0)
        nc.vector.memset(self.stbf[:, :], 0.0)
        nc.gpsimd.memset(self.ctail[:, :], 0.0)
        nc.all_engine_barrier()

    def wdma(self, ctr, dst_ap, name, row_off, nfree):
        base = (BLOB_OFF[name] + row_off) * 2048
        ap = bass.AP(self.wbf.tensor, base, [[nfree, 128], [1, nfree]])
        ctr.inc(self.nc.sync.dma_start(out=dst_ap, in_=ap))

    def emit_tile(self, seq, t):
        nc = self.nc
        idx = seq * NTILE + t
        xap = bass.AP(self.xin.tensor, idx * (DC * 128 * P),
                      [[P, 128], [128 * P, DC], [1, P]])
        self.c_x.inc(nc.gpsimd.dma_start(
            out=self.x_res[:, :].rearrange("p (k c) -> p k c", c=P), in_=xap))
        nc.gpsimd.wait_ge(self.c_x.h, self.c_x.val())
        nc.all_engine_barrier()
        self.emit_ffn()
        nc.all_engine_barrier()
        if self.do_attn:
            self.emit_ln_standalone()
            nc.all_engine_barrier()
            self.emit_attn()
            nc.all_engine_barrier()
        if not (self.dev_stage or "").startswith("attn"):
            self.emit_ffn()
            nc.all_engine_barrier()
        yap = bass.AP(self.xout.tensor, idx * (DC * 128 * P),
                      [[P, 128], [128 * P, DC], [1, P]])
        self.c_x.inc(nc.gpsimd.dma_start(
            out=yap, in_=self.x_res[:, :].rearrange("p (k c) -> p k c", c=P)))
        nc.gpsimd.wait_ge(self.c_x.h, self.c_x.val())
        nc.sync.wait_ge(self.c_w.h, self.c_w.val())
        nc.sync.wait_ge(self.c_ws[0].h, self.c_ws[0].val())
        nc.sync.wait_ge(self.c_ws[1].h, self.c_ws[1].val())
        nc.scalar.wait_ge(self.c_s.h, self.c_s.val())
        nc.all_engine_barrier()

    def emit_ln_standalone(self):
        nc = self.nc
        K = self.K
        c_pe, c_act, c_dve, c_pool = self.c_pe, self.c_act, self.c_dve, self.c_pool
        self._ln_n = getattr(self, "_ln_n", 0) + 1
        lsx = f"_l{self._ln_n}"
        with nc.sbuf_tensor("l_sc" + lsx, [128, 2 * 512], F32) as sc32, \
             nc.sbuf_tensor("l_rows" + lsx, [1, 2048], F32) as rows, \
             nc.sbuf_tensor("l_sq" + lsx, [128, 2 * 512], BF16) as sqb, \
             nc.psum_tensor("lp6" + lsx, [128, 512], F32) as p6, \
             nc.psum_tensor("lp7" + lsx, [128, 512], F32) as p7:
            xr = self.x_res
            xln = self.xln
            for k in range(DC):
                c_pool.inc(nc.gpsimd.tensor_copy(out=xln[:, 512 * k:512 * (k + 1)],
                                                 in_=xr[:, 512 * k:512 * (k + 1)]))
            pe_sq0 = c_pe.val(0)
            for k in range(DC):
                self.w(nc.scalar, c_pool, -(DC - 1 - k))
                if k >= 2:
                    nc.scalar.wait_ge(c_pe.h, pe_sq0 + (k - 1))
                c_act.inc(nc.scalar.activation(out=sqb[:, 512 * (k % 2):512 * (k % 2 + 1)],
                                               in_=xln[:, 512 * k:512 * (k + 1)], func=AF.Square))
                self.w(nc.tensor, c_pool, -(DC - 1 - k))
                nc.tensor.matmul(p6[0:1, :], K["ones128_bf"][:, 0:1], xln[:, 512 * k:512 * (k + 1)],
                                 start=(k == 0), stop=(k == DC - 1))
                self.w(nc.tensor, c_act)
                c_pe.inc(nc.tensor.matmul(p7[0:1, :], K["ones128_bf"][:, 0:1],
                                          sqb[:, 512 * (k % 2):512 * (k % 2 + 1)],
                                          start=(k == 0), stop=(k == DC - 1)))
            self.w(nc.scalar, c_pe)
            c_act.inc(nc.scalar.mul(rows[0:1, 0:512], p6[0:1, :], 1.0 / D))
            c_act.inc(nc.scalar.mul(rows[0:1, 512:1024], p7[0:1, :], 1.0 / D))
            self.w(nc.vector, c_act)
            c_dve.inc(nc.vector.tensor_mul(rows[0:1, 1024:1536], rows[0:1, 0:512], rows[0:1, 0:512]))
            self.w(nc.vector, c_dve)
            c_dve.inc(nc.vector.tensor_sub(rows[0:1, 512:1024], rows[0:1, 512:1024], rows[0:1, 1024:1536]))
            self.w(nc.scalar, c_dve)
            c_act.inc(nc.scalar.activation(out=rows[0:1, 1024:1536], in_=rows[0:1, 512:1024],
                                           func=AF.Ln, bias=self.cvals[0:1, 0:1]))
            self.w(nc.scalar, c_act)
            c_act.inc(nc.scalar.activation(out=rows[0:1, 512:1024], in_=rows[0:1, 1024:1536],
                                           func=AF.Exp, scale=-0.5))
            self.w(nc.vector, c_act)
            c_dve.inc(nc.vector.tensor_mul(rows[0:1, 1024:1536], rows[0:1, 0:512], rows[0:1, 512:1024]))
            self.w(nc.tensor, c_dve)
            nc.tensor.matmul(p6[:, :], K["ones128_f32"][0:1, :], rows[0:1, 512:1024], start=True, stop=True)
            c_pe.inc(nc.tensor.matmul(p7[:, :], K["ones128_f32"][0:1, :], rows[0:1, 1024:1536],
                                      start=True, stop=True))
            self.w(nc.scalar, c_pe)
            c_act.inc(nc.scalar.copy(out=sc32[:, 0:512], in_=p6[:, :]))
            c_act.inc(nc.scalar.copy(out=sc32[:, 512:1024], in_=p7[:, :]))
            for k in range(DC):
                self.w(nc.gpsimd, c_act)
                c_pool.inc(nc.gpsimd.tensor_mul(xln[:, 512 * k:512 * (k + 1)],
                                                xr[:, 512 * k:512 * (k + 1)], sc32[:, 0:512]))
                self.w(nc.vector, c_pool)
                c_dve.inc(nc.vector.tensor_sub(xln[:, 512 * k:512 * (k + 1)],
                                               xln[:, 512 * k:512 * (k + 1)], sc32[:, 512:1024]))
            nc.vector.wait_ge(c_dve.h, c_dve.val())

    def emit_ffn(self):
        nc = self.nc
        K = self.K
        self._ffn_n = getattr(self, "_ffn_n", 0) + 1
        sfx = f"_{self._ffn_n}"
        c_w, c_pe, c_act, c_dve, c_pool = self.c_w, self.c_pe, self.c_act, self.c_dve, self.c_pool
        with nc.sbuf_tensor("f_sc" + sfx, [128, 3 * 512], F32) as sc32, \
             nc.sbuf_tensor("f_rows" + sfx, [1, 2048], F32) as rows, \
             nc.sbuf_tensor("f_sq" + sfx, [128, 2 * 512], BF16) as sqb, \
             nc.sbuf_tensor("f_base" + sfx, [128, 2 * 512], BF16) as baseb, \
             nc.sbuf_tensor("f_sg" + sfx, [128, 6 * 512], BF16) as sgb, \
             nc.sbuf_tensor("f_t1" + sfx, [128, 2 * 512], BF16) as t1b, \
             nc.sbuf_tensor("f_t2" + sfx, [128, 512], BF16) as t2ev, \
             nc.sbuf_tensor("f_wb" + sfx, [128, 8 * 512], BF16) as wb8, \
             nc.sbuf_tensor("f_wt" + sfx, [32, 512], BF16) as wT32, \
             nc.sbuf_tensor("f_wfl" + sfx, [128, 64], F32) as wfl, \
             nc.sbuf_tensor("f_sact" + sfx, [128, FC * 512], BF16) as sact, \
             nc.psum_tensor("fp0" + sfx, [128, 512], F32) as p0, \
             nc.psum_tensor("fp1" + sfx, [128, 512], F32) as p1, \
             nc.psum_tensor("fp2" + sfx, [128, 512], F32) as p2, \
             nc.psum_tensor("fp3" + sfx, [128, 512], F32) as p3, \
             nc.psum_tensor("fp4" + sfx, [128, 512], F32) as p4, \
             nc.psum_tensor("fp5" + sfx, [128, 512], F32) as p5, \
             nc.psum_tensor("fp6" + sfx, [128, 512], F32) as p6, \
             nc.psum_tensor("fp7" + sfx, [128, 512], F32) as p7:
            xr = self.x_res
            xln = self.xln
            for k in range(DC):
                c_pool.inc(nc.gpsimd.tensor_copy(out=xln[:, 512 * k:512 * (k + 1)],
                                                 in_=xr[:, 512 * k:512 * (k + 1)]))
            pe_sq0 = c_pe.val(0)
            for k in range(DC):
                self.w(nc.scalar, c_pool, -(DC - 1 - k))
                if k >= 2:
                    nc.scalar.wait_ge(c_pe.h, pe_sq0 + (k - 1))
                c_act.inc(nc.scalar.activation(out=sqb[:, 512 * (k % 2):512 * (k % 2 + 1)],
                                               in_=xln[:, 512 * k:512 * (k + 1)], func=AF.Square))
                self.w(nc.tensor, c_pool, -(DC - 1 - k))
                nc.tensor.matmul(p6[0:1, :], K["ones128_bf"][:, 0:1], xln[:, 512 * k:512 * (k + 1)],
                                 start=(k == 0), stop=(k == DC - 1))
                self.w(nc.tensor, c_act)
                c_pe.inc(nc.tensor.matmul(p7[0:1, :], K["ones128_bf"][:, 0:1],
                                          sqb[:, 512 * (k % 2):512 * (k % 2 + 1)],
                                          start=(k == 0), stop=(k == DC - 1)))
            self.w(nc.scalar, c_pe)
            c_act.inc(nc.scalar.mul(rows[0:1, 0:512], p6[0:1, :], 1.0 / D))
            c_act.inc(nc.scalar.mul(rows[0:1, 512:1024], p7[0:1, :], 1.0 / D))
            self.w(nc.vector, c_act)
            c_dve.inc(nc.vector.tensor_mul(rows[0:1, 1024:1536], rows[0:1, 0:512], rows[0:1, 0:512]))
            self.w(nc.vector, c_dve)
            c_dve.inc(nc.vector.tensor_sub(rows[0:1, 512:1024], rows[0:1, 512:1024], rows[0:1, 1024:1536]))
            self.w(nc.scalar, c_dve)
            c_act.inc(nc.scalar.activation(out=rows[0:1, 1024:1536], in_=rows[0:1, 512:1024],
                                           func=AF.Ln, bias=self.cvals[0:1, 0:1]))
            self.w(nc.scalar, c_act)
            c_act.inc(nc.scalar.activation(out=rows[0:1, 512:1024], in_=rows[0:1, 1024:1536],
                                           func=AF.Exp, scale=-0.5))
            self.w(nc.vector, c_act)
            c_dve.inc(nc.vector.tensor_mul(rows[0:1, 1024:1536], rows[0:1, 0:512], rows[0:1, 512:1024]))
            self.w(nc.tensor, c_dve)
            nc.tensor.matmul(p6[:, :], K["ones128_f32"][0:1, :], rows[0:1, 512:1024], start=True, stop=True)
            c_pe.inc(nc.tensor.matmul(p7[:, :], K["ones128_f32"][0:1, :], rows[0:1, 1024:1536],
                                      start=True, stop=True))
            self.w(nc.scalar, c_pe)
            c_act.inc(nc.scalar.copy(out=sc32[:, 0:512], in_=p6[:, :]))
            c_act.inc(nc.scalar.copy(out=sc32[:, 512:1024], in_=p7[:, :]))
            for k in range(DC):
                self.w(nc.gpsimd, c_act)
                c_pool.inc(nc.gpsimd.tensor_mul(xln[:, 512 * k:512 * (k + 1)],
                                                xr[:, 512 * k:512 * (k + 1)], sc32[:, 0:512]))
                self.w(nc.vector, c_pool)
                c_dve.inc(nc.vector.tensor_sub(xln[:, 512 * k:512 * (k + 1)],
                                               xln[:, 512 * k:512 * (k + 1)], sc32[:, 512:1024]))
            ln_dve_done = c_dve.val()
            if self.dev_stage == "ln":
                return
            nc.tensor.wait_ge(c_dve.h, ln_dve_done)
            for g in range(4):
                for k in range(DC):
                    mm = nc.tensor.matmul(p0[:, 8 * g:8 * (g + 1)],
                                          xln[:, 512 * k + 128 * g:512 * k + 128 * g + 128],
                                          self.gates[:, 8 * k:8 * (k + 1)],
                                          start=(k == 0), stop=(k == DC - 1))
            c_pe.inc(mm)
            self.w(nc.vector, c_pe)
            for g in range(4):
                lg = p0[:, 8 * g:8 * (g + 1)]
                m1 = sc32[:, 1024 + 16 * g:1024 + 16 * g + 1]
                m2 = sc32[:, 1024 + 16 * g + 1:1024 + 16 * g + 2]
                w1c = sc32[:, 1024 + 16 * g + 2:1024 + 16 * g + 3]
                w2c = sc32[:, 1024 + 16 * g + 3:1024 + 16 * g + 4]
                eq1 = sc32[:, 1024 + 16 * g + 4:1024 + 16 * g + 12]
                l2v = wfl[:, 16 * g:16 * g + 8]
                eq2 = wfl[:, 16 * g + 8:16 * g + 16]
                c_dve.inc(nc.vector.reduce_max(m1, lg, axis=AX.X))
                self.w(nc.vector, c_dve)
                c_dve.inc(nc.vector.tensor_scalar(eq1, lg, m1, None, op0=ALU.is_equal))
                self.w(nc.vector, c_dve)
                c_dve.inc(nc.vector.scalar_tensor_tensor(out=l2v, in0=eq1, scalar=-1e30, in1=lg,
                                                         op0=ALU.mult, op1=ALU.add))
                self.w(nc.vector, c_dve)
                c_dve.inc(nc.vector.reduce_max(m2, l2v, axis=AX.X))
                self.w(nc.vector, c_dve)
                c_dve.inc(nc.vector.tensor_sub(w1c, m1, m2))
                self.w(nc.scalar, c_dve)
                c_act.inc(nc.scalar.activation(out=w1c, in_=w1c, func=AF.Sigmoid))
                self.w(nc.vector, c_act)
                c_dve.inc(nc.vector.tensor_scalar(w2c, w1c, -1.0, 1.0, op0=ALU.mult, op1=ALU.add))
                c_dve.inc(nc.vector.tensor_scalar(eq2, l2v, m2, None, op0=ALU.is_equal))
                self.w(nc.vector, c_dve)
                c_dve.inc(nc.vector.tensor_scalar(eq1, eq1, w1c, None, op0=ALU.mult))
                self.w(nc.vector, c_dve)
                c_dve.inc(nc.vector.tensor_scalar(eq2, eq2, w2c, None, op0=ALU.mult))
                self.w(nc.vector, c_dve)
                c_dve.inc(nc.vector.tensor_add(sgb[:, 128 * g:128 * g + 8], eq1, eq2))
            c_pool.inc(nc.gpsimd.memset(wT32[:, :], 0.0))
            self.w(nc.tensor, c_dve)
            for g in range(4):
                mm = nc.tensor.matmul(p1[0:8, 128 * g:128 * (g + 1)],
                                      sgb[:, 128 * g:128 * g + 8], K["id_bf"][:, :],
                                      start=True, stop=True)
            c_pe.inc(mm)
            self.w(nc.scalar, c_pe)
            self.w(nc.scalar, c_pool)
            c_act.inc(nc.scalar.copy(out=wT32[0:8, :], in_=p1[0:8, :]))
            for e in range(8):
                pp = p2 if e % 2 == 0 else p3
                self.w(nc.tensor, c_act)
                c_pe.inc(nc.tensor.matmul(pp[:, :], K["onehot8p"][0:32, 128 * e:128 * (e + 1)],
                                          wT32[:, :], start=True, stop=True))
                self.w(nc.scalar, c_pe)
                c_act.inc(nc.scalar.mul(wb8[:, 512 * e:512 * (e + 1)], pp[:, :], 0.5))
            if self.dev_stage == "gate":
                return
            for k in range(DC):
                nc.tensor.matmul(p4[:, :], self.a1s[:, 128 * k:128 * (k + 1)],
                                 xln[:, 512 * k:512 * (k + 1)], start=(k == 0), stop=(k == DC - 1))
                mm = nc.tensor.matmul(p5[:, :], self.a3s[:, 128 * k:128 * (k + 1)],
                                      xln[:, 512 * k:512 * (k + 1)], start=(k == 0), stop=(k == DC - 1))
            c_pe.inc(mm)
            self.w(nc.scalar, c_pe)
            c_act.inc(nc.scalar.mul(t1b[:, 0:512], p4[:, :], 2.0))
            c_act.inc(nc.scalar.mul(t1b[:, 512:1024], p5[:, :], 2.0))
            self.w(nc.tensor, c_act)
            c_pe.inc(nc.tensor.matmul(p5[:, :], K["zeros_bf"][:, :], t1b[:, 0:512],
                                      start=True, stop=True))
            if self.dev_stage == "t1":
                return
            WS = self.WS
            SLOT = 6656
            ACT_PER_F = 19

            def fpair(j):
                for par in range(2):
                    f = j * 2 + par
                    self.w(nc.sync, c_pe, -26 * (1 - par))
                    self.wdma(self.c_ws[par], WS[:, SLOT * par:SLOT * par + 2048], "w1t", f * 128, 2048)
                    self.wdma(self.c_ws[par], WS[:, SLOT * par + 2048:SLOT * par + 4096], "w3t", f * 128, 2048)
                    self.wdma(self.c_ws[par], WS[:, SLOT * par + 4096:SLOT * par + 5120], "b1p", f * 64, 1024)
                    self.wdma(self.c_ws[par], WS[:, SLOT * par + 5120:SLOT * par + 6144], "b3p", f * 64, 1024)
                    self.wdma(self.c_ws[par], WS[:, SLOT * par + 6144:SLOT * par + 6656], "a2p", f * 32, 512)
                for par in range(2):
                    f = j * 2 + par
                    so = SLOT * par
                    bo = 512 * par
                    self.w(nc.tensor, self.c_ws[par])
                    for k in range(DC):
                        mm = nc.tensor.matmul(p0[:, :], WS[:, so + 128 * k:so + 128 * (k + 1)],
                                              xln[:, 512 * k:512 * (k + 1)],
                                              start=(k == 0), stop=(k == DC - 1))
                    c_pe.inc(mm)
                    for k in range(DC):
                        mm = nc.tensor.matmul(p1[:, :], WS[:, so + 2048 + 128 * k:so + 2048 + 128 * (k + 1)],
                                              xln[:, 512 * k:512 * (k + 1)],
                                              start=(k == 0), stop=(k == DC - 1))
                    c_pe.inc(mm)
                    self.w(nc.scalar, c_pe, -1)
                    c_act.inc(nc.scalar.copy(out=baseb[:, bo:bo + 512], in_=p0[:, :]))
                    self.w(nc.scalar, c_pe)
                    c_act.inc(nc.scalar.copy(out=sgb[:, 2048 + bo:2048 + bo + 512], in_=p1[:, :]))
                    def sact_t2(ep_, e_):
                        self.w(nc.tensor, c_dve)
                        nc.tensor.matmul(p4[:, :], K["id_bf"][:, :], sgb[:, 512 * ep_:512 * (ep_ + 1)],
                                         start=(e_ == 0), stop=(e_ == 7))
                        c_pe.inc(nc.tensor.matmul(p5[64 * (e_ // 4):64 * (e_ // 4) + 64, :],
                                                  WS[:, so + 6144 + 64 * e_:so + 6144 + 64 * (e_ + 1)],
                                                  sgb[:, 512 * ep_:512 * (ep_ + 1)], start=False,
                                                  stop=False, skip_group_check=True))
                    for e in range(8):
                        ep = e % 2
                        hp1 = p2 if ep == 0 else p3
                        hp3 = p6 if ep == 0 else p7
                        if e == 0:
                            self.w(nc.tensor, c_act)
                            self.w(nc.tensor, c_dve)
                        elif e >= 2:
                            self.w(nc.tensor, c_act, -1)
                            self.w(nc.tensor, c_dve, -7)
                        nc.tensor.matmul(hp1[:, :],
                                         WS[:, so + 4096 + 128 * e:so + 4096 + 128 * (e + 1)],
                                         t1b[:, 0:512], start=True, stop=False)
                        c_pe.inc(nc.tensor.matmul(hp1[:, :], K["id_bf"][:, :],
                                                  baseb[:, bo:bo + 512], start=False, stop=True))
                        nc.tensor.matmul(hp3[:, :],
                                         WS[:, so + 5120 + 128 * e:so + 5120 + 128 * (e + 1)],
                                         t1b[:, 512:1024], start=True, stop=False)
                        c_pe.inc(nc.tensor.matmul(hp3[:, :], K["id_bf"][:, :],
                                                  sgb[:, 2048 + bo:2048 + bo + 512], start=False, stop=True))
                        if e >= 1:
                            sact_t2(1 - ep, e - 1)
                        nc.scalar.wait_ge(c_pe.h, c_pe.val(-2 if e >= 1 else -1))
                        c_act.inc(nc.scalar.activation(out=sgb[:, 1024 + 512 * ep:1024 + 512 * (ep + 1)],
                                                       in_=hp1[:, :], func=AF.Silu))
                        nc.vector.wait_ge(c_pe.h, c_pe.val(-1 if e >= 1 else 0))
                        c_dve.inc(nc.vector.tensor_copy(out=sgb[:, 512 * ep:512 * (ep + 1)], in_=hp3[:, :]))
                        self.w(nc.vector, c_dve)
                        c_dve.inc(nc.vector.tensor_mul(sgb[:, 512 * ep:512 * (ep + 1)],
                                                       sgb[:, 512 * ep:512 * (ep + 1)],
                                                       wb8[:, 512 * e:512 * (e + 1)]))
                        self.w(nc.vector, c_dve)
                        self.w(nc.vector, c_act)
                        c_dve.inc(nc.vector.tensor_mul(sgb[:, 512 * ep:512 * (ep + 1)],
                                                       sgb[:, 512 * ep:512 * (ep + 1)],
                                                       sgb[:, 1024 + 512 * ep:1024 + 512 * (ep + 1)]))
                    sact_t2(1, 7)
                    self.w(nc.scalar, c_pe)
                    c_act.inc(nc.scalar.copy(out=sact[:, bass.ds(512 * f, 512)], in_=p4[:, :]))

            self.sync_fence()
            InnerLoop(nc, self.tile_ctrs, FC // 2, fpair)
            if self.dev_stage == "floop":
                return
            self.w(nc.scalar, c_pe)
            c_act.inc(nc.scalar.mul(t2ev[:, :], p5[:, :], 2.0))
            if self.dev_stage == "t2ev":
                return
            DSLOT = 6656
            self.sync_fence()
            nc.tensor.wait_ge(c_act.h, c_act.val())
            nc.sync.wait_ge(c_pe.h, c_pe.val())

            def dpair(j):
                for par in range(2):
                    d = j * 2 + par
                    self.w(nc.sync, c_dve, par - 1)
                    self.wdma(self.c_ws[par], WS[:, DSLOT * par:DSLOT * par + 5632], "w2t", d * 352, 5632)
                    self.wdma(self.c_ws[par], WS[:, DSLOT * par + 5632:DSLOT * par + 6656], "b2p", d * 64, 1024)
                for par in range(2):
                    d = j * 2 + par
                    so = DSLOT * par
                    pp = p0 if par == 0 else p1
                    self.w(nc.tensor, self.c_ws[par])
                    for f in range(FC):
                        nc.tensor.matmul(pp[:, :], WS[:, so + 128 * f:so + 128 * (f + 1)],
                                         sact[:, 512 * f:512 * (f + 1)],
                                         start=(f == 0), stop=False)
                    for e in range(8):
                        mm = nc.tensor.matmul(pp[:, :],
                                              WS[:, so + 5632 + 128 * e:so + 5632 + 128 * (e + 1)],
                                              t2ev[:, :], start=False, stop=(e == 7))
                    c_pe.inc(mm)
                    self.w(nc.vector, c_pe)
                    c_dve.inc(nc.vector.tensor_add(self.x_res[:, bass.ds(512 * d, 512)],
                                                   self.x_res[:, bass.ds(512 * d, 512)], pp[:, :]))
            InnerLoop(nc, self.tile_ctrs, DC // 2, dpair)
            nc.vector.wait_ge(c_dve.h, c_dve.val())

    def emit_attn(self):
        nc = self.nc
        K = self.K
        c_pe, c_act, c_dve, c_pool = self.c_pe, self.c_act, self.c_dve, self.c_pool
        self._attn_n = getattr(self, "_attn_n", 0) + 1
        sx = f"_a{self._attn_n}"
        WS = self.WS
        _es = contextlib.ExitStack()
        q_sb = _es.enter_context(nc.sbuf_tensor("q_sb" + sx, [128, KH * HALF], BF16))
        k_sb = _es.enter_context(nc.sbuf_tensor("k_sb" + sx, [128, KH * HALF], BF16))
        v_sb = _es.enter_context(nc.sbuf_tensor("v_sb" + sx, [128, VH * HALF], BF16))
        mp = _es.enter_context(nc.sbuf_tensor("mp" + sx, [128, 2 * 262], BF16))
        cvt = _es.enter_context(nc.sbuf_tensor("cvt" + sx, [128, 2 * 256], BF16))
        qsr = _es.enter_context(nc.sbuf_tensor("qsr" + sx, [128, 2 * 256], BF16))
        rrow = _es.enter_context(nc.sbuf_tensor("rrow" + sx, [1, 2 * 256], BF16))
        rrf = _es.enter_context(nc.sbuf_tensor("rrf" + sx, [1, 2 * 256], F32))
        siluz = _es.enter_context(nc.sbuf_tensor("siluz" + sx, [128, ZD], BF16))
        xg = _es.enter_context(nc.sbuf_tensor("xg" + sx, [128, ZD], BF16))
        nrm = _es.enter_context(nc.sbuf_tensor("nrm" + sx, [128, 32 * HALF], BF16))
        tmp8 = _es.enter_context(nc.sbuf_tensor("tmp8" + sx, [128, 10 * 32], F32))
        gjfm = _es.enter_context(nc.sbuf_tensor("gjfm" + sx, [32, 128], F32))
        colb = _es.enter_context(nc.sbuf_tensor("colb" + sx, [128, 128], F32))
        colbb = _es.enter_context(nc.sbuf_tensor("colbb" + sx, [128, 128], BF16))
        tsc = _es.enter_context(nc.sbuf_tensor("tsc" + sx, [128, 128], F32))
        eA = _es.enter_context(nc.sbuf_tensor("eA" + sx, [128, 128], F32))
        eTb = _es.enter_context(nc.sbuf_tensor("eT" + sx, [128, 2 * 128], F32))
        gmlb = _es.enter_context(nc.sbuf_tensor("gml" + sx, [128, 2 * 128], F32))
        gmub = _es.enter_context(nc.sbuf_tensor("gmu" + sx, [128, 2 * 128], F32))
        t2f = _es.enter_context(nc.sbuf_tensor("t2f" + sx, [128, 128], F32))
        xcur = _es.enter_context(nc.sbuf_tensor("xcur" + sx, [128, 2 * 128], BF16))
        ycur = _es.enter_context(nc.sbuf_tensor("ycur" + sx, [128, 2 * 128], BF16))
        xrh = _es.enter_context(nc.sbuf_tensor("xrh" + sx, [128, 2 * 256], BF16))
        kcd = _es.enter_context(nc.sbuf_tensor("kcd" + sx, [128, 2 * 128], BF16))
        nkcdt = _es.enter_context(nc.sbuf_tensor("nkcdt" + sx, [128, 2 * 128], BF16))
        vnew = _es.enter_context(nc.sbuf_tensor("vnew" + sx, [128, 2 * 128], BF16))
        intrat = _es.enter_context(nc.sbuf_tensor("intrat" + sx, [128, 2 * 128], BF16))
        qeb = _es.enter_context(nc.sbuf_tensor("qe" + sx, [128, 2 * 128], BF16))
        kdecb = _es.enter_context(nc.sbuf_tensor("kdec" + sx, [128, 2 * 128], BF16))
        pa0 = _es.enter_context(nc.psum_tensor("pa0" + sx, [128, 512], F32))
        pa1 = _es.enter_context(nc.psum_tensor("pa1" + sx, [128, 512], F32))
        pa2 = _es.enter_context(nc.psum_tensor("pa2" + sx, [128, 512], F32))
        pa3 = _es.enter_context(nc.psum_tensor("pa3" + sx, [128, 512], F32))
        pa4 = _es.enter_context(nc.psum_tensor("pa4" + sx, [128, 1024], BF16))
        pa5 = _es.enter_context(nc.psum_tensor("pa5" + sx, [128, 1024], BF16))
        pa6 = _es.enter_context(nc.psum_tensor("pa6" + sx, [128, 512], F32))
        pa7 = _es.enter_context(nc.psum_tensor("pa7" + sx, [128, 512], F32))
        if True:
            t_beta = tmp8[:, 0:32]
            t_gj = tmp8[:, 32:64]
            t_egj = tmp8[:, 64:96]
            t_edec = tmp8[:, 96:128]
            t_begj = tmp8[:, 128:160]
            t_cb = tmp8[:, 160:192]
            t_gs = tmp8[:, 192:224]
            env = dict(locals())
            ds_ = self.dev_stage or ""
            for half in range(2):
                self.attn_frontend(half, env)
                if ds_ == "attn_fe" and half == 0:
                    self._dump([(q_sb, 0, KH * HALF), (k_sb, KH * HALF, KH * HALF)])
                    break
                if ds_ in ("attn_v", "attn_v_noconv") and half == 0:
                    self._dump([(v_sb, 0, VH * HALF)])
                    break
                stop = False
                for cpar in range(2):
                    self.attn_chunk(half, cpar, env)
                    if ds_ in ("attn_z", "attn_core") and half == 0 and cpar == 0:
                        if ds_ == "attn_z":
                            self._dump([(siluz, 0, ZD), (tmp8, ZD, 320)])
                        else:
                            self._dump([(xg, 0, ZD), (nrm, ZD, 4096)])
                        stop = True
                        break
                if stop:
                    break
                self.attn_outproj(half, env)
        _es.close()

    def _dump(self, items):
        nc = self.nc
        nc.all_engine_barrier()
        for src, off, width in items:
            nc.vector.tensor_copy(out=self.x_res[:, off:off + width], in_=src[:, 0:width])
        nc.all_engine_barrier()

    def attn_frontend(self, half, L):
        nc = self.nc
        K = self.K
        c_pe, c_act, c_dve, c_pool = self.c_pe, self.c_act, self.c_dve, self.c_pool
        WS = self.WS
        q_sb, k_sb, v_sb = L["q_sb"], L["k_sb"], L["v_sb"]
        mp, cvt, qsr, rrow, rrf = L["mp"], L["cvt"], L["qsr"], L["rrow"], L["rrf"]
        pa0, pa1, pa2 = L["pa0"], L["pa1"], L["pa2"]
        xln = self.xln
        tok0 = half * HALF

        self.sync_fence()
        nc.tensor.wait_ge(c_dve.h, c_dve.val())

        def cpair(kind, cbase, dst):
            def body(j):
                for par in range(2):
                    c = cbase + j * 2 + par
                    self.w(nc.sync, c_pe, -(1 - par) * self._fe_pe_per_c[kind])
                    self.wdma(self.c_ws[par], WS[:, 2048 * par:2048 * (par + 1)],
                              "qkvt", c * 128, 2048)
                for par in range(2):
                    c = cbase + j * 2 + par
                    pp = pa0 if par == 0 else pa1
                    self.w(nc.tensor, self.c_ws[par])
                    apc = 5 if kind != "v" else 2
                    self.w(nc.tensor, c_act, 1 - 2 * apc)
                    for k in range(DC):
                        mm = nc.tensor.matmul(
                            pp[:, 0:HALF], WS[:, 2048 * par + 128 * k:2048 * par + 128 * (k + 1)],
                            xln[:, bass.ds(512 * k + tok0, HALF)],
                            start=(k == 0), stop=(k == DC - 1))
                    c_pe.inc(mm)
                    self.w(nc.scalar, c_pe)
                    c_act.inc(nc.scalar.copy(out=mp[:, 262 * par + 3:262 * par + 259], in_=pp[:, 0:HALF]))
                    self.w(nc.gpsimd, c_act)
                    c_pool.inc(nc.gpsimd.tensor_copy(out=mp[:, 262 * par:262 * par + 3],
                                                     in_=self.ctail[:, bass.ds(4 * c, 3)]))
                    self.w(nc.vector, c_pool)
                    if (self.dev_stage or "").endswith("noconv"):
                        c_dve.inc(nc.vector.tensor_copy(out=cvt[:, 256 * par:256 * (par + 1)],
                                                        in_=mp[:, 262 * par + 3:262 * par + 259]))
                    else:
                        c_dve.inc(nc.vector.tensor_scalar(cvt[:, 256 * par:256 * (par + 1)],
                                                          mp[:, 262 * par:262 * par + 256],
                                                          self.convw[:, bass.ds(4 * c, 1)], None, op0=ALU.mult))
                    for kk in range(1, KCONV):
                        if (self.dev_stage or "").endswith("noconv"):
                            self.w(nc.vector, c_dve)
                            c_dve.inc(nc.vector.tensor_copy(out=cvt[:, 256 * par:256 * (par + 1)],
                                                            in_=mp[:, 262 * par + 3:262 * par + 259]))
                            continue
                        self.w(nc.vector, c_dve)
                        c_dve.inc(nc.vector.scalar_tensor_tensor(
                            out=cvt[:, 256 * par:256 * (par + 1)],
                            in0=mp[:, 262 * par + kk:262 * par + kk + 256],
                            scalar=self.convw[:, bass.ds(4 * c + kk, 1)],
                            in1=cvt[:, 256 * par:256 * (par + 1)], op0=ALU.mult, op1=ALU.add))
                    self.w(nc.gpsimd, c_dve)
                    c_pool.inc(nc.gpsimd.tensor_copy(out=self.ctail[:, bass.ds(4 * c, 3)],
                                                     in_=mp[:, 262 * par + 256:262 * par + 259]))
                    if kind == "v":
                        self.w(nc.scalar, c_dve)
                        c_act.inc(nc.scalar.activation(
                            out=dst[:, bass.ds((c - cbase) * HALF, HALF)],
                            in_=cvt[:, 256 * par:256 * (par + 1)], func=AF.Silu))
                    else:
                        self.w(nc.scalar, c_dve)
                        c_act.inc(nc.scalar.activation(out=qsr[:, 256 * par:256 * (par + 1)],
                                                       in_=cvt[:, 256 * par:256 * (par + 1)], func=AF.Silu))
                        self.w(nc.scalar, c_act)
                        c_act.inc(nc.scalar.activation(out=cvt[:, 256 * par:256 * (par + 1)],
                                                       in_=qsr[:, 256 * par:256 * (par + 1)], func=AF.Square))
                        self.w(nc.tensor, c_act)
                        self.w(nc.tensor, c_dve)
                        c_pe.inc(nc.tensor.matmul(pa2[0:1, bass.ds(256 * par, 256)],
                                                  K["ones128_bf"][:, 0:1],
                                                  cvt[:, 256 * par:256 * (par + 1)], start=True, stop=True))
                        self.w(nc.scalar, c_pe)
                        c_act.inc(nc.scalar.activation(out=rrf[0:1, 256 * par:256 * (par + 1)],
                                                       in_=pa2[0:1, bass.ds(256 * par, 256)],
                                                       func=AF.Ln, bias=self.cvals[0:1, 1:2]))
                        self.w(nc.scalar, c_act)
                        if kind == "q":
                            c_act.inc(nc.scalar.activation(out=rrow[0:1, 256 * par:256 * (par + 1)],
                                                           in_=rrf[0:1, 256 * par:256 * (par + 1)],
                                                           func=AF.Exp, scale=-0.5,
                                                           bias=self.cvals[0:1, 3:4]))
                        else:
                            c_act.inc(nc.scalar.activation(out=rrow[0:1, 256 * par:256 * (par + 1)],
                                                           in_=rrf[0:1, 256 * par:256 * (par + 1)],
                                                           func=AF.Exp, scale=-0.5))
                        self.w(nc.tensor, c_act)
                        c_pe.inc(nc.tensor.matmul(pa2[:, bass.ds(256 * par, 256)],
                                                  K["ones128_bf"][0:1, :],
                                                  rrow[0:1, 256 * par:256 * (par + 1)],
                                                  start=True, stop=True))
                        self.w(nc.vector, c_pe)
                        c_dve.inc(nc.vector.tensor_mul(
                            dst[:, bass.ds((c - cbase) * HALF, HALF)],
                            qsr[:, 256 * par:256 * (par + 1)],
                            pa2[:, bass.ds(256 * par, 256)]))
            return body
        self._fe_pe_per_c = {"q": 3, "k": 3, "v": 1}
        InnerLoop(nc, self.tile_ctrs, 8, cpair("q", 0, q_sb))
        InnerLoop(nc, self.tile_ctrs, 8, cpair("k", 16, k_sb))
        InnerLoop(nc, self.tile_ctrs, 16, cpair("v", 32, v_sb))

    def attn_chunk(self, half, cpar, L):
        nc = self.nc
        K = self.K
        c_pe, c_act, c_dve, c_pool = self.c_pe, self.c_act, self.c_dve, self.c_pool
        WS = self.WS
        xln = self.xln
        q_sb, k_sb, v_sb, siluz, xg, nrm = L["q_sb"], L["k_sb"], L["v_sb"], L["siluz"], L["xg"], L["nrm"]
        gjfm, colb, colbb, tsc, eA, eTb = L["gjfm"], L["colb"], L["colbb"], L["tsc"], L["eA"], L["eTb"]
        gmlb, gmub, t2f = L["gmlb"], L["gmub"], L["t2f"]
        xcur, ycur, xrh, kcd, nkcdt, vnew, intrat, qeb, kdecb = \
            L["xcur"], L["ycur"], L["xrh"], L["kcd"], L["nkcdt"], L["vnew"], L["intrat"], L["qeb"], L["kdecb"]
        pa0, pa1, pa2, pa3, pa4, pa5, pa6, pa7 = \
            L["pa0"], L["pa1"], L["pa2"], L["pa3"], L["pa4"], L["pa5"], L["pa6"], L["pa7"]
        t_beta, t_gj, t_egj, t_edec, t_begj, t_cb, t_gs = \
            L["t_beta"], L["t_gj"], L["t_egj"], L["t_edec"], L["t_begj"], L["t_cb"], L["t_gs"]
        tmp8 = L["tmp8"]
        t_ssq = tmp8[:, 224:256]
        t_rstd = tmp8[:, 256:288]
        tok0 = half * HALF + cpar * CH
        csl = cpar * CH

        self.sync_fence()
        nc.tensor.wait_ge(c_dve.h, c_dve.val())
        nc.tensor.wait_ge(c_act.h, c_act.val())
        for g in range(4):
            for k in range(DC):
                par = k % 2
                self.w(nc.sync, c_pe, -1)
                base = (BLOB_OFF["wz"]) * 2048 + k * 128 * ZD + 1024 * g
                ap = bass.AP(self.wbf.tensor, base, [[ZD, 128], [1, 1024]])
                self.c_ws[par].inc(nc.sync.dma_start(
                    out=WS[:, 4096 + 1024 * par:4096 + 1024 * (par + 1)], in_=ap))
                self.w(nc.tensor, self.c_ws[par])
                if k == 0 and g > 0:
                    nc.tensor.wait_ge(c_act.h, c_act.val())
                nc.tensor.matmul(pa0[:, :], xln[:, bass.ds(512 * k + tok0, 128)],
                                 WS[:, 4096 + 1024 * par:4096 + 1024 * par + 512],
                                 start=(k == 0), stop=(k == DC - 1))
                mm = nc.tensor.matmul(pa1[:, :], xln[:, bass.ds(512 * k + tok0, 128)],
                                      WS[:, 4096 + 1024 * par + 512:4096 + 1024 * (par + 1)],
                                      start=(k == 0), stop=(k == DC - 1))
                c_pe.inc(mm)
            self.w(nc.scalar, c_pe)
            c_act.inc(nc.scalar.activation(out=siluz[:, 1024 * g:1024 * g + 512],
                                           in_=pa0[:, :], func=AF.Silu))
            c_act.inc(nc.scalar.activation(out=siluz[:, 1024 * g + 512:1024 * (g + 1)],
                                           in_=pa1[:, :], func=AF.Silu))
        for k in range(DC):
            mm = nc.tensor.matmul(pa2[:, 256:320], xln[:, bass.ds(512 * k + tok0, 128)],
                                  self.wba[:, 64 * k:64 * (k + 1)],
                                  start=(k == 0), stop=(k == DC - 1))
        c_pe.inc(mm)
        self.w(nc.scalar, c_pe)
        c_act.inc(nc.scalar.activation(out=t_beta, in_=pa2[:, 256:288], func=AF.Sigmoid))
        self.w(nc.vector, c_pe)
        c_dve.inc(nc.vector.tensor_add(tmp8[:, 288:320], pa2[:, 288:320], self.dtb_b[:, :]))
        self.w(nc.scalar, c_dve)
        c_act.inc(nc.scalar.activation(out=t_gj, in_=tmp8[:, 288:320], func=AF.Abs))
        self.w(nc.scalar, c_act)
        c_act.inc(nc.scalar.activation(out=t_egj, in_=t_gj, func=AF.Exp, scale=-1.0))
        self.w(nc.scalar, c_act)
        c_act.inc(nc.scalar.activation(out=t_egj, in_=t_egj, func=AF.Ln, bias=self.cvals[:, 2:3]))
        c_act.inc(nc.scalar.activation(out=t_edec, in_=tmp8[:, 288:320], func=AF.Relu))
        self.w(nc.vector, c_act)
        c_dve.inc(nc.vector.tensor_add(t_edec, t_edec, t_egj))
        self.w(nc.vector, c_dve)
        c_dve.inc(nc.vector.tensor_mul(tmp8[:, 288:320], t_edec, self.negeA_b[:, :]))
        g_tm = tmp8[:, 288:320]
        self.w(nc.tensor, c_dve)
        c_pe.inc(nc.tensor.matmul(pa3[0:32, 0:128], g_tm, K["mask_up01"][:, :],
                                  start=True, stop=True))
        self.w(nc.scalar, c_pe)
        c_act.inc(nc.scalar.copy(out=gjfm[:, :], in_=pa3[0:32, 0:128]))
        self.w(nc.tensor, c_act)
        c_pe.inc(nc.tensor.matmul(pa3[:, 128:160], K["ones128_f32"][:, :], g_tm,
                                  start=True, stop=True))
        c_pe.inc(nc.tensor.transpose(pa3[:, 160:192], gjfm[:, :], K["id_f32"][0:32, 0:32]))
        self.w(nc.scalar, c_pe)
        c_act.inc(nc.scalar.copy(out=t_gj, in_=pa3[:, 160:192]))
        self.w(nc.scalar, c_act)
        c_act.inc(nc.scalar.activation(out=t_egj, in_=t_gj, func=AF.Exp))
        self.w(nc.vector, c_act)
        c_dve.inc(nc.vector.tensor_sub(t_edec, pa3[:, 128:160], t_gj))
        self.w(nc.scalar, c_dve)
        c_act.inc(nc.scalar.activation(out=t_edec, in_=t_edec, func=AF.Exp))
        c_act.inc(nc.scalar.activation(out=t_cb, in_=pa3[:, 128:160], func=AF.Exp))
        self.w(nc.vector, c_act)
        c_dve.inc(nc.vector.tensor_mul(t_begj, t_beta, t_egj))
        for h in range(VH):
            j = h // 2
            hp = h % 2
            p6 = pa6 if hp == 0 else pa7
            pb = pa4 if hp == 0 else pa5
            prb = pa3[:, 0:128]
            qsl = q_sb[:, j * HALF + csl:j * HALF + csl + CH]
            ksl = k_sb[:, j * HALF + csl:j * HALF + csl + CH]
            vsl = v_sb[:, h * HALF + csl:h * HALF + csl + CH]
            stsl = self.st[:, 128 * h:128 * (h + 1)]
            stbsl = self.stbf[:, 128 * h:128 * (h + 1)]
            eT = eTb[:, 128 * hp:128 * (hp + 1)]
            gml = gmlb[:, 128 * (j % 2):128 * (j % 2 + 1)]
            gmu = gmub[:, 128 * (j % 2):128 * (j % 2 + 1)]
            xc = xcur[:, 128 * hp:128 * (hp + 1)]
            yc = ycur[:, 128 * hp:128 * (hp + 1)]
            xr = xrh[:, 256 * hp:256 * (hp + 1)]
            if hp == 0:
                nc.tensor.wait_ge(c_dve.h, c_dve.val())
                nc.tensor.matmul(pa2[:, 0:128], ksl, ksl, start=True, stop=True)
                c_pe.inc(nc.tensor.matmul(pa2[:, 128:256], ksl, qsl, start=True, stop=True))
                nc.tensor.wait_ge(c_dve.h, c_dve.val())
                c_pe.inc(nc.tensor.transpose(pa4[:, 512:640], ksl, K["id_bf"][:, :]))
                self.w(nc.vector, c_pe)
                c_dve.inc(nc.vector.tensor_mul(gml, pa2[:, 0:128], K["mask_lowneg"][:, :]))
                c_dve.inc(nc.vector.tensor_mul(gmu, pa2[:, 0:128], K["mask_upneg"][:, :]))
            c_dve.inc(nc.vector.tensor_scalar(colb[:, :], K["ones128_f32"][:, :],
                                              t_gj[:, h:h + 1], None, op0=ALU.mult))
            self.w(nc.tensor, c_dve)
            c_pe.inc(nc.tensor.transpose(prb, colb[:, :], K["id_f32"][:, :]))
            self.w(nc.vector, c_pe)
            c_dve.inc(nc.vector.tensor_scalar(tsc[:, :], prb, t_gj[:, h:h + 1], None,
                                              op0=ALU.subtract))
            self.w(nc.vector, c_dve)
            c_dve.inc(nc.vector.tensor_mul(eA[:, :], tsc[:, :], K["mask_lowneg"][:, :]))
            self.w(nc.scalar, c_dve)
            c_act.inc(nc.scalar.activation(out=eA[:, :], in_=eA[:, :], func=AF.Exp))
            c_dve.inc(nc.vector.tensor_mul(eT, tsc[:, :], K["mask_up01"][:, :]))
            self.w(nc.scalar, c_dve)
            c_act.inc(nc.scalar.activation(out=eT, in_=eT, func=AF.Exp))
            self.w(nc.vector, c_act)
            c_dve.inc(nc.vector.scalar_tensor_tensor(out=xc, in0=eA[:, :],
                                                     scalar=t_beta[:, h:h + 1], in1=gml,
                                                     op0=ALU.mult, op1=ALU.mult))
            c_dve.inc(nc.vector.tensor_scalar(colbb[:, :], K["ones128_bf"][:, :],
                                              t_beta[:, h:h + 1], None, op0=ALU.mult))
            self.w(nc.tensor, c_dve)
            c_pe.inc(nc.tensor.transpose(pb[:, 0:128], colbb[:, :], K["id_bf"][:, :]))
            self.w(nc.vector, c_pe)
            c_dve.inc(nc.vector.tensor_mul(t2f[:, :], eT, gmu))
            self.w(nc.vector, c_dve)
            c_dve.inc(nc.vector.tensor_mul(yc, t2f[:, :], pb[:, 0:128]))
            self.w(nc.tensor, c_dve)
            c_pe.inc(nc.tensor.transpose(pb[:, 256:384], vsl, K["id_bf"][:, :]))
            self.w(nc.vector, c_pe)
            c_dve.inc(nc.vector.tensor_scalar(xr[:, 0:128], pb[:, 256:384],
                                              t_beta[:, h:h + 1], None, op0=ALU.mult))
            c_dve.inc(nc.vector.tensor_scalar(xr[:, 128:256], pa4[:, 512:640],
                                              t_begj[:, h:h + 1], None, op0=ALU.mult))
            c_dve.inc(nc.vector.tensor_scalar(kdecb[:, 128 * hp:128 * (hp + 1)],
                                              pa4[:, 512:640],
                                              t_edec[:, h:h + 1], None, op0=ALU.mult))
            self.w(nc.tensor, c_dve)
            self.w(nc.tensor, c_act)
            nc.tensor.matmul(p6[:, 0:256], K["id_bf"][:, :], xr, start=True, stop=True)
            c_pe.inc(nc.tensor.matmul(p6[:, 0:256], yc, xr, start=False, stop=True,
                                      skip_group_check=True))
            for lev in range(1, 7):
                nc.tensor.wait_ge(c_act.h, c_act.val())
                nc.tensor.wait_ge(c_dve.h, c_dve.val())
                nc.tensor.matmul(pa2[:, 256:384], xc, yc, start=True, stop=True)
                if lev < 6:
                    c_pe.inc(nc.tensor.matmul(pa2[:, 384:512], yc, xc, start=True, stop=True,
                                              skip_group_check=True))
                else:
                    c_pe.inc(nc.tensor.matmul(pa2[:, 384:512], yc, xc, start=True, stop=True,
                                              skip_group_check=True))
                self.w(nc.scalar, c_pe, 0)
                c_act.inc(nc.scalar.copy(out=xr, in_=p6[:, 0:256]))
                c_act.inc(nc.scalar.copy(out=yc, in_=pa2[:, 256:384]))
                self.w(nc.vector, c_pe, 0)
                c_dve.inc(nc.vector.tensor_copy(out=xc, in_=pa2[:, 384:512]))
                self.w(nc.tensor, c_act)
                c_pe.inc(nc.tensor.matmul(p6[:, 0:256], yc, xr, start=False, stop=True,
                                          skip_group_check=True))
            self.w(nc.vector, c_pe)
            c_dve.inc(nc.vector.tensor_copy(out=kcd[:, 128 * hp:128 * (hp + 1)], in_=p6[:, 128:256]))
            self.w(nc.tensor, c_dve)
            c_pe.inc(nc.tensor.transpose(pb[:, 384:512], kcd[:, 128 * hp:128 * (hp + 1)],
                                         K["id_bf"][:, :]))
            self.w(nc.scalar, c_pe)
            c_act.inc(nc.scalar.mul(nkcdt[:, 128 * hp:128 * (hp + 1)], pb[:, 384:512], -1.0))
            self.w(nc.tensor, c_act)
            self.w(nc.tensor, c_pool)
            c_pe.inc(nc.tensor.matmul(p6[:, 0:128], nkcdt[:, 128 * hp:128 * (hp + 1)],
                                      stbsl, start=False, stop=True, skip_group_check=True))
            self.w(nc.vector, c_pe)
            c_dve.inc(nc.vector.tensor_copy(out=vnew[:, 128 * hp:128 * (hp + 1)], in_=p6[:, 0:128]))
            self.w(nc.vector, c_act)
            c_dve.inc(nc.vector.tensor_mul(t2f[:, :], pa2[:, 128:256], eT))
            self.w(nc.vector, c_dve)
            c_dve.inc(nc.vector.tensor_mul(intrat[:, 128 * hp:128 * (hp + 1)], t2f[:, :],
                                           K["mask_up01"][:, :]))
            c_dve.inc(nc.vector.tensor_scalar(colbb[:, :], K["ones128_bf"][:, :],
                                              t_egj[:, h:h + 1], None, op0=ALU.mult))
            self.w(nc.tensor, c_dve)
            c_pe.inc(nc.tensor.transpose(pb[:, 128:256], colbb[:, :], K["id_bf"][:, :]))
            self.w(nc.vector, c_pe)
            c_dve.inc(nc.vector.tensor_mul(qeb[:, 128 * hp:128 * (hp + 1)], qsl,
                                           pb[:, 128:256]))
            self.w(nc.tensor, c_dve)
            nc.tensor.wait_ge(c_dve.h, c_dve.val())
            self.w(nc.tensor, c_pool)
            nc.tensor.matmul(p6[:, 384:512], intrat[:, 128 * hp:128 * (hp + 1)],
                             vnew[:, 128 * hp:128 * (hp + 1)], start=True, stop=True)
            c_pe.inc(nc.tensor.matmul(p6[:, 384:512], qeb[:, 128 * hp:128 * (hp + 1)],
                                      stbsl, start=False, stop=True, skip_group_check=True))
            self.w(nc.vector, c_pe)
            c_dve.inc(nc.vector.tensor_mul(xg[:, 128 * h:128 * (h + 1)], p6[:, 384:512],
                                           siluz[:, 128 * h:128 * (h + 1)]))
            pss = pa3[:, 288:416]
            c_pe.inc(nc.tensor.matmul(pss, kdecb[:, 128 * hp:128 * (hp + 1)],
                                      vnew[:, 128 * hp:128 * (hp + 1)], start=True, stop=True))
            self.w(nc.vector, c_pe)
            c_dve.inc(nc.vector.tensor_scalar(stsl, stsl, t_cb[:, h:h + 1], None, op0=ALU.mult))
            self.w(nc.vector, c_dve)
            c_dve.inc(nc.vector.tensor_add(stsl, stsl, pss))
            self.w(nc.gpsimd, c_dve)
            c_pool.inc(nc.gpsimd.tensor_copy(out=stbsl, in_=stsl))
        self.w(nc.scalar, c_dve)
        c_act.inc(nc.scalar.activation(out=siluz[:, :], in_=xg[:, :], func=AF.Square))
        self.w(nc.vector, c_act)
        c_dve.inc(nc.vector.tensor_reduce(
            t_ssq, siluz[:, :].rearrange("p (h u) -> p h u", u=128), axis=AX.X,
            op=ALU.add))
        self.w(nc.vector, c_dve)
        c_dve.inc(nc.vector.tensor_scalar(t_ssq, t_ssq, 1.0 / 128.0, None, op0=ALU.mult))
        self.w(nc.scalar, c_dve)
        c_act.inc(nc.scalar.activation(out=t_rstd, in_=t_ssq, func=AF.Ln,
                                       bias=self.cvals[:, 4:5]))
        self.w(nc.scalar, c_act)
        c_act.inc(nc.scalar.activation(out=t_rstd, in_=t_rstd, func=AF.Exp, scale=-0.5))
        self.w(nc.vector, c_act)
        for h in range(VH):
            c_dve.inc(nc.vector.tensor_scalar(xg[:, 128 * h:128 * (h + 1)],
                                              xg[:, 128 * h:128 * (h + 1)],
                                              t_rstd[:, h:h + 1], None, op0=ALU.mult))
        self.w(nc.vector, c_dve)
        c_dve.inc(nc.vector.tensor_mul(xg[:, :], xg[:, :], self.normw_rep[:, :]))
        for fc in range(32):
            pslot = pa4 if fc % 2 == 0 else pa5
            self.w(nc.tensor, c_dve)
            if fc >= 2:
                self.w(nc.tensor, c_act)
            c_pe.inc(nc.tensor.transpose(pslot[:, 640:768], xg[:, 128 * fc:128 * (fc + 1)],
                                         K["id_bf"][:, :]))
            self.w(nc.scalar, c_pe)
            c_act.inc(nc.scalar.copy(out=nrm[:, fc * HALF + 128 * cpar:fc * HALF + 128 * cpar + 128],
                                     in_=pslot[:, 640:768]))

    def attn_outproj(self, half, L):
        nc = self.nc
        c_pe, c_act, c_dve = self.c_pe, self.c_act, self.c_dve
        WS = self.WS
        nrm = L["nrm"]
        pa0, pa1 = L["pa0"], L["pa1"]
        self.sync_fence()
        nc.sync.wait_ge(c_pe.h, c_pe.val())
        nc.tensor.wait_ge(c_act.h, c_act.val())

        def dpair(jj):
            for par in range(2):
                d = jj * 2 + par
                self.w(nc.sync, c_dve, par - 1)
                self.wdma(self.c_ws[par], WS[:, 4096 + 4096 * par:4096 + 4096 * (par + 1)],
                          "woutt", d * 256, 4096)
            for par in range(2):
                d = jj * 2 + par
                pp = pa0 if par == 0 else pa1
                self.w(nc.tensor, self.c_ws[par])
                for fc in range(32):
                    mm = nc.tensor.matmul(pp[:, 0:HALF],
                                          WS[:, 4096 + 4096 * par + 128 * fc:4096 + 4096 * par + 128 * (fc + 1)],
                                          nrm[:, fc * HALF:(fc + 1) * HALF],
                                          start=(fc == 0), stop=(fc == 31))
                c_pe.inc(mm)
                self.w(nc.vector, c_pe)
                c_dve.inc(nc.vector.tensor_add(
                    self.x_res[:, bass.ds(512 * d + HALF * half, HALF)],
                    self.x_res[:, bass.ds(512 * d + HALF * half, HALF)], pp[:, 0:HALF]))
        InnerLoop(nc, self.tile_ctrs, 8, dpair)



def make_in_maps(inputs):
    blob = prep_weights(inputs)
    consts = prep_consts()
    convw = np.asarray(inputs["L0_attn_conv_weight"], np.float32)
    convw_r = np.ascontiguousarray(
        convw.reshape(CC, 128, KCONV).transpose(1, 0, 2).reshape(128, CC * KCONV))
    smalls = np.zeros((128, 512), np.float32)
    smalls[0, 0:32] = np.asarray(inputs["L0_attn_dt_bias"], np.float32)
    smalls[0, 32:64] = np.asarray(inputs["L0_attn_A_log"], np.float32)
    smalls[0, 64:192] = np.asarray(inputs["L0_attn_norm_w"], np.float32)
    cvals = np.zeros((128, 16), np.float32)
    cvals[:, 0] = LN_EPS
    cvals[:, 1] = L2_EPS
    cvals[:, 2] = 1.0
    cvals[:, 3] = -0.5 * np.log(128.0)
    cvals[:, 4] = RMS_EPS
    in_maps = []
    for core in range(8):
        m = {"xin": prep_core_x(inputs, core),
             "wf32": blob, "convw": convw_r, "smalls": smalls, "cvals": cvals}
        for nm, shp, dt in CONST_SPECS:
            m["c_" + nm] = consts[nm]
        in_maps.append(m)
    return in_maps


def run_dev(inputs, do_attn=True, dev_tiles=None, dev_stage=None):
    b = Builder(do_attn=do_attn, dev_tiles=dev_tiles, dev_stage=dev_stage)
    nc = b.build()
    in_maps = make_in_maps(inputs)
    res = run_bass_kernel_spmd(nc, in_maps, core_ids=list(range(8)))
    return [r["xout"] for r in res.results]


def kernel(**inputs):
    outs = run_dev(inputs, do_attn=True, dev_tiles=None)
    full = np.empty((16, 2048, 2048), np.float32)
    for core in range(8):
        o = outs[core].reshape(NSEQ, NTILE, DC, 128, P)
        xf = o.transpose(0, 2, 3, 1, 4).reshape(NSEQ, D, 2048)
        full[2 * core:2 * core + 2] = xf.transpose(0, 2, 1)
    return full


# version 08 (final)
# speedup vs baseline: 1.2537x; 1.0362x over previous
import contextlib
import numpy as np
import ml_dtypes
import concourse.bass as bass
import concourse.bacc as bacc
import concourse.mybir as mybir
from concourse.bass_utils import run_bass_kernel_spmd

AF = mybir.ActivationFunctionType
ALU = mybir.AluOpType
AX = mybir.AxisListType
F32 = mybir.dt.float32
BF16 = mybir.dt.bfloat16
BF = ml_dtypes.bfloat16

D = 2048
DC = 16
P = 512
NTILE = 4
NSEQ = 2
F = 5632
FC = 44
E = 8
R = 16
CONVD = 8192
CC = 64
KH = 16
VH = 32
A = 128
U = 128
CH = 128
HALF = 256
ZD = 4096
KCONV = 4
LN_EPS = 1e-5
L2_EPS = 1e-6
RMS_EPS = 1e-6

def _blob_layout():
    off = {}
    r = 0
    def add(name, rows):
        nonlocal r
        off[name] = r
        r += rows
    add("w1t", FC * 128)
    add("w3t", FC * 128)
    add("w2t", DC * 352)
    add("qkvt", CC * 128)
    add("wz", DC * 256)
    add("woutt", DC * 256)
    add("wba", 64)
    add("b1p", FC * 64)
    add("b3p", FC * 64)
    add("a2p", FC * 32)
    add("b2p", DC * 64)
    add("a1t", 128)
    add("a3t", 128)
    add("gatet", 8)
    if r % 256:
        add("_pad", 256 - r % 256)
    return off, r

BLOB_OFF, BLOB_ROWS = _blob_layout()

def _tile_lhs(w):
    K_, M_ = w.shape
    return w.reshape(K_ // 128, 128, M_ // 128, 128).transpose(2, 0, 1, 3)


def prep_weights(i):
    blob = np.zeros((BLOB_ROWS, 2048), np.float32)

    def put(name, arr):
        rows = arr.size // 2048
        assert rows * 2048 == arr.size
        blob[BLOB_OFF[name]:BLOB_OFF[name] + rows] = \
            np.ascontiguousarray(arr, np.float32).reshape(rows, 2048)

    w1 = np.asarray(i["L0_ffn_w1"], np.float32)
    w3 = np.asarray(i["L0_ffn_w3"], np.float32)
    w2 = np.asarray(i["L0_ffn_w2"], np.float32)
    put("w1t", np.ascontiguousarray(_tile_lhs(w1.T).transpose(0, 2, 1, 3)))
    put("w3t", np.ascontiguousarray(_tile_lhs(w3.T).transpose(0, 2, 1, 3)))
    w2t = _tile_lhs(w2.T)
    put("w2t", np.ascontiguousarray(w2t.transpose(0, 2, 1, 3)).reshape(DC, 128, FC * 128))
    wqkv = np.asarray(i["L0_attn_W_qkv"], np.float32)
    put("qkvt", np.ascontiguousarray(_tile_lhs(wqkv).transpose(0, 2, 1, 3)))
    wz = np.asarray(i["L0_attn_W_z"], np.float32)
    put("wz", wz.reshape(DC, 128, ZD))
    wout = np.asarray(i["L0_attn_W_out"], np.float32)
    wot = wout.reshape(32, 128, DC, 128).transpose(2, 0, 1, 3)
    put("woutt", np.ascontiguousarray(wot.transpose(0, 2, 1, 3)).reshape(DC, 128, 32 * 128))
    wba = np.concatenate([np.asarray(i["L0_attn_W_b"], np.float32),
                          np.asarray(i["L0_attn_W_a"], np.float32)], axis=1)
    put("wba", wba.reshape(DC, 128, 64))

    def bpad(b):
        out = np.zeros((FC, 128, 1024), np.float32)
        for e in range(E):
            t = b[e].reshape(FC, 128, R).transpose(0, 2, 1)
            out[:, 16 * e:16 * e + 16, 128 * e:128 * (e + 1)] = t
        return out
    put("b1p", bpad(np.asarray(i["L0_ffn_b1"], np.float32)))
    put("b3p", bpad(np.asarray(i["L0_ffn_b3"], np.float32)))
    a2 = np.asarray(i["L0_ffn_a2"], np.float32)
    a2p = np.zeros((FC, 128, 512), np.float32)
    for e in range(E):
        t = a2[e].reshape(R, FC, 128).transpose(1, 2, 0)
        a2p[:, :, 64 * e + 16 * (e % 4):64 * e + 16 * (e % 4) + 16] = t
    put("a2p", a2p)
    b2 = np.asarray(i["L0_ffn_b2"], np.float32)
    b2p = np.zeros((DC, 128, 1024), np.float32)
    for e in range(E):
        rbase = 64 * (e // 4) + 16 * (e % 4)
        t = b2[e].reshape(DC, 128, R).transpose(0, 2, 1)
        b2p[:, rbase:rbase + 16, 128 * e:128 * (e + 1)] = t
    put("b2p", b2p)
    a1 = np.asarray(i["L0_ffn_a1"], np.float32).reshape(E * R, D)
    put("a1t", a1.T.reshape(DC, 128, 128))
    a3 = np.asarray(i["L0_ffn_a3"], np.float32).reshape(E * R, D)
    put("a3t", a3.T.reshape(DC, 128, 128))
    gate = np.asarray(i["L0_ffn_gate_w"], np.float32)
    put("gatet", gate.T.reshape(DC, 128, 8))
    return blob


def prep_consts():
    c = {}
    idx = np.arange(128)
    l, m = np.meshgrid(idx, idx, indexing="ij")
    c["mask_low01"] = (m <= l).astype(np.float32)
    c["mask_up01"] = (m >= l).astype(np.float32)
    c["mask_lowneg"] = -(m < l).astype(np.float32)
    c["mask_upneg"] = -(m > l).astype(np.float32)
    c["id_bf"] = np.eye(128).astype(BF)
    c["id_f32"] = np.eye(128).astype(np.float32)
    c["zeros_bf"] = np.zeros((128, 128), BF)
    c["ones128_bf"] = np.ones((128, 128), BF)
    c["ones128_f32"] = np.ones((128, 128), np.float32)
    o8 = np.zeros((128, 8 * 128), BF)
    for e in range(8):
        o8[e, 128 * e:128 * (e + 1)] = 1.0
    c["onehot8p"] = o8
    return c


CONST_SPECS = [("mask_low01", [128, 128], F32), ("mask_up01", [128, 128], F32),
               ("mask_lowneg", [128, 128], F32), ("mask_upneg", [128, 128], F32),
               ("id_bf", [128, 128], BF16), ("id_f32", [128, 128], F32),
               ("zeros_bf", [128, 128], BF16),
               ("ones128_bf", [128, 128], BF16),
               ("ones128_f32", [128, 128], F32),
               ("onehot8p", [128, 1024], BF16)]


def prep_core_x(inputs, core):
    hid = np.asarray(inputs["L0_attn_hidden"], np.float32)
    xs = hid[2 * core:2 * core + 2]
    xf = xs.transpose(0, 2, 1).reshape(NSEQ, DC, 128, NTILE, P).transpose(0, 3, 1, 2, 4)
    return np.ascontiguousarray(xf, np.float32).reshape(NSEQ * NTILE, DC, 128, P)


class Ctr:
    def __init__(self, nc, name, unit=1):
        self._cm = nc.semaphore(name)
        self.h = self._cm.__enter__()
        self.stat = 0
        self.terms = []
        self.unit = unit

    def inc(self, instr, n=1):
        instr.then_inc(self.h, n * self.unit)
        self.stat += n * self.unit

    def val(self, ofs=0):
        v = self.stat + ofs * self.unit
        for reg, coef in self.terms:
            if coef:
                v = reg * coef + v
        return v

    def is_static(self):
        return not any(coef for _, coef in self.terms)

    def clear(self):
        assert not self.terms
        self.stat = 0


def affine_fori(nc, ctrs, start, n, body, deltas):
    if n <= start:
        return
    entry = [c.stat for c in ctrs]
    with nc.Fori(start, n) as i:
        for c, d in zip(ctrs, deltas):
            c.terms.append([i + (-start), d])
        body(i)
    for c, en, d in zip(ctrs, entry, deltas):
        c.terms.pop()
        assert c.stat - en == d, f"body delta mismatch {c.stat - en} vs {d}"
        c.stat = en + d * (n - start)


class InnerLoop:
    def __init__(self, nc, ctrs, n, body):
        assert n >= 2
        s0 = [c.stat for c in ctrs]
        body(0)
        s1 = [c.stat for c in ctrs]
        body(1)
        deltas = [c.stat - b for c, b in zip(ctrs, s1)]
        for c, b0, b1, d in zip(ctrs, s0, s1, deltas):
            assert b1 - b0 == d, f"nonuniform delta {b1 - b0} vs {d}"
        if n > 2:
            entry = [c.stat for c in ctrs]
            with nc.Fori(2, n) as i:
                for c, d in zip(ctrs, deltas):
                    c.terms.append([i + (-2), d])
                body(i)
            for c, en, d in zip(ctrs, entry, deltas):
                c.terms.pop()
                assert c.stat - en == d, f"body delta changed {c.stat - en} vs {d}"
                c.stat = en + d * (n - 2)


class Builder:
    def __init__(self, do_attn=True, dev_tiles=None, dev_stage=None):
        self.do_attn = do_attn
        self.dev_tiles = dev_tiles
        self.dev_stage = dev_stage
        nc = self.nc = bacc.Bacc("TRN2", target_bir_lowering=False, debug=False,
                                 dynamic_dma_scratch_size=16384)
        self.ctx = []
        self.xin = nc.dram_tensor("xin", [NSEQ * NTILE, DC, 128, P], F32, kind="ExternalInput").ap()
        self.xout = nc.dram_tensor("xout", [NSEQ * NTILE, DC, 128, P], F32, kind="ExternalOutput").ap()
        self.wf32 = nc.dram_tensor("wf32", [BLOB_ROWS, 2048], F32, kind="ExternalInput").ap()
        self.wbf = nc.dram_tensor("wbf", [BLOB_ROWS, 2048], BF16, kind="Internal").ap()
        self.convw_d = nc.dram_tensor("convw", [128, 256], F32, kind="ExternalInput").ap()
        self.smalls_d = nc.dram_tensor("smalls", [128, 512], F32, kind="ExternalInput").ap()
        self.cvals_d = nc.dram_tensor("cvals", [128, 16], F32, kind="ExternalInput").ap()
        self.cst_d = {}
        for nm, shp, dt in CONST_SPECS:
            self.cst_d[nm] = nc.dram_tensor("c_" + nm, shp, dt, kind="ExternalInput").ap()

    def sb(self, name, shape, dtype):
        t = self.nc.sbuf_tensor(name, shape, dtype)
        h = t.__enter__()
        self.ctx.append(t)
        return h

    def sync_fence(self):
        nc = self.nc
        nc.sync.wait_ge(self.c_ws[0].h, self.c_ws[0].val())
        nc.sync.wait_ge(self.c_ws[1].h, self.c_ws[1].val())

    def w(self, eng, ctr, ofs=0):
        v = ctr.val(ofs)
        if isinstance(v, int):
            if v <= 0:
                return
        eng.wait_ge(ctr.h, v)

    def build(self):
        nc = self.nc
        self.x_res = self.sb("x_res", [128, DC * P], F32)
        self.xln = self.sb("xln", [128, DC * P], BF16)
        self.WS = self.sb("WS", [128, 13312], BF16)
        self.st = self.sb("st", [128, VH * 128], F32)
        self.stbf = self.sb("stbf", [128, VH * 128], BF16)
        self.K = {}
        for nm, shp, dt in CONST_SPECS:
            self.K[nm] = self.sb("k_" + nm, shp, dt)
        self.convw = self.sb("convw_sb", [128, 256], F32)
        self.dtb_b = self.sb("dtb_b", [128, 32], F32)
        self.negeA_b = self.sb("negeA_b", [128, 32], F32)
        self.normw_rep = self.sb("normw_rep", [128, ZD], BF16)
        self.gates = self.sb("gates", [128, DC * 8], BF16)
        self.wba = self.sb("wba_sb", [128, DC * 64], BF16)
        self.a1s = self.sb("a1s", [128, DC * 128], BF16)
        self.a3s = self.sb("a3s", [128, DC * 128], BF16)
        self.smalls = self.sb("smalls_sb", [128, 512], F32)
        self.cvals = self.sb("cvals_sb", [128, 16], F32)
        self.ctail = self.sb("ctail", [128, CC * 4], BF16)

        self.c_w = Ctr(nc, "c_w", unit=16)
        self.c_ws = [Ctr(nc, "c_ws0", unit=16), Ctr(nc, "c_ws1", unit=16)]
        self.c_x = Ctr(nc, "c_x", unit=16)
        self.c_s = Ctr(nc, "c_s", unit=16)
        self.c_pe = Ctr(nc, "c_pe")
        self.c_act = Ctr(nc, "c_act")
        self.c_dve = Ctr(nc, "c_dve")
        self.c_pool = Ctr(nc, "c_pool")
        self.tile_ctrs = [self.c_w, self.c_ws[0], self.c_ws[1], self.c_x, self.c_s,
                          self.c_pe, self.c_act, self.c_dve, self.c_pool]

        self.emit_prologue()
        ntiles_total = self.dev_tiles if self.dev_tiles is not None else NSEQ * NTILE
        if ntiles_total == 0:
            nc.finalize()
            return nc
        done = 0
        deltas = None
        for seq in range(NSEQ):
            if done >= ntiles_total:
                break
            self.emit_seq_reset()
            ntile = min(NTILE, ntiles_total - done)
            start = 0
            if deltas is None:
                s0 = [c.stat for c in self.tile_ctrs]
                self.emit_tile(seq, 0)
                deltas = [c.stat - b for c, b in zip(self.tile_ctrs, s0)]
                start = 1
            affine_fori(nc, self.tile_ctrs, start, ntile,
                        lambda t, seq=seq: self.emit_tile(seq, t), deltas)
            done += ntile
        nc.finalize()
        return nc

    def emit_prologue(self):
        nc = self.nc
        c_w, c_pe, c_act, c_dve, c_pool = self.c_w, self.c_pe, self.c_act, self.c_dve, self.c_pool
        for nm, shp, dt in CONST_SPECS:
            c_w.inc(nc.sync.dma_start(out=self.K[nm][:, :], in_=self.cst_d[nm][:, :]))
        c_w.inc(nc.sync.dma_start(out=self.convw[:, :], in_=self.convw_d[:, :]))
        c_w.inc(nc.sync.dma_start(out=self.smalls[:, :], in_=self.smalls_d[:, :]))
        c_w.inc(nc.sync.dma_start(out=self.cvals[:, :], in_=self.cvals_d[:, :]))
        NCAST = BLOB_ROWS // 128
        wr32 = self.wf32.rearrange("(n p) d -> n p d", p=128)
        wrbf = self.wbf.rearrange("(n p) d -> n p d", p=128)
        c_cast = Ctr(nc, "c_cast")
        c_cp = [Ctr(nc, "c_cp0"), Ctr(nc, "c_cp1")]
        c_ld = [Ctr(nc, "c_ld0", unit=16), Ctr(nc, "c_ld1", unit=16)]
        c_st = [Ctr(nc, "c_st0", unit=16), Ctr(nc, "c_st1", unit=16)]
        with nc.sbuf_tensor("cast_in", [128, 2 * 2048], F32) as cin, \
             nc.sbuf_tensor("cast_out", [128, 2 * 2048], BF16) as cout:
            def cast_body(j):
                for par in range(2):
                    self.w(nc.sync, c_cp[par], 0)
                    c_ld[par].inc(nc.sync.dma_start(out=cin[:, par * 2048:(par + 1) * 2048],
                                                    in_=wr32[j * 2 + par, :, :]))
                for par in range(2):
                    self.w(nc.gpsimd, c_ld[par])
                    self.w(nc.gpsimd, c_st[par], 0)
                    c_cp[par].inc(nc.gpsimd.tensor_copy(out=cout[:, par * 2048:(par + 1) * 2048],
                                                        in_=cin[:, par * 2048:(par + 1) * 2048]))
                for par in range(2):
                    self.w(nc.scalar, c_cp[par])
                    c_st[par].inc(nc.scalar.dma_start(out=wrbf[j * 2 + par, :, :],
                                                      in_=cout[:, par * 2048:(par + 1) * 2048]))
            InnerLoop(nc, [c_ld[0], c_ld[1], c_cp[0], c_cp[1], c_st[0], c_st[1]],
                      NCAST // 2, cast_body)
            nc.scalar.wait_ge(c_st[0].h, c_st[0].val())
            nc.scalar.wait_ge(c_st[1].h, c_st[1].val())
            nc.sync.wait_ge(c_ld[0].h, c_ld[0].val())
            nc.sync.wait_ge(c_ld[1].h, c_ld[1].val())
        nc.all_engine_barrier()
        with nc.psum_tensor("p_pro", [128, 512], F32) as pb:
            nc.tensor.wait_ge(c_w.h, c_w.val())
            mm = nc.tensor.matmul(pb[:, 0:192], self.K["ones128_f32"][0:1, :],
                                  self.smalls[0:1, 0:192], start=True, stop=True)
            c_pe.inc(mm)
            self.w(nc.vector, self.c_pe)
            c_dve.inc(nc.vector.tensor_copy(out=self.dtb_b[:, :], in_=pb[:, 0:32]))
            self.w(nc.scalar, self.c_pe)
            c_act.inc(nc.scalar.activation(out=self.negeA_b[:, :], in_=pb[:, 32:64], func=AF.Exp))
            self.w(nc.vector, self.c_act)
            c_dve.inc(nc.vector.tensor_scalar_mul(self.negeA_b[:, :], self.negeA_b[:, :], -1.0))
            self.w(nc.vector, self.c_pe)
            c_dve.inc(nc.vector.tensor_copy(out=self.normw_rep[:, 0:128], in_=pb[:, 64:192]))
            for rep in range(1, VH):
                self.w(nc.gpsimd, self.c_dve)
                c_pool.inc(nc.gpsimd.tensor_copy(out=self.normw_rep[:, 128 * rep:128 * (rep + 1)],
                                                 in_=self.normw_rep[:, 0:128]))
            nc.sync.wait_ge(c_w.h, c_w.val())
            for nm, dst, width in [("gatet", self.gates, 8),
                                   ("a1t", self.a1s, 128),
                                   ("a3t", self.a3s, 128),
                                   ("wba", self.wba, 64)]:
                base = BLOB_OFF[nm] * 2048
                ap = bass.AP(self.wbf.tensor, base,
                             [[width, 128], [128 * width, DC], [1, width]])
                dstv = dst[:, :].rearrange("p (k c) -> p k c", c=width)
                c_w.inc(nc.sync.dma_start(out=dstv, in_=ap))
            nc.sync.wait_ge(c_w.h, c_w.val())
        nc.all_engine_barrier()

    def emit_seq_reset(self):
        nc = self.nc
        nc.vector.memset(self.st[:, :], 0.0)
        nc.vector.memset(self.stbf[:, :], 0.0)
        nc.gpsimd.memset(self.ctail[:, :], 0.0)
        nc.all_engine_barrier()

    def wdma(self, ctr, dst_ap, name, row_off, nfree):
        base = (BLOB_OFF[name] + row_off) * 2048
        ap = bass.AP(self.wbf.tensor, base, [[nfree, 128], [1, nfree]])
        ctr.inc(self.nc.sync.dma_start(out=dst_ap, in_=ap))

    def emit_tile(self, seq, t):
        nc = self.nc
        idx = seq * NTILE + t
        xap = bass.AP(self.xin.tensor, idx * (DC * 128 * P),
                      [[P, 128], [128 * P, DC], [1, P]])
        self.c_x.inc(nc.gpsimd.dma_start(
            out=self.x_res[:, :].rearrange("p (k c) -> p k c", c=P), in_=xap))
        nc.gpsimd.wait_ge(self.c_x.h, self.c_x.val())
        nc.all_engine_barrier()
        self.emit_ffn()
        nc.all_engine_barrier()
        if self.do_attn:
            self.emit_ln_standalone()
            nc.all_engine_barrier()
            self.emit_attn()
            nc.all_engine_barrier()
        if not (self.dev_stage or "").startswith("attn"):
            self.emit_ffn()
            nc.all_engine_barrier()
        yap = bass.AP(self.xout.tensor, idx * (DC * 128 * P),
                      [[P, 128], [128 * P, DC], [1, P]])
        self.c_x.inc(nc.gpsimd.dma_start(
            out=yap, in_=self.x_res[:, :].rearrange("p (k c) -> p k c", c=P)))
        nc.gpsimd.wait_ge(self.c_x.h, self.c_x.val())
        nc.sync.wait_ge(self.c_w.h, self.c_w.val())
        nc.sync.wait_ge(self.c_ws[0].h, self.c_ws[0].val())
        nc.sync.wait_ge(self.c_ws[1].h, self.c_ws[1].val())
        nc.scalar.wait_ge(self.c_s.h, self.c_s.val())
        nc.all_engine_barrier()

    def emit_ln_standalone(self):
        nc = self.nc
        K = self.K
        c_pe, c_act, c_dve, c_pool = self.c_pe, self.c_act, self.c_dve, self.c_pool
        self._ln_n = getattr(self, "_ln_n", 0) + 1
        lsx = f"_l{self._ln_n}"
        with nc.sbuf_tensor("l_sc" + lsx, [128, 2 * 512], F32) as sc32, \
             nc.sbuf_tensor("l_rows" + lsx, [1, 2048], F32) as rows, \
             nc.sbuf_tensor("l_sq" + lsx, [128, 2 * 512], BF16) as sqb, \
             nc.psum_tensor("lp6" + lsx, [128, 512], F32) as p6, \
             nc.psum_tensor("lp7" + lsx, [128, 512], F32) as p7:
            xr = self.x_res
            xln = self.xln
            for k in range(DC):
                c_pool.inc(nc.gpsimd.tensor_copy(out=xln[:, 512 * k:512 * (k + 1)],
                                                 in_=xr[:, 512 * k:512 * (k + 1)]))
            pe_sq0 = c_pe.val(0)
            for k in range(DC):
                self.w(nc.scalar, c_pool, -(DC - 1 - k))
                if k >= 2:
                    nc.scalar.wait_ge(c_pe.h, pe_sq0 + (k - 1))
                c_act.inc(nc.scalar.activation(out=sqb[:, 512 * (k % 2):512 * (k % 2 + 1)],
                                               in_=xln[:, 512 * k:512 * (k + 1)], func=AF.Square))
                self.w(nc.tensor, c_pool, -(DC - 1 - k))
                nc.tensor.matmul(p6[0:1, :], K["ones128_bf"][:, 0:1], xln[:, 512 * k:512 * (k + 1)],
                                 start=(k == 0), stop=(k == DC - 1))
                self.w(nc.tensor, c_act)
                c_pe.inc(nc.tensor.matmul(p7[0:1, :], K["ones128_bf"][:, 0:1],
                                          sqb[:, 512 * (k % 2):512 * (k % 2 + 1)],
                                          start=(k == 0), stop=(k == DC - 1)))
            self.w(nc.scalar, c_pe)
            c_act.inc(nc.scalar.mul(rows[0:1, 0:512], p6[0:1, :], 1.0 / D))
            c_act.inc(nc.scalar.mul(rows[0:1, 512:1024], p7[0:1, :], 1.0 / D))
            self.w(nc.vector, c_act)
            c_dve.inc(nc.vector.tensor_mul(rows[0:1, 1024:1536], rows[0:1, 0:512], rows[0:1, 0:512]))
            self.w(nc.vector, c_dve)
            c_dve.inc(nc.vector.tensor_sub(rows[0:1, 512:1024], rows[0:1, 512:1024], rows[0:1, 1024:1536]))
            self.w(nc.scalar, c_dve)
            c_act.inc(nc.scalar.activation(out=rows[0:1, 1024:1536], in_=rows[0:1, 512:1024],
                                           func=AF.Ln, bias=self.cvals[0:1, 0:1]))
            self.w(nc.scalar, c_act)
            c_act.inc(nc.scalar.activation(out=rows[0:1, 512:1024], in_=rows[0:1, 1024:1536],
                                           func=AF.Exp, scale=-0.5))
            self.w(nc.vector, c_act)
            c_dve.inc(nc.vector.tensor_mul(rows[0:1, 1024:1536], rows[0:1, 0:512], rows[0:1, 512:1024]))
            self.w(nc.tensor, c_dve)
            nc.tensor.matmul(p6[:, :], K["ones128_f32"][0:1, :], rows[0:1, 512:1024], start=True, stop=True)
            c_pe.inc(nc.tensor.matmul(p7[:, :], K["ones128_f32"][0:1, :], rows[0:1, 1024:1536],
                                      start=True, stop=True))
            self.w(nc.scalar, c_pe)
            c_act.inc(nc.scalar.copy(out=sc32[:, 0:512], in_=p6[:, :]))
            c_act.inc(nc.scalar.copy(out=sc32[:, 512:1024], in_=p7[:, :]))
            for k in range(DC):
                self.w(nc.gpsimd, c_act)
                c_pool.inc(nc.gpsimd.tensor_mul(xln[:, 512 * k:512 * (k + 1)],
                                                xr[:, 512 * k:512 * (k + 1)], sc32[:, 0:512]))
                self.w(nc.vector, c_pool)
                c_dve.inc(nc.vector.tensor_sub(xln[:, 512 * k:512 * (k + 1)],
                                               xln[:, 512 * k:512 * (k + 1)], sc32[:, 512:1024]))
            nc.vector.wait_ge(c_dve.h, c_dve.val())

    def emit_ffn(self):
        nc = self.nc
        K = self.K
        self._ffn_n = getattr(self, "_ffn_n", 0) + 1
        sfx = f"_{self._ffn_n}"
        c_w, c_pe, c_act, c_dve, c_pool = self.c_w, self.c_pe, self.c_act, self.c_dve, self.c_pool
        with nc.sbuf_tensor("f_sc" + sfx, [128, 3 * 512], F32) as sc32, \
             nc.sbuf_tensor("f_rows" + sfx, [1, 2048], F32) as rows, \
             nc.sbuf_tensor("f_sq" + sfx, [128, 2 * 512], BF16) as sqb, \
             nc.sbuf_tensor("f_base" + sfx, [128, 2 * 512], BF16) as baseb, \
             nc.sbuf_tensor("f_sg" + sfx, [128, 6 * 512], BF16) as sgb, \
             nc.sbuf_tensor("f_t1" + sfx, [128, 2 * 512], BF16) as t1b, \
             nc.sbuf_tensor("f_t2" + sfx, [128, 512], BF16) as t2ev, \
             nc.sbuf_tensor("f_wb" + sfx, [128, 8 * 512], BF16) as wb8, \
             nc.sbuf_tensor("f_wt" + sfx, [32, 512], BF16) as wT32, \
             nc.sbuf_tensor("f_wfl" + sfx, [128, 64], F32) as wfl, \
             nc.sbuf_tensor("f_sact" + sfx, [128, FC * 512], BF16) as sact, \
             nc.psum_tensor("fp0" + sfx, [128, 512], F32) as p0, \
             nc.psum_tensor("fp1" + sfx, [128, 512], F32) as p1, \
             nc.psum_tensor("fp2" + sfx, [128, 512], F32) as p2, \
             nc.psum_tensor("fp3" + sfx, [128, 512], F32) as p3, \
             nc.psum_tensor("fp4" + sfx, [128, 512], F32) as p4, \
             nc.psum_tensor("fp5" + sfx, [128, 512], F32) as p5, \
             nc.psum_tensor("fp6" + sfx, [128, 512], F32) as p6, \
             nc.psum_tensor("fp7" + sfx, [128, 512], F32) as p7:
            xr = self.x_res
            xln = self.xln
            for k in range(DC):
                c_pool.inc(nc.gpsimd.tensor_copy(out=xln[:, 512 * k:512 * (k + 1)],
                                                 in_=xr[:, 512 * k:512 * (k + 1)]))
            pe_sq0 = c_pe.val(0)
            for k in range(DC):
                self.w(nc.scalar, c_pool, -(DC - 1 - k))
                if k >= 2:
                    nc.scalar.wait_ge(c_pe.h, pe_sq0 + (k - 1))
                c_act.inc(nc.scalar.activation(out=sqb[:, 512 * (k % 2):512 * (k % 2 + 1)],
                                               in_=xln[:, 512 * k:512 * (k + 1)], func=AF.Square))
                self.w(nc.tensor, c_pool, -(DC - 1 - k))
                nc.tensor.matmul(p6[0:1, :], K["ones128_bf"][:, 0:1], xln[:, 512 * k:512 * (k + 1)],
                                 start=(k == 0), stop=(k == DC - 1))
                self.w(nc.tensor, c_act)
                c_pe.inc(nc.tensor.matmul(p7[0:1, :], K["ones128_bf"][:, 0:1],
                                          sqb[:, 512 * (k % 2):512 * (k % 2 + 1)],
                                          start=(k == 0), stop=(k == DC - 1)))
            self.w(nc.scalar, c_pe)
            c_act.inc(nc.scalar.mul(rows[0:1, 0:512], p6[0:1, :], 1.0 / D))
            c_act.inc(nc.scalar.mul(rows[0:1, 512:1024], p7[0:1, :], 1.0 / D))
            self.w(nc.vector, c_act)
            c_dve.inc(nc.vector.tensor_mul(rows[0:1, 1024:1536], rows[0:1, 0:512], rows[0:1, 0:512]))
            self.w(nc.vector, c_dve)
            c_dve.inc(nc.vector.tensor_sub(rows[0:1, 512:1024], rows[0:1, 512:1024], rows[0:1, 1024:1536]))
            self.w(nc.scalar, c_dve)
            c_act.inc(nc.scalar.activation(out=rows[0:1, 1024:1536], in_=rows[0:1, 512:1024],
                                           func=AF.Ln, bias=self.cvals[0:1, 0:1]))
            self.w(nc.scalar, c_act)
            c_act.inc(nc.scalar.activation(out=rows[0:1, 512:1024], in_=rows[0:1, 1024:1536],
                                           func=AF.Exp, scale=-0.5))
            self.w(nc.vector, c_act)
            c_dve.inc(nc.vector.tensor_mul(rows[0:1, 1024:1536], rows[0:1, 0:512], rows[0:1, 512:1024]))
            self.w(nc.tensor, c_dve)
            nc.tensor.matmul(p6[:, :], K["ones128_f32"][0:1, :], rows[0:1, 512:1024], start=True, stop=True)
            c_pe.inc(nc.tensor.matmul(p7[:, :], K["ones128_f32"][0:1, :], rows[0:1, 1024:1536],
                                      start=True, stop=True))
            self.w(nc.scalar, c_pe)
            c_act.inc(nc.scalar.copy(out=sc32[:, 0:512], in_=p6[:, :]))
            c_act.inc(nc.scalar.copy(out=sc32[:, 512:1024], in_=p7[:, :]))
            for k in range(DC):
                self.w(nc.gpsimd, c_act)
                c_pool.inc(nc.gpsimd.tensor_mul(xln[:, 512 * k:512 * (k + 1)],
                                                xr[:, 512 * k:512 * (k + 1)], sc32[:, 0:512]))
                self.w(nc.vector, c_pool)
                c_dve.inc(nc.vector.tensor_sub(xln[:, 512 * k:512 * (k + 1)],
                                               xln[:, 512 * k:512 * (k + 1)], sc32[:, 512:1024]))
            ln_dve_done = c_dve.val()
            if self.dev_stage == "ln":
                return
            nc.tensor.wait_ge(c_dve.h, ln_dve_done)
            for g in range(4):
                for k in range(DC):
                    mm = nc.tensor.matmul(p0[:, 8 * g:8 * (g + 1)],
                                          xln[:, 512 * k + 128 * g:512 * k + 128 * g + 128],
                                          self.gates[:, 8 * k:8 * (k + 1)],
                                          start=(k == 0), stop=(k == DC - 1))
            c_pe.inc(mm)
            self.w(nc.vector, c_pe)
            for g in range(4):
                lg = p0[:, 8 * g:8 * (g + 1)]
                m1 = sc32[:, 1024 + 16 * g:1024 + 16 * g + 1]
                m2 = sc32[:, 1024 + 16 * g + 1:1024 + 16 * g + 2]
                w1c = sc32[:, 1024 + 16 * g + 2:1024 + 16 * g + 3]
                w2c = sc32[:, 1024 + 16 * g + 3:1024 + 16 * g + 4]
                eq1 = sc32[:, 1024 + 16 * g + 4:1024 + 16 * g + 12]
                l2v = wfl[:, 16 * g:16 * g + 8]
                eq2 = wfl[:, 16 * g + 8:16 * g + 16]
                c_dve.inc(nc.vector.reduce_max(m1, lg, axis=AX.X))
                self.w(nc.vector, c_dve)
                c_dve.inc(nc.vector.tensor_scalar(eq1, lg, m1, None, op0=ALU.is_equal))
                self.w(nc.vector, c_dve)
                c_dve.inc(nc.vector.scalar_tensor_tensor(out=l2v, in0=eq1, scalar=-1e30, in1=lg,
                                                         op0=ALU.mult, op1=ALU.add))
                self.w(nc.vector, c_dve)
                c_dve.inc(nc.vector.reduce_max(m2, l2v, axis=AX.X))
                self.w(nc.vector, c_dve)
                c_dve.inc(nc.vector.tensor_sub(w1c, m1, m2))
                self.w(nc.scalar, c_dve)
                c_act.inc(nc.scalar.activation(out=w1c, in_=w1c, func=AF.Sigmoid))
                self.w(nc.vector, c_act)
                c_dve.inc(nc.vector.tensor_scalar(w2c, w1c, -1.0, 1.0, op0=ALU.mult, op1=ALU.add))
                c_dve.inc(nc.vector.tensor_scalar(eq2, l2v, m2, None, op0=ALU.is_equal))
                self.w(nc.vector, c_dve)
                c_dve.inc(nc.vector.tensor_scalar(eq1, eq1, w1c, None, op0=ALU.mult))
                self.w(nc.vector, c_dve)
                c_dve.inc(nc.vector.tensor_scalar(eq2, eq2, w2c, None, op0=ALU.mult))
                self.w(nc.vector, c_dve)
                c_dve.inc(nc.vector.tensor_add(sgb[:, 128 * g:128 * g + 8], eq1, eq2))
            c_pool.inc(nc.gpsimd.memset(wT32[:, :], 0.0))
            self.w(nc.tensor, c_dve)
            for g in range(4):
                mm = nc.tensor.matmul(p1[0:8, 128 * g:128 * (g + 1)],
                                      sgb[:, 128 * g:128 * g + 8], K["id_bf"][:, :],
                                      start=True, stop=True)
            c_pe.inc(mm)
            self.w(nc.scalar, c_pe)
            self.w(nc.scalar, c_pool)
            c_act.inc(nc.scalar.copy(out=wT32[0:8, :], in_=p1[0:8, :]))
            for e in range(8):
                pp = p2 if e % 2 == 0 else p3
                self.w(nc.tensor, c_act)
                c_pe.inc(nc.tensor.matmul(pp[:, :], K["onehot8p"][0:32, 128 * e:128 * (e + 1)],
                                          wT32[:, :], start=True, stop=True))
                self.w(nc.scalar, c_pe)
                c_act.inc(nc.scalar.mul(wb8[:, 512 * e:512 * (e + 1)], pp[:, :], 0.5))
            if self.dev_stage == "gate":
                return
            for k in range(DC):
                nc.tensor.matmul(p4[:, :], self.a1s[:, 128 * k:128 * (k + 1)],
                                 xln[:, 512 * k:512 * (k + 1)], start=(k == 0), stop=(k == DC - 1))
                mm = nc.tensor.matmul(p5[:, :], self.a3s[:, 128 * k:128 * (k + 1)],
                                      xln[:, 512 * k:512 * (k + 1)], start=(k == 0), stop=(k == DC - 1))
            c_pe.inc(mm)
            self.w(nc.scalar, c_pe)
            c_act.inc(nc.scalar.mul(t1b[:, 0:512], p4[:, :], 2.0))
            c_act.inc(nc.scalar.mul(t1b[:, 512:1024], p5[:, :], 2.0))
            self.w(nc.tensor, c_act)
            c_pe.inc(nc.tensor.matmul(p5[:, :], K["zeros_bf"][:, :], t1b[:, 0:512],
                                      start=True, stop=True))
            if self.dev_stage == "t1":
                return
            WS = self.WS
            SLOT = 6656
            ACT_PER_F = 19

            def fpair(j):
                for par in range(2):
                    f = j * 2 + par
                    self.w(nc.sync, c_pe, -26 * (1 - par))
                    self.wdma(self.c_ws[par], WS[:, SLOT * par:SLOT * par + 2048], "w1t", f * 128, 2048)
                    self.wdma(self.c_ws[par], WS[:, SLOT * par + 2048:SLOT * par + 4096], "w3t", f * 128, 2048)
                    self.wdma(self.c_ws[par], WS[:, SLOT * par + 4096:SLOT * par + 5120], "b1p", f * 64, 1024)
                    self.wdma(self.c_ws[par], WS[:, SLOT * par + 5120:SLOT * par + 6144], "b3p", f * 64, 1024)
                    self.wdma(self.c_ws[par], WS[:, SLOT * par + 6144:SLOT * par + 6656], "a2p", f * 32, 512)
                for par in range(2):
                    f = j * 2 + par
                    so = SLOT * par
                    bo = 512 * par
                    self.w(nc.tensor, self.c_ws[par])
                    for k in range(DC):
                        mm = nc.tensor.matmul(p0[:, :], WS[:, so + 128 * k:so + 128 * (k + 1)],
                                              xln[:, 512 * k:512 * (k + 1)],
                                              start=(k == 0), stop=(k == DC - 1))
                    c_pe.inc(mm)
                    for k in range(DC):
                        mm = nc.tensor.matmul(p1[:, :], WS[:, so + 2048 + 128 * k:so + 2048 + 128 * (k + 1)],
                                              xln[:, 512 * k:512 * (k + 1)],
                                              start=(k == 0), stop=(k == DC - 1))
                    c_pe.inc(mm)
                    self.w(nc.scalar, c_pe, -1)
                    c_act.inc(nc.scalar.copy(out=baseb[:, bo:bo + 512], in_=p0[:, :]))
                    self.w(nc.scalar, c_pe)
                    c_act.inc(nc.scalar.copy(out=sgb[:, 2048 + bo:2048 + bo + 512], in_=p1[:, :]))
                    def sact_t2(ep_, e_):
                        self.w(nc.tensor, c_dve)
                        nc.tensor.matmul(p4[:, :], K["id_bf"][:, :], sgb[:, 512 * ep_:512 * (ep_ + 1)],
                                         start=(e_ == 0), stop=(e_ == 7))
                        c_pe.inc(nc.tensor.matmul(p5[64 * (e_ // 4):64 * (e_ // 4) + 64, :],
                                                  WS[:, so + 6144 + 64 * e_:so + 6144 + 64 * (e_ + 1)],
                                                  sgb[:, 512 * ep_:512 * (ep_ + 1)], start=False,
                                                  stop=False, skip_group_check=True))
                    for e in range(8):
                        ep = e % 2
                        hp1 = p2 if ep == 0 else p3
                        hp3 = p6 if ep == 0 else p7
                        if e == 0:
                            self.w(nc.tensor, c_act)
                            self.w(nc.tensor, c_dve)
                        elif e >= 2:
                            self.w(nc.tensor, c_act, -1)
                            self.w(nc.tensor, c_dve, -7)
                        nc.tensor.matmul(hp1[:, :],
                                         WS[:, so + 4096 + 128 * e:so + 4096 + 128 * (e + 1)],
                                         t1b[:, 0:512], start=True, stop=False)
                        c_pe.inc(nc.tensor.matmul(hp1[:, :], K["id_bf"][:, :],
                                                  baseb[:, bo:bo + 512], start=False, stop=True))
                        nc.tensor.matmul(hp3[:, :],
                                         WS[:, so + 5120 + 128 * e:so + 5120 + 128 * (e + 1)],
                                         t1b[:, 512:1024], start=True, stop=False)
                        c_pe.inc(nc.tensor.matmul(hp3[:, :], K["id_bf"][:, :],
                                                  sgb[:, 2048 + bo:2048 + bo + 512], start=False, stop=True))
                        if e >= 1:
                            sact_t2(1 - ep, e - 1)
                        nc.scalar.wait_ge(c_pe.h, c_pe.val(-2 if e >= 1 else -1))
                        c_act.inc(nc.scalar.activation(out=sgb[:, 1024 + 512 * ep:1024 + 512 * (ep + 1)],
                                                       in_=hp1[:, :], func=AF.Silu))
                        nc.vector.wait_ge(c_pe.h, c_pe.val(-1 if e >= 1 else 0))
                        c_dve.inc(nc.vector.tensor_copy(out=sgb[:, 512 * ep:512 * (ep + 1)], in_=hp3[:, :]))
                        self.w(nc.vector, c_dve)
                        c_dve.inc(nc.vector.tensor_mul(sgb[:, 512 * ep:512 * (ep + 1)],
                                                       sgb[:, 512 * ep:512 * (ep + 1)],
                                                       wb8[:, 512 * e:512 * (e + 1)]))
                        self.w(nc.vector, c_dve)
                        self.w(nc.vector, c_act)
                        c_dve.inc(nc.vector.tensor_mul(sgb[:, 512 * ep:512 * (ep + 1)],
                                                       sgb[:, 512 * ep:512 * (ep + 1)],
                                                       sgb[:, 1024 + 512 * ep:1024 + 512 * (ep + 1)]))
                    sact_t2(1, 7)
                    self.w(nc.scalar, c_pe)
                    c_act.inc(nc.scalar.copy(out=sact[:, bass.ds(512 * f, 512)], in_=p4[:, :]))

            self.sync_fence()
            InnerLoop(nc, self.tile_ctrs, FC // 2, fpair)
            if self.dev_stage == "floop":
                return
            self.w(nc.scalar, c_pe)
            c_act.inc(nc.scalar.mul(t2ev[:, :], p5[:, :], 2.0))
            if self.dev_stage == "t2ev":
                return
            DSLOT = 6656
            self.sync_fence()
            nc.tensor.wait_ge(c_act.h, c_act.val())
            nc.sync.wait_ge(c_pe.h, c_pe.val())

            def dpair(j):
                for par in range(2):
                    d = j * 2 + par
                    self.w(nc.sync, c_dve, par - 1)
                    self.wdma(self.c_ws[par], WS[:, DSLOT * par:DSLOT * par + 5632], "w2t", d * 352, 5632)
                    self.wdma(self.c_ws[par], WS[:, DSLOT * par + 5632:DSLOT * par + 6656], "b2p", d * 64, 1024)
                for par in range(2):
                    d = j * 2 + par
                    so = DSLOT * par
                    pp = p0 if par == 0 else p1
                    self.w(nc.tensor, self.c_ws[par])
                    for f in range(FC):
                        nc.tensor.matmul(pp[:, :], WS[:, so + 128 * f:so + 128 * (f + 1)],
                                         sact[:, 512 * f:512 * (f + 1)],
                                         start=(f == 0), stop=False)
                    for e in range(8):
                        mm = nc.tensor.matmul(pp[:, :],
                                              WS[:, so + 5632 + 128 * e:so + 5632 + 128 * (e + 1)],
                                              t2ev[:, :], start=False, stop=(e == 7))
                    c_pe.inc(mm)
                    self.w(nc.vector, c_pe)
                    c_dve.inc(nc.vector.tensor_add(self.x_res[:, bass.ds(512 * d, 512)],
                                                   self.x_res[:, bass.ds(512 * d, 512)], pp[:, :]))
            InnerLoop(nc, self.tile_ctrs, DC // 2, dpair)
            nc.vector.wait_ge(c_dve.h, c_dve.val())

    def emit_attn(self):
        nc = self.nc
        K = self.K
        c_pe, c_act, c_dve, c_pool = self.c_pe, self.c_act, self.c_dve, self.c_pool
        self._attn_n = getattr(self, "_attn_n", 0) + 1
        sx = f"_a{self._attn_n}"
        WS = self.WS
        _es = contextlib.ExitStack()
        q_sb = _es.enter_context(nc.sbuf_tensor("q_sb" + sx, [128, KH * HALF], BF16))
        k_sb = _es.enter_context(nc.sbuf_tensor("k_sb" + sx, [128, KH * HALF], BF16))
        v_sb = _es.enter_context(nc.sbuf_tensor("v_sb" + sx, [128, VH * HALF], BF16))
        mp = _es.enter_context(nc.sbuf_tensor("mp" + sx, [128, 2 * 262], BF16))
        cvt = _es.enter_context(nc.sbuf_tensor("cvt" + sx, [128, 2 * 256], BF16))
        qsr = _es.enter_context(nc.sbuf_tensor("qsr" + sx, [128, 2 * 256], BF16))
        rrow = _es.enter_context(nc.sbuf_tensor("rrow" + sx, [1, 2 * 256], BF16))
        rrf = _es.enter_context(nc.sbuf_tensor("rrf" + sx, [1, 2 * 256], F32))
        siluz = _es.enter_context(nc.sbuf_tensor("siluz" + sx, [128, ZD], BF16))
        xg = _es.enter_context(nc.sbuf_tensor("xg" + sx, [128, ZD], BF16))
        nrm = _es.enter_context(nc.sbuf_tensor("nrm" + sx, [128, 32 * HALF], BF16))
        tmp8 = _es.enter_context(nc.sbuf_tensor("tmp8" + sx, [128, 10 * 32], F32))
        gjfm = _es.enter_context(nc.sbuf_tensor("gjfm" + sx, [32, 128], F32))
        colb = _es.enter_context(nc.sbuf_tensor("colb" + sx, [128, 128], F32))
        colbb = _es.enter_context(nc.sbuf_tensor("colbb" + sx, [128, 128], BF16))
        tsc = _es.enter_context(nc.sbuf_tensor("tsc" + sx, [128, 128], F32))
        eA = _es.enter_context(nc.sbuf_tensor("eA" + sx, [128, 128], F32))
        eTb = _es.enter_context(nc.sbuf_tensor("eT" + sx, [128, 2 * 128], F32))
        gmlb = _es.enter_context(nc.sbuf_tensor("gml" + sx, [128, 2 * 128], F32))
        gmub = _es.enter_context(nc.sbuf_tensor("gmu" + sx, [128, 2 * 128], F32))
        t2f = _es.enter_context(nc.sbuf_tensor("t2f" + sx, [128, 128], F32))
        xcur = _es.enter_context(nc.sbuf_tensor("xcur" + sx, [128, 2 * 128], BF16))
        ycur = _es.enter_context(nc.sbuf_tensor("ycur" + sx, [128, 2 * 128], BF16))
        xrh = _es.enter_context(nc.sbuf_tensor("xrh" + sx, [128, 2 * 256], BF16))
        kcd = _es.enter_context(nc.sbuf_tensor("kcd" + sx, [128, 2 * 128], BF16))
        nkcdt = _es.enter_context(nc.sbuf_tensor("nkcdt" + sx, [128, 2 * 128], BF16))
        vnew = _es.enter_context(nc.sbuf_tensor("vnew" + sx, [128, 2 * 128], BF16))
        intrat = _es.enter_context(nc.sbuf_tensor("intrat" + sx, [128, 2 * 128], BF16))
        qeb = _es.enter_context(nc.sbuf_tensor("qe" + sx, [128, 2 * 128], BF16))
        kdecb = _es.enter_context(nc.sbuf_tensor("kdec" + sx, [128, 2 * 128], BF16))
        pa0 = _es.enter_context(nc.psum_tensor("pa0" + sx, [128, 512], F32))
        pa1 = _es.enter_context(nc.psum_tensor("pa1" + sx, [128, 512], F32))
        pa2 = _es.enter_context(nc.psum_tensor("pa2" + sx, [128, 512], F32))
        pa3 = _es.enter_context(nc.psum_tensor("pa3" + sx, [128, 512], F32))
        pa4 = _es.enter_context(nc.psum_tensor("pa4" + sx, [128, 1024], BF16))
        pa5 = _es.enter_context(nc.psum_tensor("pa5" + sx, [128, 1024], BF16))
        pa6 = _es.enter_context(nc.psum_tensor("pa6" + sx, [128, 512], F32))
        pa7 = _es.enter_context(nc.psum_tensor("pa7" + sx, [128, 512], F32))
        if True:
            t_beta = tmp8[:, 0:32]
            t_gj = tmp8[:, 32:64]
            t_egj = tmp8[:, 64:96]
            t_edec = tmp8[:, 96:128]
            t_begj = tmp8[:, 128:160]
            t_cb = tmp8[:, 160:192]
            t_gs = tmp8[:, 192:224]
            env = dict(locals())
            ds_ = self.dev_stage or ""
            for half in range(2):
                self.attn_frontend(half, env)
                if ds_ == "attn_fe" and half == 0:
                    self._dump([(q_sb, 0, KH * HALF), (k_sb, KH * HALF, KH * HALF)])
                    break
                if ds_ in ("attn_v", "attn_v_noconv") and half == 0:
                    self._dump([(v_sb, 0, VH * HALF)])
                    break
                stop = False
                for cpar in range(2):
                    self.attn_chunk(half, cpar, env)
                    if ds_ in ("attn_z", "attn_core") and half == 0 and cpar == 0:
                        if ds_ == "attn_z":
                            self._dump([(siluz, 0, ZD), (tmp8, ZD, 320)])
                        else:
                            self._dump([(xg, 0, ZD), (nrm, ZD, 4096)])
                        stop = True
                        break
                if stop:
                    break
                self.attn_outproj(half, env)
        _es.close()

    def _dump(self, items):
        nc = self.nc
        nc.all_engine_barrier()
        for src, off, width in items:
            nc.vector.tensor_copy(out=self.x_res[:, off:off + width], in_=src[:, 0:width])
        nc.all_engine_barrier()

    def attn_frontend(self, half, L):
        nc = self.nc
        K = self.K
        c_pe, c_act, c_dve, c_pool = self.c_pe, self.c_act, self.c_dve, self.c_pool
        WS = self.WS
        q_sb, k_sb, v_sb = L["q_sb"], L["k_sb"], L["v_sb"]
        mp, cvt, qsr, rrow, rrf = L["mp"], L["cvt"], L["qsr"], L["rrow"], L["rrf"]
        pa0, pa1, pa2 = L["pa0"], L["pa1"], L["pa2"]
        xln = self.xln
        tok0 = half * HALF

        self.sync_fence()
        nc.tensor.wait_ge(c_dve.h, c_dve.val())

        def cpair(kind, cbase, dst):
            def front(j, par):
                c = cbase + j * 2 + par
                pp = pa0 if par == 0 else pa1
                self.w(nc.tensor, self.c_ws[par])
                apc = 5 if kind != "v" else 2
                self.w(nc.tensor, c_act, 1 - 2 * apc)
                for k in range(DC):
                    mm = nc.tensor.matmul(
                        pp[:, 0:HALF], WS[:, 2048 * par + 128 * k:2048 * par + 128 * (k + 1)],
                        xln[:, bass.ds(512 * k + tok0, HALF)],
                        start=(k == 0), stop=(k == DC - 1))
                c_pe.inc(mm)
                self.w(nc.scalar, c_pe)
                c_act.inc(nc.scalar.copy(out=mp[:, 262 * par + 3:262 * par + 259], in_=pp[:, 0:HALF]))
                self.w(nc.gpsimd, c_act)
                c_pool.inc(nc.gpsimd.tensor_copy(out=mp[:, 262 * par:262 * par + 3],
                                                 in_=self.ctail[:, bass.ds(4 * c, 3)]))
                self.w(nc.vector, c_pool)
                c_dve.inc(nc.vector.tensor_scalar(cvt[:, 256 * par:256 * (par + 1)],
                                                  mp[:, 262 * par:262 * par + 256],
                                                  self.convw[:, bass.ds(4 * c, 1)], None, op0=ALU.mult))
                for kk in range(1, KCONV):
                    self.w(nc.vector, c_dve)
                    c_dve.inc(nc.vector.scalar_tensor_tensor(
                        out=cvt[:, 256 * par:256 * (par + 1)],
                        in0=mp[:, 262 * par + kk:262 * par + kk + 256],
                        scalar=self.convw[:, bass.ds(4 * c + kk, 1)],
                        in1=cvt[:, 256 * par:256 * (par + 1)], op0=ALU.mult, op1=ALU.add))
                self.w(nc.gpsimd, c_dve)
                c_pool.inc(nc.gpsimd.tensor_copy(out=self.ctail[:, bass.ds(4 * c, 3)],
                                                 in_=mp[:, 262 * par + 256:262 * par + 259]))
                if kind == "v":
                    self.w(nc.scalar, c_dve)
                    c_act.inc(nc.scalar.activation(
                        out=dst[:, bass.ds((c - cbase) * HALF, HALF)],
                        in_=cvt[:, 256 * par:256 * (par + 1)], func=AF.Silu))
                else:
                    self.w(nc.scalar, c_dve)
                    c_act.inc(nc.scalar.activation(out=qsr[:, 256 * par:256 * (par + 1)],
                                                   in_=cvt[:, 256 * par:256 * (par + 1)], func=AF.Silu))
                    self.w(nc.scalar, c_act)
                    c_act.inc(nc.scalar.activation(out=cvt[:, 256 * par:256 * (par + 1)],
                                                   in_=qsr[:, 256 * par:256 * (par + 1)], func=AF.Square))

            def tail(j, par):
                c = cbase + j * 2 + par
                self.w(nc.tensor, c_act, 0 if par == 1 else -3)
                self.w(nc.tensor, c_dve)
                c_pe.inc(nc.tensor.matmul(pa2[0:1, bass.ds(256 * par, 256)],
                                          K["ones128_bf"][:, 0:1],
                                          cvt[:, 256 * par:256 * (par + 1)], start=True, stop=True))
                self.w(nc.scalar, c_pe)
                c_act.inc(nc.scalar.activation(out=rrf[0:1, 256 * par:256 * (par + 1)],
                                               in_=pa2[0:1, bass.ds(256 * par, 256)],
                                               func=AF.Ln, bias=self.cvals[0:1, 1:2]))
                self.w(nc.scalar, c_act)
                if kind == "q":
                    c_act.inc(nc.scalar.activation(out=rrow[0:1, 256 * par:256 * (par + 1)],
                                                   in_=rrf[0:1, 256 * par:256 * (par + 1)],
                                                   func=AF.Exp, scale=-0.5,
                                                   bias=self.cvals[0:1, 3:4]))
                else:
                    c_act.inc(nc.scalar.activation(out=rrow[0:1, 256 * par:256 * (par + 1)],
                                                   in_=rrf[0:1, 256 * par:256 * (par + 1)],
                                                   func=AF.Exp, scale=-0.5))
                self.w(nc.tensor, c_act)
                c_pe.inc(nc.tensor.matmul(pa2[:, bass.ds(256 * par, 256)],
                                          K["ones128_bf"][0:1, :],
                                          rrow[0:1, 256 * par:256 * (par + 1)],
                                          start=True, stop=True))
                self.w(nc.vector, c_pe)
                c_dve.inc(nc.vector.tensor_mul(
                    dst[:, bass.ds((c - cbase) * HALF, HALF)],
                    qsr[:, 256 * par:256 * (par + 1)],
                    pa2[:, bass.ds(256 * par, 256)]))

            def body(j):
                for par in range(2):
                    c = cbase + j * 2 + par
                    self.w(nc.sync, c_pe, -(1 - par) * self._fe_pe_per_c[kind])
                    self.wdma(self.c_ws[par], WS[:, 2048 * par:2048 * (par + 1)],
                              "qkvt", c * 128, 2048)
                front(j, 0)
                front(j, 1)
                if kind != "v":
                    tail(j, 0)
                    tail(j, 1)
            return body
        self._fe_pe_per_c = {"q": 3, "k": 3, "v": 1}
        InnerLoop(nc, self.tile_ctrs, 8, cpair("q", 0, q_sb))
        InnerLoop(nc, self.tile_ctrs, 8, cpair("k", 16, k_sb))
        InnerLoop(nc, self.tile_ctrs, 16, cpair("v", 32, v_sb))

    def attn_chunk(self, half, cpar, L):
        nc = self.nc
        K = self.K
        c_pe, c_act, c_dve, c_pool = self.c_pe, self.c_act, self.c_dve, self.c_pool
        WS = self.WS
        xln = self.xln
        q_sb, k_sb, v_sb, siluz, xg, nrm = L["q_sb"], L["k_sb"], L["v_sb"], L["siluz"], L["xg"], L["nrm"]
        gjfm, colb, colbb, tsc, eA, eTb = L["gjfm"], L["colb"], L["colbb"], L["tsc"], L["eA"], L["eTb"]
        gmlb, gmub, t2f = L["gmlb"], L["gmub"], L["t2f"]
        xcur, ycur, xrh, kcd, nkcdt, vnew, intrat, qeb, kdecb = \
            L["xcur"], L["ycur"], L["xrh"], L["kcd"], L["nkcdt"], L["vnew"], L["intrat"], L["qeb"], L["kdecb"]
        pa0, pa1, pa2, pa3, pa4, pa5, pa6, pa7 = \
            L["pa0"], L["pa1"], L["pa2"], L["pa3"], L["pa4"], L["pa5"], L["pa6"], L["pa7"]
        t_beta, t_gj, t_egj, t_edec, t_begj, t_cb, t_gs = \
            L["t_beta"], L["t_gj"], L["t_egj"], L["t_edec"], L["t_begj"], L["t_cb"], L["t_gs"]
        tmp8 = L["tmp8"]
        t_ssq = tmp8[:, 224:256]
        t_rstd = tmp8[:, 256:288]
        tok0 = half * HALF + cpar * CH
        csl = cpar * CH

        self.sync_fence()
        nc.tensor.wait_ge(c_dve.h, c_dve.val())
        nc.tensor.wait_ge(c_act.h, c_act.val())
        for g in range(4):
            for k in range(DC):
                par = k % 2
                self.w(nc.sync, c_pe, -1)
                base = (BLOB_OFF["wz"]) * 2048 + k * 128 * ZD + 1024 * g
                ap = bass.AP(self.wbf.tensor, base, [[ZD, 128], [1, 1024]])
                self.c_ws[par].inc(nc.sync.dma_start(
                    out=WS[:, 4096 + 1024 * par:4096 + 1024 * (par + 1)], in_=ap))
                self.w(nc.tensor, self.c_ws[par])
                if k == 0 and g > 0:
                    nc.tensor.wait_ge(c_act.h, c_act.val())
                nc.tensor.matmul(pa0[:, :], xln[:, bass.ds(512 * k + tok0, 128)],
                                 WS[:, 4096 + 1024 * par:4096 + 1024 * par + 512],
                                 start=(k == 0), stop=(k == DC - 1))
                mm = nc.tensor.matmul(pa1[:, :], xln[:, bass.ds(512 * k + tok0, 128)],
                                      WS[:, 4096 + 1024 * par + 512:4096 + 1024 * (par + 1)],
                                      start=(k == 0), stop=(k == DC - 1))
                c_pe.inc(mm)
            self.w(nc.scalar, c_pe)
            c_act.inc(nc.scalar.activation(out=siluz[:, 1024 * g:1024 * g + 512],
                                           in_=pa0[:, :], func=AF.Silu))
            c_act.inc(nc.scalar.activation(out=siluz[:, 1024 * g + 512:1024 * (g + 1)],
                                           in_=pa1[:, :], func=AF.Silu))
        for k in range(DC):
            mm = nc.tensor.matmul(pa2[:, 256:320], xln[:, bass.ds(512 * k + tok0, 128)],
                                  self.wba[:, 64 * k:64 * (k + 1)],
                                  start=(k == 0), stop=(k == DC - 1))
        c_pe.inc(mm)
        self.w(nc.scalar, c_pe)
        c_act.inc(nc.scalar.activation(out=t_beta, in_=pa2[:, 256:288], func=AF.Sigmoid))
        self.w(nc.vector, c_pe)
        c_dve.inc(nc.vector.tensor_add(tmp8[:, 288:320], pa2[:, 288:320], self.dtb_b[:, :]))
        self.w(nc.scalar, c_dve)
        c_act.inc(nc.scalar.activation(out=t_gj, in_=tmp8[:, 288:320], func=AF.Abs))
        self.w(nc.scalar, c_act)
        c_act.inc(nc.scalar.activation(out=t_egj, in_=t_gj, func=AF.Exp, scale=-1.0))
        self.w(nc.scalar, c_act)
        c_act.inc(nc.scalar.activation(out=t_egj, in_=t_egj, func=AF.Ln, bias=self.cvals[:, 2:3]))
        c_act.inc(nc.scalar.activation(out=t_edec, in_=tmp8[:, 288:320], func=AF.Relu))
        self.w(nc.vector, c_act)
        c_dve.inc(nc.vector.tensor_add(t_edec, t_edec, t_egj))
        self.w(nc.vector, c_dve)
        c_dve.inc(nc.vector.tensor_mul(tmp8[:, 288:320], t_edec, self.negeA_b[:, :]))
        g_tm = tmp8[:, 288:320]
        self.w(nc.tensor, c_dve)
        c_pe.inc(nc.tensor.matmul(pa3[0:32, 0:128], g_tm, K["mask_up01"][:, :],
                                  start=True, stop=True))
        self.w(nc.scalar, c_pe)
        c_act.inc(nc.scalar.copy(out=gjfm[:, :], in_=pa3[0:32, 0:128]))
        self.w(nc.tensor, c_act)
        c_pe.inc(nc.tensor.matmul(pa3[:, 128:160], K["ones128_f32"][:, :], g_tm,
                                  start=True, stop=True))
        c_pe.inc(nc.tensor.transpose(pa3[:, 160:192], gjfm[:, :], K["id_f32"][0:32, 0:32]))
        self.w(nc.scalar, c_pe)
        c_act.inc(nc.scalar.copy(out=t_gj, in_=pa3[:, 160:192]))
        self.w(nc.scalar, c_act)
        c_act.inc(nc.scalar.activation(out=t_egj, in_=t_gj, func=AF.Exp))
        self.w(nc.vector, c_act)
        c_dve.inc(nc.vector.tensor_sub(t_edec, pa3[:, 128:160], t_gj))
        self.w(nc.scalar, c_dve)
        c_act.inc(nc.scalar.activation(out=t_edec, in_=t_edec, func=AF.Exp))
        c_act.inc(nc.scalar.activation(out=t_cb, in_=pa3[:, 128:160], func=AF.Exp))
        self.w(nc.vector, c_act)
        c_dve.inc(nc.vector.tensor_mul(t_begj, t_beta, t_egj))
        for h in range(VH):
            j = h // 2
            hp = h % 2
            p6 = pa6 if hp == 0 else pa7
            pb = pa4 if hp == 0 else pa5
            prb = pa3[:, 0:128]
            qsl = q_sb[:, j * HALF + csl:j * HALF + csl + CH]
            ksl = k_sb[:, j * HALF + csl:j * HALF + csl + CH]
            vsl = v_sb[:, h * HALF + csl:h * HALF + csl + CH]
            stsl = self.st[:, 128 * h:128 * (h + 1)]
            stbsl = self.stbf[:, 128 * h:128 * (h + 1)]
            eT = eTb[:, 128 * hp:128 * (hp + 1)]
            gml = gmlb[:, 128 * (j % 2):128 * (j % 2 + 1)]
            gmu = gmub[:, 128 * (j % 2):128 * (j % 2 + 1)]
            xc = xcur[:, 128 * hp:128 * (hp + 1)]
            yc = ycur[:, 128 * hp:128 * (hp + 1)]
            xr = xrh[:, 256 * hp:256 * (hp + 1)]
            if hp == 0:
                nc.tensor.wait_ge(c_dve.h, c_dve.val())
                nc.tensor.matmul(pa2[:, 0:128], ksl, ksl, start=True, stop=True)
                c_pe.inc(nc.tensor.matmul(pa2[:, 128:256], ksl, qsl, start=True, stop=True))
                nc.tensor.wait_ge(c_dve.h, c_dve.val())
                c_pe.inc(nc.tensor.transpose(pa4[:, 512:640], ksl, K["id_bf"][:, :]))
                self.w(nc.vector, c_pe)
                c_dve.inc(nc.vector.tensor_mul(gml, pa2[:, 0:128], K["mask_lowneg"][:, :]))
                c_dve.inc(nc.vector.tensor_mul(gmu, pa2[:, 0:128], K["mask_upneg"][:, :]))
            c_dve.inc(nc.vector.tensor_scalar(colb[:, :], K["ones128_f32"][:, :],
                                              t_gj[:, h:h + 1], None, op0=ALU.mult))
            self.w(nc.tensor, c_dve)
            c_pe.inc(nc.tensor.transpose(prb, colb[:, :], K["id_f32"][:, :]))
            self.w(nc.vector, c_pe)
            c_dve.inc(nc.vector.tensor_scalar(tsc[:, :], prb, t_gj[:, h:h + 1], None,
                                              op0=ALU.subtract))
            self.w(nc.vector, c_dve)
            c_dve.inc(nc.vector.tensor_mul(eA[:, :], tsc[:, :], K["mask_lowneg"][:, :]))
            self.w(nc.scalar, c_dve)
            c_act.inc(nc.scalar.activation(out=eA[:, :], in_=eA[:, :], func=AF.Exp))
            c_dve.inc(nc.vector.tensor_mul(eT, tsc[:, :], K["mask_up01"][:, :]))
            self.w(nc.scalar, c_dve)
            c_act.inc(nc.scalar.activation(out=eT, in_=eT, func=AF.Exp))
            self.w(nc.vector, c_act)
            c_dve.inc(nc.vector.scalar_tensor_tensor(out=xc, in0=eA[:, :],
                                                     scalar=t_beta[:, h:h + 1], in1=gml,
                                                     op0=ALU.mult, op1=ALU.mult))
            c_dve.inc(nc.vector.tensor_scalar(colbb[:, :], K["ones128_bf"][:, :],
                                              t_beta[:, h:h + 1], None, op0=ALU.mult))
            self.w(nc.tensor, c_dve)
            c_pe.inc(nc.tensor.transpose(pb[:, 0:128], colbb[:, :], K["id_bf"][:, :]))
            self.w(nc.vector, c_pe)
            c_dve.inc(nc.vector.tensor_mul(t2f[:, :], eT, gmu))
            self.w(nc.vector, c_dve)
            c_dve.inc(nc.vector.tensor_mul(yc, t2f[:, :], pb[:, 0:128]))
            self.w(nc.tensor, c_dve)
            c_pe.inc(nc.tensor.transpose(pb[:, 256:384], vsl, K["id_bf"][:, :]))
            self.w(nc.vector, c_pe)
            c_dve.inc(nc.vector.tensor_scalar(xr[:, 0:128], pb[:, 256:384],
                                              t_beta[:, h:h + 1], None, op0=ALU.mult))
            c_dve.inc(nc.vector.tensor_scalar(xr[:, 128:256], pa4[:, 512:640],
                                              t_begj[:, h:h + 1], None, op0=ALU.mult))
            c_dve.inc(nc.vector.tensor_scalar(kdecb[:, 128 * hp:128 * (hp + 1)],
                                              pa4[:, 512:640],
                                              t_edec[:, h:h + 1], None, op0=ALU.mult))
            self.w(nc.tensor, c_dve)
            self.w(nc.tensor, c_act)
            nc.tensor.matmul(p6[:, 0:256], K["id_bf"][:, :], xr, start=True, stop=True)
            c_pe.inc(nc.tensor.matmul(p6[:, 0:256], yc, xr, start=False, stop=True,
                                      skip_group_check=True))
            for lev in range(1, 7):
                nc.tensor.wait_ge(c_act.h, c_act.val())
                nc.tensor.wait_ge(c_dve.h, c_dve.val())
                nc.tensor.matmul(pa2[:, 256:384], xc, yc, start=True, stop=True)
                if lev < 6:
                    c_pe.inc(nc.tensor.matmul(pa2[:, 384:512], yc, xc, start=True, stop=True,
                                              skip_group_check=True))
                else:
                    c_pe.inc(nc.tensor.matmul(pa2[:, 384:512], yc, xc, start=True, stop=True,
                                              skip_group_check=True))
                self.w(nc.scalar, c_pe, 0)
                c_act.inc(nc.scalar.copy(out=xr, in_=p6[:, 0:256]))
                c_act.inc(nc.scalar.copy(out=yc, in_=pa2[:, 256:384]))
                self.w(nc.vector, c_pe, 0)
                c_dve.inc(nc.vector.tensor_copy(out=xc, in_=pa2[:, 384:512]))
                self.w(nc.tensor, c_act)
                c_pe.inc(nc.tensor.matmul(p6[:, 0:256], yc, xr, start=False, stop=True,
                                          skip_group_check=True))
            self.w(nc.vector, c_pe)
            c_dve.inc(nc.vector.tensor_copy(out=kcd[:, 128 * hp:128 * (hp + 1)], in_=p6[:, 128:256]))
            self.w(nc.tensor, c_dve)
            c_pe.inc(nc.tensor.transpose(pb[:, 384:512], kcd[:, 128 * hp:128 * (hp + 1)],
                                         K["id_bf"][:, :]))
            self.w(nc.scalar, c_pe)
            c_act.inc(nc.scalar.mul(nkcdt[:, 128 * hp:128 * (hp + 1)], pb[:, 384:512], -1.0))
            self.w(nc.tensor, c_act)
            self.w(nc.tensor, c_pool)
            c_pe.inc(nc.tensor.matmul(p6[:, 0:128], nkcdt[:, 128 * hp:128 * (hp + 1)],
                                      stbsl, start=False, stop=True, skip_group_check=True))
            self.w(nc.vector, c_pe)
            c_dve.inc(nc.vector.tensor_copy(out=vnew[:, 128 * hp:128 * (hp + 1)], in_=p6[:, 0:128]))
            self.w(nc.vector, c_act)
            c_dve.inc(nc.vector.tensor_mul(t2f[:, :], pa2[:, 128:256], eT))
            self.w(nc.vector, c_dve)
            c_dve.inc(nc.vector.tensor_mul(intrat[:, 128 * hp:128 * (hp + 1)], t2f[:, :],
                                           K["mask_up01"][:, :]))
            c_dve.inc(nc.vector.tensor_scalar(colbb[:, :], K["ones128_bf"][:, :],
                                              t_egj[:, h:h + 1], None, op0=ALU.mult))
            self.w(nc.tensor, c_dve)
            c_pe.inc(nc.tensor.transpose(pb[:, 128:256], colbb[:, :], K["id_bf"][:, :]))
            self.w(nc.vector, c_pe)
            c_dve.inc(nc.vector.tensor_mul(qeb[:, 128 * hp:128 * (hp + 1)], qsl,
                                           pb[:, 128:256]))
            self.w(nc.tensor, c_dve)
            nc.tensor.wait_ge(c_dve.h, c_dve.val())
            self.w(nc.tensor, c_pool)
            nc.tensor.matmul(p6[:, 384:512], intrat[:, 128 * hp:128 * (hp + 1)],
                             vnew[:, 128 * hp:128 * (hp + 1)], start=True, stop=True)
            c_pe.inc(nc.tensor.matmul(p6[:, 384:512], qeb[:, 128 * hp:128 * (hp + 1)],
                                      stbsl, start=False, stop=True, skip_group_check=True))
            self.w(nc.vector, c_pe)
            c_dve.inc(nc.vector.tensor_mul(xg[:, 128 * h:128 * (h + 1)], p6[:, 384:512],
                                           siluz[:, 128 * h:128 * (h + 1)]))
            pss = pa3[:, 288:416]
            c_pe.inc(nc.tensor.matmul(pss, kdecb[:, 128 * hp:128 * (hp + 1)],
                                      vnew[:, 128 * hp:128 * (hp + 1)], start=True, stop=True))
            self.w(nc.vector, c_pe)
            c_dve.inc(nc.vector.tensor_scalar(stsl, stsl, t_cb[:, h:h + 1], None, op0=ALU.mult))
            self.w(nc.vector, c_dve)
            c_dve.inc(nc.vector.tensor_add(stsl, stsl, pss))
            self.w(nc.gpsimd, c_dve)
            c_pool.inc(nc.gpsimd.tensor_copy(out=stbsl, in_=stsl))
        self.w(nc.scalar, c_dve)
        c_act.inc(nc.scalar.activation(out=siluz[:, :], in_=xg[:, :], func=AF.Square))
        self.w(nc.vector, c_act)
        c_dve.inc(nc.vector.tensor_reduce(
            t_ssq, siluz[:, :].rearrange("p (h u) -> p h u", u=128), axis=AX.X,
            op=ALU.add))
        self.w(nc.vector, c_dve)
        c_dve.inc(nc.vector.tensor_scalar(t_ssq, t_ssq, 1.0 / 128.0, None, op0=ALU.mult))
        self.w(nc.scalar, c_dve)
        c_act.inc(nc.scalar.activation(out=t_rstd, in_=t_ssq, func=AF.Ln,
                                       bias=self.cvals[:, 4:5]))
        self.w(nc.scalar, c_act)
        c_act.inc(nc.scalar.activation(out=t_rstd, in_=t_rstd, func=AF.Exp, scale=-0.5))
        self.w(nc.vector, c_act)
        for h in range(VH):
            c_dve.inc(nc.vector.tensor_scalar(xg[:, 128 * h:128 * (h + 1)],
                                              xg[:, 128 * h:128 * (h + 1)],
                                              t_rstd[:, h:h + 1], None, op0=ALU.mult))
        self.w(nc.vector, c_dve)
        c_dve.inc(nc.vector.tensor_mul(xg[:, :], xg[:, :], self.normw_rep[:, :]))
        for fc in range(32):
            pslot = pa4 if fc % 2 == 0 else pa5
            self.w(nc.tensor, c_dve)
            if fc >= 2:
                self.w(nc.tensor, c_act)
            c_pe.inc(nc.tensor.transpose(pslot[:, 640:768], xg[:, 128 * fc:128 * (fc + 1)],
                                         K["id_bf"][:, :]))
            self.w(nc.scalar, c_pe)
            c_act.inc(nc.scalar.copy(out=nrm[:, fc * HALF + 128 * cpar:fc * HALF + 128 * cpar + 128],
                                     in_=pslot[:, 640:768]))

    def attn_outproj(self, half, L):
        nc = self.nc
        c_pe, c_act, c_dve = self.c_pe, self.c_act, self.c_dve
        WS = self.WS
        nrm = L["nrm"]
        pa0, pa1 = L["pa0"], L["pa1"]
        self.sync_fence()
        nc.sync.wait_ge(c_pe.h, c_pe.val())
        nc.tensor.wait_ge(c_act.h, c_act.val())

        def dpair(jj):
            for par in range(2):
                d = jj * 2 + par
                self.w(nc.sync, c_dve, par - 1)
                self.wdma(self.c_ws[par], WS[:, 4096 + 4096 * par:4096 + 4096 * (par + 1)],
                          "woutt", d * 256, 4096)
            for par in range(2):
                d = jj * 2 + par
                pp = pa0 if par == 0 else pa1
                self.w(nc.tensor, self.c_ws[par])
                for fc in range(32):
                    mm = nc.tensor.matmul(pp[:, 0:HALF],
                                          WS[:, 4096 + 4096 * par + 128 * fc:4096 + 4096 * par + 128 * (fc + 1)],
                                          nrm[:, fc * HALF:(fc + 1) * HALF],
                                          start=(fc == 0), stop=(fc == 31))
                c_pe.inc(mm)
                self.w(nc.vector, c_pe)
                c_dve.inc(nc.vector.tensor_add(
                    self.x_res[:, bass.ds(512 * d + HALF * half, HALF)],
                    self.x_res[:, bass.ds(512 * d + HALF * half, HALF)], pp[:, 0:HALF]))
        InnerLoop(nc, self.tile_ctrs, 8, dpair)



def make_in_maps(inputs):
    blob = prep_weights(inputs)
    consts = prep_consts()
    convw = np.asarray(inputs["L0_attn_conv_weight"], np.float32)
    convw_r = np.ascontiguousarray(
        convw.reshape(CC, 128, KCONV).transpose(1, 0, 2).reshape(128, CC * KCONV))
    smalls = np.zeros((128, 512), np.float32)
    smalls[0, 0:32] = np.asarray(inputs["L0_attn_dt_bias"], np.float32)
    smalls[0, 32:64] = np.asarray(inputs["L0_attn_A_log"], np.float32)
    smalls[0, 64:192] = np.asarray(inputs["L0_attn_norm_w"], np.float32)
    cvals = np.zeros((128, 16), np.float32)
    cvals[:, 0] = LN_EPS
    cvals[:, 1] = L2_EPS
    cvals[:, 2] = 1.0
    cvals[:, 3] = -0.5 * np.log(128.0)
    cvals[:, 4] = RMS_EPS
    in_maps = []
    for core in range(8):
        m = {"xin": prep_core_x(inputs, core),
             "wf32": blob, "convw": convw_r, "smalls": smalls, "cvals": cvals}
        for nm, shp, dt in CONST_SPECS:
            m["c_" + nm] = consts[nm]
        in_maps.append(m)
    return in_maps


def run_dev(inputs, do_attn=True, dev_tiles=None, dev_stage=None):
    b = Builder(do_attn=do_attn, dev_tiles=dev_tiles, dev_stage=dev_stage)
    nc = b.build()
    in_maps = make_in_maps(inputs)
    res = run_bass_kernel_spmd(nc, in_maps, core_ids=list(range(8)))
    return [r["xout"] for r in res.results]


def kernel(**inputs):
    outs = run_dev(inputs, do_attn=True, dev_tiles=None)
    full = np.empty((16, 2048, 2048), np.float32)
    for core in range(8):
        o = outs[core].reshape(NSEQ, NTILE, DC, 128, P)
        xf = o.transpose(0, 2, 3, 1, 4).reshape(NSEQ, D, 2048)
        full[2 * core:2 * core + 2] = xf.transpose(0, 2, 1)
    return full
